# version 13
# baseline (speedup 1.0000x reference)
import math
import numpy as np
from contextlib import ExitStack

import concourse.bass as bass
import concourse.mybir as mybir
from concourse.bass_utils import run_bass_kernel_spmd

F32 = mybir.dt.float32
BF16 = mybir.dt.bfloat16
AF = mybir.ActivationFunctionType
ALU = mybir.AluOpType
AX = mybir.AxisListType

D = 1024
S = 4096
NCH = 8
LN_EPS = 1e-5
RSQRT_DK0 = 1.0 / math.sqrt(128.0)

ENGINES = ("tensor", "vector", "scalar", "sync", "gpsimd")

WEIGHT_NAMES = [
    "L0_attn_Wq", "L0_attn_bq", "L0_attn_Wk", "L0_attn_bk", "L0_attn_Wv",
    "L0_attn_bv", "L0_attn_gamma", "L0_attn_beta",
    "L0_ffn_gate", "L0_ffn_W1", "L0_ffn_b1", "L0_ffn_W2", "L0_ffn_b2",
    "L1_attn_Wq", "L1_attn_Wk", "L1_attn_Wv", "L1_attn_Wo",
    "L1_ffn_Wg", "L1_ffn_Wu", "L1_ffn_Wd", "L1_ffn_Wr", "L1_ffn_temperature",
]


class Sem:
    def __init__(self, h):
        self.h = h
        self.n = 0


class Plan:

    def __init__(self, nc):
        self.nc = nc
        self.q = {e: [] for e in ENGINES}

    def op(self, eng, build, inc=None, inc_k=1, wait=()):
        waits = [(s.h, v) for (s, v) in wait]
        if inc is not None:
            inc.n += inc_k
            inc_t = (inc.h, inc_k)
        else:
            inc_t = None

        def run(e, waits=waits, build=build, inc_t=inc_t):
            for h, v in waits:
                e.wait_ge(h, v)
            ins = build(e)
            if inc_t is not None:
                ins.then_inc(*inc_t)
        self.q[eng].append(run)

    def run(self):
        with self.nc.Block() as blk:
            for e in ENGINES:
                lst = self.q[e]
                if lst:
                    getattr(blk, e)(lambda eng, lst=lst: [f(eng) for f in lst])


def make_sems(nc, es, names):
    class NS:
        pass
    ns = NS()
    for n in names:
        setattr(ns, n, Sem(es.enter_context(nc.semaphore(n))))
    return ns


def emit_setup(nc, es, ctx):
    ctx.ones_col_bf = es.enter_context(nc.sbuf_tensor("c_ones_col", [128, 1], BF16))
    ctx.ones_row_bf = es.enter_context(nc.sbuf_tensor("c_ones_row", [1, 512], BF16))
    ctx.ones_1x128_bf = es.enter_context(nc.sbuf_tensor("c_ones1x128b", [1, 128], BF16))
    ctx.ones_1x128_f = es.enter_context(nc.sbuf_tensor("c_ones1x128f", [1, 128], F32))
    ctx.identb = es.enter_context(nc.sbuf_tensor("c_identb", [128, 128], BF16))
    ctx.identf = es.enter_context(nc.sbuf_tensor("c_identf", [128, 128], F32))
    ctx.identf8 = es.enter_context(nc.sbuf_tensor("c_identf8", [8, 8], F32))
    ctx.sel_all = es.enter_context(nc.sbuf_tensor("c_sel", [8, 1024], BF16))
    ctx.onesf_col = es.enter_context(nc.sbuf_tensor("c_onesf_col", [128, 1], F32))

    p = Plan(nc)

    def g_fn(g):
        g.memset(ctx.ones_col_bf[:, :], 1.0)
        g.memset(ctx.ones_row_bf[:, :], 1.0)
        g.memset(ctx.ones_1x128_bf[:, :], 1.0)
        g.memset(ctx.ones_1x128_f[:, :], 1.0)
        g.memset(ctx.onesf_col[:, :], 1.0)
        for t, n in ((ctx.identb, 128), (ctx.identf, 128), (ctx.identf8, 8)):
            g.memset(t[:, :], 1.0)
            g.affine_select(t[:, :], t[:, :], [[-1, n]], ALU.is_equal, 0.0,
                            base=0, channel_multiplier=1)
        g.memset(ctx.sel_all[:, :], 1.0)
        g.affine_select(ctx.sel_all[:, :], ctx.sel_all[:, :], [[-1, 8], [0, 128]],
                        ALU.is_equal, 0.0, base=0, channel_multiplier=1)
        return g.nop()
    p.op("gpsimd", g_fn)
    p.run()


def plan_ln_rows(p, sm, ps_s, ps_q, rows, with_ln2, pre_wait):
    mu, msq, var, t0, t1, A, negMA = (rows[k] for k in
                                      ("mu", "msq", "var", "t0", "t1", "A", "negMA"))

    def v(build, wait=()):
        p.op("vector", build, inc=sm.sv, wait=list(wait) + [(sm.sv, sm.sv.n)])

    def s(build):
        p.op("scalar", build, inc=sm.ss, wait=[(sm.sv, sm.sv.n)])

    v(lambda e: e.tensor_scalar_mul(mu[:, :], ps_s[:, :], 1.0 / 1024.0), wait=pre_wait)
    v(lambda e: e.tensor_scalar_mul(msq[:, :], ps_q[:, :], 1.0 / 1024.0))
    v(lambda e: e.tensor_mul(t0[:, :], mu[:, :], mu[:, :]))
    v(lambda e: e.tensor_sub(var[:, :], msq[:, :], t0[:, :]))
    v(lambda e: e.tensor_scalar_add(t0[:, :], var[:, :], LN_EPS))
    s(lambda e: e.activation(t1[:, :], t0[:, :], AF.Ln))
    s(lambda e: e.activation(A[:, :], t1[:, :], AF.Exp, scale=np.float32(-0.5)))
    v(lambda e: e.nop(), wait=[(sm.ss, sm.ss.n)])
    if with_ln2:
        v(lambda e: e.tensor_mul(t0[:, :], A[:, :], A[:, :]))
        v(lambda e: e.tensor_mul(t0[:, :], var[:, :], t0[:, :]))
        v(lambda e: e.tensor_scalar_add(t0[:, :], t0[:, :], LN_EPS))
        s(lambda e: e.activation(t1[:, :], t0[:, :], AF.Ln))
        s(lambda e: e.activation(t0[:, :], t1[:, :], AF.Exp, scale=np.float32(-0.5)))
        v(lambda e: e.tensor_mul(A[:, :], A[:, :], t0[:, :]), wait=[(sm.ss, sm.ss.n)])
    v(lambda e: e.tensor_mul(negMA[:, :], mu[:, :], A[:, :]))
    v(lambda e: e.tensor_scalar_mul(negMA[:, :], negMA[:, :], -1.0))


def emit_phase0(nc, ctx, sm, x_src, x_dst, w):
    es = ExitStack()
    sb = lambda name, shape, dt: es.enter_context(nc.sbuf_tensor("p0_" + name, shape, dt))

    v_tok = sb("v_tok", [128, 32 * 1024], BF16)
    bias_fm = sb("bias_fm", [128, 32], F32)
    bvr_row = sb("bvr", [1, 1024], BF16)
    bvr_f = sb("bvrf", [1, 1024], F32)
    rows = {k: sb("row_" + k, [1, 512], F32)
            for k in ("mu", "msq", "var", "t0", "t1", "A", "negMA")}
    q_scr = nc.dram_tensor("p0_q_scr", [8, 128, 4096], BF16, kind="Internal").ap()
    k_scr = nc.dram_tensor("p0_k_scr", [8, 128, 4096], BF16, kind="Internal").ap()

    es1 = ExitStack()
    sb1 = lambda name, shape, dt: es1.enter_context(nc.sbuf_tensor("p0s_" + name, shape, dt))
    wq_sb = sb1("wq", [128, 8 * 1024], BF16)
    wk_sb = sb1("wk", [128, 8 * 1024], BF16)
    wvr_sb = sb1("wvr", [128, 8 * 1024], BF16)
    es0 = ExitStack()
    sb0 = lambda name, shape, dt: es0.enter_context(nc.sbuf_tensor("p0z_" + name, shape, dt))
    wv_buf = sb0("wvbuf", [128, 2 * 4096], F32)
    wvr_f = sb0("wvrf", [128, 512], F32)
    bv_row = sb0("bv_row", [1, 8192], BF16)
    vec8 = sb0("vec8", [8, 4 * 128], F32)

    esp = ExitStack()
    psb = lambda name, shape: esp.enter_context(nc.psum_tensor("p0_" + name, shape, F32))
    ps_s = psb("s", [1, 512])
    ps_q = psb("q", [1, 512])
    ps_A = psb("A", [128, 512])
    ps_M = psb("M", [128, 512])
    ps_p0 = psb("pp0", [128, 512])
    ps_p1 = psb("pp1", [128, 512])
    ps_pp = [ps_p0, ps_p1]

    p = Plan(nc)
    p.op("gpsimd", lambda g: g.dma_start(
        wq_sb[:, :].rearrange("p (c f) -> p c f", f=1024),
        w["L0_attn_Wq"].rearrange("(c p) f -> p c f", p=128)), inc=sm.d_w, inc_k=16)
    p.op("gpsimd", lambda g: g.dma_start(
        wk_sb[:, :].rearrange("p (c f) -> p c f", f=1024),
        w["L0_attn_Wk"].rearrange("(c p) f -> p c f", p=128)), inc=sm.d_w, inc_k=16)
    for i, nm in enumerate(["L0_attn_bq", "L0_attn_bk", "L0_attn_gamma", "L0_attn_beta"]):
        p.op("gpsimd", lambda g, i=i, nm=nm: g.dma_start(
            vec8[:, i * 128:(i + 1) * 128],
            w[nm].rearrange("(a b) -> a b", b=128)), inc=sm.d_v, inc_k=16)
    p.op("gpsimd", lambda g: g.dma_start(
        bv_row[:, :], w["L0_attn_bv"].rearrange("(a b) -> a b", a=1)),
        inc=sm.d_v, inc_k=16)
    for i in range(16):
        c, hf = i // 2, i % 2
        wv_waits = [(sm.v_wvr, i - 1)] if i >= 2 else []
        p.op("sync", lambda e, c=c, hf=hf, i=i: e.dma_start(
            wv_buf[:, (i % 2) * 4096:(i % 2 + 1) * 4096],
            w["L0_attn_Wv"][c * 128:(c + 1) * 128, hf * 4096:(hf + 1) * 4096]),
            inc=sm.d_wv, inc_k=16, wait=wv_waits)
        p.op("vector", lambda e, c=c, hf=hf, i=i: e.reduce_sum(
            wvr_f[:, :].rearrange("p (r e) -> p r e", e=1),
            wv_buf[:, (i % 2) * 4096:(i % 2 + 1) * 4096].rearrange("p (r e) -> p r e", e=8),
            axis=AX.X), wait=[(sm.d_wv, 16 * (i + 1))] + [(sm.v_wvr, sm.v_wvr.n)])
        p.op("vector", lambda e, c=c, hf=hf: e.tensor_copy(
            wvr_sb[:, c * 1024 + hf * 512: c * 1024 + (hf + 1) * 512], wvr_f[:, :]),
            inc=sm.v_wvr)
    p.op("vector", lambda e: e.reduce_sum(
        bvr_f[:, :].rearrange("p (r e) -> p r e", e=1),
        bv_row[:, :].rearrange("p (r e) -> p r e", e=8), axis=AX.X),
        wait=[(sm.d_v, sm.d_v.n), (sm.v_wvr, sm.v_wvr.n)])
    p.op("vector", lambda e: e.tensor_copy(bvr_row[:, :], bvr_f[:, :]),
        inc=sm.v_wvr)
    def bias_mms(t):
        for i in range(4):
            ins = t.matmul(ps_A[:, i * 8:(i + 1) * 8], vec8[:, i * 128:(i + 1) * 128],
                           ctx.identf8[:, :], start=(i == 0), stop=(i == 3))
        return ins
    p.op("tensor", bias_mms, inc=sm.t_mm, wait=[(sm.d_v, sm.d_v.n)])
    p.op("scalar", lambda s: s.copy(bias_fm[:, :], ps_A[:, 0:32]),
         inc=sm.s_ev, wait=[(sm.t_mm, sm.t_mm.n)])
    p.run()
    es0.close()

    xblk = sb1("xblk", [128, 8 * 512], F32)
    xbf = sb1("xbf", [128, 8 * 512], BF16)
    xsq = sb1("xsq", [128, 8 * 512], BF16)
    xn0 = sb1("xn0", [128, 2 * 512], F32)
    xn_sb = sb1("xn", [128, 8 * 512], BF16)
    q_stage = sb1("qstage", [128, 8 * 512], BF16)
    k_stage = sb1("kstage", [128, 8 * 512], BF16)

    p = Plan(nc)
    t_proj_end = 0
    t_statq_end = 0
    for sbi in range(8):
        xdma_gate = [(sm.v_xn0, sm.v_xn0.n), (sm.s_sq, sm.s_sq.n),
                     (sm.v_cast, sm.v_cast.n)] if sbi else []
        for c in range(NCH):
            p.op("sync", lambda e, c=c, sbi=sbi: e.dma_start(
                xblk[:, c * 512:(c + 1) * 512],
                x_src[c * 128:(c + 1) * 128, sbi * 512:(sbi + 1) * 512]),
                inc=sm.d_x, inc_k=16, wait=xdma_gate if c == 0 else ())
        xw = [(sm.d_x, sm.d_x.n)]
        for c in range(NCH):
            p.op("vector", lambda e, c=c: e.tensor_copy(
                xbf[:, c * 512:(c + 1) * 512], xblk[:, c * 512:(c + 1) * 512]),
                inc=sm.v_cast, wait=xw if c == 0 else ())
            sqg = [(sm.t_mm, t_statq_end)] if (sbi and c < 2) else []
            p.op("scalar", lambda e, c=c: e.activation(
                xsq[:, c * 512:(c + 1) * 512], xblk[:, c * 512:(c + 1) * 512],
                AF.Square), inc=sm.s_sq, wait=(xw if c == 0 else []) + sqg)

        def stats_s(t):
            for c in range(NCH):
                ins = t.matmul(ps_s[:, :], ctx.ones_col_bf[:, :],
                               xbf[:, c * 512:(c + 1) * 512],
                               start=(c == 0), stop=(c == 7))
            return ins
        p.op("tensor", stats_s, inc=sm.t_mm,
             wait=[(sm.v_cast, sm.v_cast.n), (sm.sv, sm.sv.n)])

        def stats_q(t):
            for c in range(NCH):
                ins = t.matmul(ps_q[:, :], ctx.ones_col_bf[:, :],
                               xsq[:, c * 512:(c + 1) * 512],
                               start=(c == 0), stop=(c == 7))
            return ins
        p.op("tensor", stats_q, inc=sm.t_mm, wait=[(sm.s_sq, sm.s_sq.n)])
        t_statq_end = sm.t_mm.n

        plan_ln_rows(p, sm, ps_s, ps_q, rows, with_ln2=True,
                     pre_wait=[(sm.t_mm, sm.t_mm.n)])
        p.op("tensor", lambda t: t.matmul(ps_A[:, :], ctx.ones_1x128_f[:, :],
                                          rows["A"][:, :], start=True, stop=True),
             inc=sm.t_mm, wait=[(sm.sv, sm.sv.n), (sm.v_xn0, sm.v_xn0.n)])
        p.op("tensor", lambda t: t.matmul(ps_M[:, :], ctx.ones_1x128_f[:, :],
                                          rows["negMA"][:, :], start=True, stop=True),
             inc=sm.t_mm)
        bcw = [(sm.t_mm, sm.t_mm.n)]
        for c in range(NCH):
            b2 = c % 2
            slot_free = [(sm.s_xn, sm.s_xn.n - 1)] if sm.s_xn.n >= 1 else []
            p.op("vector", lambda e, c=c, b2=b2: e.tensor_mul(
                xn0[:, b2 * 512:(b2 + 1) * 512], xblk[:, c * 512:(c + 1) * 512],
                ps_A[:, :]), wait=(bcw if c == 0 else []) + slot_free)
            p.op("vector", lambda e, c=c, b2=b2: e.tensor_add(
                xn0[:, b2 * 512:(b2 + 1) * 512], xn0[:, b2 * 512:(b2 + 1) * 512],
                ps_M[:, :]), inc=sm.v_xn0)
            xng = [(sm.t_mm, t_proj_end)] if (sbi and c == 0) else []
            p.op("scalar", lambda e, c=c, b2=b2: e.activation(
                xn_sb[:, c * 512:(c + 1) * 512], xn0[:, b2 * 512:(b2 + 1) * 512],
                AF.Identity, bias=bias_fm[:, 24 + c:25 + c],
                scale=bias_fm[:, 16 + c:17 + c]),
                inc=sm.s_xn, wait=[(sm.v_xn0, sm.v_xn0.n)] + xng)

        bank_records = []
        xnw = [(sm.s_xn, sm.s_xn.n)]
        dq_base = sm.d_qk.n

        def group(t_build, evac_eng, evac_build, evac_sem, first=False):
            G = len(bank_records)
            waits = ([bank_records[G - 2]] if G >= 2 else []) + (xnw if first else [])
            p.op("tensor", t_build, inc=sm.t_mm, wait=waits)
            p.op(evac_eng, evac_build, inc=evac_sem, wait=[(sm.t_mm, sm.t_mm.n)])
            bank_records.append((evac_sem, evac_sem.n))

        for wsb, stage_t, bcol in ((wq_sb, q_stage, 0), (wk_sb, k_stage, 8)):
            for ft in range(8):
                G = len(bank_records)

                def t_build(t, wsb=wsb, ft=ft, G=G):
                    for c in range(NCH):
                        ins = t.matmul(
                            ps_pp[G % 2][:, :],
                            wsb[:, c * 1024 + ft * 128: c * 1024 + (ft + 1) * 128],
                            xn_sb[:, c * 512:(c + 1) * 512],
                            start=(c == 0), stop=(c == 7))
                    return ins

                def e_build(e, stage_t=stage_t, ft=ft, bcol=bcol, G=G):
                    return e.activation(
                        stage_t[:, ft * 512:(ft + 1) * 512], ps_pp[G % 2][:, :],
                        AF.Identity, bias=bias_fm[:, bcol + ft:bcol + ft + 1])
                if G == 0 and sbi:
                    xnw.append((sm.d_qk, dq_base))
                group(t_build, "scalar", e_build, sm.s_ev, first=(G == 0))
        for tt in range(4):
            for half in range(2):
                G = len(bank_records)
                ttg = sbi * 4 + tt

                def t_build(t, tt=tt, half=half, G=G):
                    for c in range(NCH):
                        t.matmul(ps_pp[G % 2][:, :],
                                 xn_sb[:, c * 512 + tt * 128: c * 512 + (tt + 1) * 128],
                                 wvr_sb[:, c * 1024 + half * 512: c * 1024 + (half + 1) * 512],
                                 start=(c == 0), stop=False)
                    return t.matmul(ps_pp[G % 2][:, :], ctx.ones_1x128_bf[:, :],
                                    bvr_row[:, half * 512:(half + 1) * 512],
                                    start=False, stop=True)

                def e_build(e, ttg=ttg, half=half, G=G):
                    return e.tensor_copy(
                        v_tok[:, ttg * 1024 + half * 512: ttg * 1024 + (half + 1) * 512],
                        ps_pp[G % 2][:, :])
                group(t_build, "vector", e_build, sm.v_ev)

        for ft in range(8):
            p.op("sync", lambda e, ft=ft, sbi=sbi: e.dma_start(
                q_scr[ft, :, sbi * 512:(sbi + 1) * 512],
                q_stage[:, ft * 512:(ft + 1) * 512]),
                inc=sm.d_qk, inc_k=16, wait=[bank_records[ft]])
            p.op("sync", lambda e, ft=ft, sbi=sbi: e.dma_start(
                k_scr[ft, :, sbi * 512:(sbi + 1) * 512],
                k_stage[:, ft * 512:(ft + 1) * 512]),
                inc=sm.d_qk, inc_k=16, wait=[bank_records[8 + ft]])
        t_proj_end = sm.t_mm.n
    p.run()

    es1.close()
    esp.close()

    esp2 = ExitStack()
    psb2 = lambda name, shape: esp2.enter_context(nc.psum_tensor("p0a_" + name, shape, F32))
    ps_sc = [psb2("scA", [128, 1024]), psb2("scB", [128, 1024])]
    ps_oo = [psb2("o0", [128, 512]), psb2("o1", [128, 512])]
    ps_sum = psb2("sum", [1, 512])
    ps_misc = psb2("misc", [128, 512])

    qh_sb = sb("qh", [128, 2 * 4096], BF16)
    kh_sb = sb("kh", [128, 2 * 4096], BF16)
    xh_sb = sb("xh", [128, 8 * 512], F32)
    xh2_sb = sb("xh2", [128, 8 * 512], F32)
    oh_sb = sb("oh", [128, 2 * 4096], F32)
    exp_sb = sb("exp", [128, 4 * 1024], BF16)
    psh_sb = sb("psh", [128, 2 * 512], F32)
    rec_bc = sb("rec_bc", [128, 512], F32)
    cm1 = sb("cm1", [128, 1], F32)
    negC = sb("negC", [1, 1], F32)
    negC_bc = sb("negC_bc", [128, 2], F32)
    sum_row = sb("sum_row", [1, 512], F32)
    rec_row = sb("rec_row", [1, 512], F32)

    n_qk_stores = sm.d_qk.n
    sc_count = [0]

    p = Plan(nc)
    for hp in range(4):
        qk_gate = [(sm.t_sc, sm.t_sc.n), (sm.t_pv, sm.t_pv.n)]
        for hh in range(2):
            h = 2 * hp + hh
            p.op("sync", lambda e, h=h, hh=hh: e.dma_start(
                qh_sb[:, hh * 4096:(hh + 1) * 4096], q_scr[h, :, :]),
                inc=sm.d_qh, inc_k=16,
                wait=([(sm.d_qk, n_qk_stores)] if h == 0 else []) +
                     (qk_gate if (hh == 0 and hp > 0) else []))
            p.op("sync", lambda e, h=h, hh=hh: e.dma_start(
                kh_sb[:, hh * 4096:(hh + 1) * 4096], k_scr[h, :, :]),
                inc=sm.d_qh, inc_k=16)
        for c in range(NCH):
            p.op("gpsimd", lambda e, c=c, hp=hp: e.dma_start(
                xh_sb[:, c * 512:(c + 1) * 512],
                x_src[c * 128:(c + 1) * 128, 2 * hp * 512:(2 * hp + 1) * 512]),
                inc=sm.d_xh, inc_k=16,
                wait=[(sm.v_res, sm.v_res.n)] if c == 0 else ())
        qkw = [(sm.d_qh, sm.d_qh.n)]
        for hh in range(2):
            corw = (qkw if hh == 0 else []) + [(sm.s_rb, sm.s_rb.n),
                                               (sm.s_ev, sm.s_ev.n)]
            p.op("tensor", lambda t, hh=hh: t.matmul(
                ps_misc[:, :], kh_sb[:, hh * 4096: hh * 4096 + 128],
                qh_sb[:, hh * 4096: hh * 4096 + 512], start=True, stop=True),
                inc=sm.t_mm, wait=corw)
            p.op("vector", lambda v: v.reduce_max(cm1[:, :], ps_misc[:, :], axis=AX.X),
                 inc=sm.sv, wait=[(sm.t_mm, sm.t_mm.n), (sm.sv, sm.sv.n)])
            p.op("tensor", lambda t: t.matmul(ps_misc[0:1, 0:128], cm1[:, :],
                                              ctx.identf[:, :], start=True, stop=True),
                 inc=sm.t_mm, wait=[(sm.sv, sm.sv.n)])
            p.op("vector", lambda v: v.reduce_max(negC[:, :], ps_misc[0:1, 0:128],
                                                  axis=AX.X, negate=True),
                 inc=sm.sv, wait=[(sm.t_mm, sm.t_mm.n), (sm.sv, sm.sv.n)])
            p.op("vector", lambda v: v.tensor_scalar_mul(negC[:, :], negC[:, :],
                                                         RSQRT_DK0),
                 inc=sm.sv, wait=[(sm.sv, sm.sv.n)])
            p.op("tensor", lambda t, hh=hh: t.matmul(
                ps_misc[:, 2 * hh:2 * hh + 1], ctx.ones_1x128_f[:, :], negC[:, :],
                start=True, stop=True),
                inc=sm.t_mm, wait=[(sm.sv, sm.sv.n)])
            p.op("scalar", lambda s, hh=hh: s.copy(negC_bc[:, hh:hh + 1],
                                                   ps_misc[:, 2 * hh:2 * hh + 1]),
                 inc=sm.s_ev, wait=[(sm.t_mm, sm.t_mm.n)])
        cready = [(sm.s_ev, sm.s_ev.n)]

        pair_resw = [(sm.v_res, sm.v_res.n)]
        psh_gate = [(sm.t_mm, sm.t_mm.n)]
        for sbi in range(8):
            sum_gate = [(sm.s_sr, sm.s_sr.n)]
            on_gates = {0: [(sm.v_on, sm.v_on.n - 1)] if sm.v_on.n >= 1 else [],
                        1: [(sm.v_on, sm.v_on.n)]}
            pending = None
            for bt in range(16):
                for hh in range(2):
                    h = 2 * hp + hh
                    j = sc_count[0]

                    def t_sc(t, bt=bt, sbi=sbi, hh=hh, j=j):
                        t.matmul(ps_sc[j % 2][:, 0:512],
                                 kh_sb[:, hh * 4096 + (2 * bt) * 128: hh * 4096 + (2 * bt + 1) * 128],
                                 qh_sb[:, hh * 4096 + sbi * 512: hh * 4096 + (sbi + 1) * 512],
                                 start=True, stop=True)
                        return t.matmul(ps_sc[j % 2][:, 512:1024],
                                        kh_sb[:, hh * 4096 + (2 * bt + 1) * 128: hh * 4096 + (2 * bt + 2) * 128],
                                        qh_sb[:, hh * 4096 + sbi * 512: hh * 4096 + (sbi + 1) * 512],
                                        start=True, stop=True)
                    scw = [(sm.s_exp, j - 1)] if j >= 2 else []
                    p.op("tensor", t_sc, inc=sm.t_sc, wait=scw)
                    sc_count[0] += 1
                    if pending is not None:
                        p.op("tensor", pending[0], inc=sm.t_pv, wait=pending[1])
                        pending = None
                    expw = [(sm.v_ps, sm.v_ps.n - 6)] if sm.v_ps.n >= 6 else []
                    p.op("scalar", lambda s, j=j, hh=hh: s.activation(
                        exp_sb[:, (j % 4) * 1024:(j % 4 + 1) * 1024], ps_sc[j % 2][:, :],
                        AF.Exp, bias=negC_bc[:, hh:hh + 1], scale=RSQRT_DK0),
                        inc=sm.s_exp,
                        wait=[(sm.t_sc, sm.t_sc.n)] + expw + (cready if (sbi == 0 and bt == 0) else []))
                    for q2 in range(2):
                        p.op("vector", lambda v, j=j, hh=hh, q2=q2, bt=bt: (
                            v.tensor_copy(psh_sb[:, hh * 512:(hh + 1) * 512],
                                          exp_sb[:, (j % 4) * 1024 + q2 * 512: (j % 4) * 1024 + (q2 + 1) * 512])
                            if (bt == 0 and q2 == 0) else
                            v.tensor_add(psh_sb[:, hh * 512:(hh + 1) * 512],
                                         psh_sb[:, hh * 512:(hh + 1) * 512],
                                         exp_sb[:, (j % 4) * 1024 + q2 * 512: (j % 4) * 1024 + (q2 + 1) * 512])),
                            inc=sm.v_ps,
                            wait=[(sm.s_exp, sm.s_exp.n)] +
                                 (psh_gate if (bt == 0 and q2 == 0) else []))

                    def t_pv(t, bt=bt, hh=hh, h=h, j=j):
                        for q2 in range(2):
                            tc = 2 * bt + q2
                            ins = t.matmul(
                                ps_oo[hh][:, :],
                                v_tok[:, tc * 1024 + h * 128: tc * 1024 + (h + 1) * 128],
                                exp_sb[:, (j % 4) * 1024 + q2 * 512: (j % 4) * 1024 + (q2 + 1) * 512],
                                start=(tc == 0), stop=(tc == 31))
                        return ins
                    pvw = [(sm.s_exp, sm.s_exp.n)] + (on_gates[hh] if bt == 0 else [])
                    pending = (t_pv, pvw)
            if pending is not None:
                p.op("tensor", pending[0], inc=sm.t_pv, wait=pending[1])
                pending = None
            for hh in range(2):
                h = 2 * hp + hh
                p.op("tensor", lambda t, hh=hh: t.matmul(
                    ps_sum[:, :], ctx.onesf_col[:, :], psh_sb[:, hh * 512:(hh + 1) * 512],
                    start=True, stop=True),
                    inc=sm.t_mm, wait=[(sm.v_ps, sm.v_ps.n)] + sum_gate)
                p.op("scalar", lambda s: s.activation(sum_row[:, :], ps_sum[:, :],
                                                      AF.Ln),
                     inc=sm.s_sr, wait=[(sm.t_mm, sm.t_mm.n)])
                p.op("scalar", lambda s: s.activation(rec_row[:, :], sum_row[:, :],
                                                      AF.Exp, scale=np.float32(-1.0)),
                     inc=sm.s_sr)
                p.op("tensor", lambda t: t.matmul(ps_misc[:, :], ctx.ones_1x128_f[:, :],
                                                  rec_row[:, :], start=True, stop=True),
                     inc=sm.t_mm, wait=[(sm.s_sr, sm.s_sr.n), (sm.s_rb, sm.s_rb.n)])
                p.op("scalar", lambda s: s.copy(rec_bc[:, :], ps_misc[:, :]),
                     inc=sm.s_rb, wait=[(sm.t_mm, sm.t_mm.n)])
                p.op("vector", lambda v, sbi=sbi, hh=hh: v.tensor_mul(
                    oh_sb[:, hh * 4096 + sbi * 512: hh * 4096 + (sbi + 1) * 512],
                    ps_oo[hh][:, :], rec_bc[:, :]),
                    inc=sm.v_on, wait=[(sm.s_rb, sm.s_rb.n)] +
                    (pair_resw if sbi == 0 else []))
                sum_gate = [(sm.s_sr, sm.s_sr.n)]
                psh_gate = [(sm.t_mm, sm.t_mm.n)]
        for hh in range(2):
            h = 2 * hp + hh
            if hh == 1:
                for c in range(NCH):
                    p.op("gpsimd", lambda e, c=c, h=h: e.dma_start(
                        xh_sb[:, c * 512:(c + 1) * 512],
                        x_src[c * 128:(c + 1) * 128, h * 512:(h + 1) * 512]),
                        inc=sm.d_xh, inc_k=16,
                        wait=[(sm.v_res, sm.v_res.n)] if c == 0 else ())
            rw = [(sm.v_on, sm.v_on.n - (1 - hh)), (sm.d_xh, sm.d_xh.n)]
            for c in range(NCH):
                p.op("vector", lambda v, c=c, hh=hh: v.tensor_add(
                    xh2_sb[:, c * 512:(c + 1) * 512], xh_sb[:, c * 512:(c + 1) * 512],
                    oh_sb[:, hh * 4096 + c: (hh + 1) * 4096: 8]),
                    inc=sm.v_res, wait=(rw if c == 0 else []) +
                    ([(sm.d_out, sm.d_out.n)] if c == 0 and hh == 1 else []))
                p.op("sync", lambda e, c=c, h=h: e.dma_start(
                    x_dst[c * 128:(c + 1) * 128, h * 512:(h + 1) * 512],
                    xh2_sb[:, c * 512:(c + 1) * 512]),
                    inc=sm.d_out, inc_k=16, wait=[(sm.v_res, sm.v_res.n)])
    p.op("sync", lambda e: e.nop(), wait=[(sm.d_out, sm.d_out.n)])
    p.run()

    es.close()
    esp2.close()


def plan_ln_block(nc, sm, ctx, p, x_src, X_sup, tb, sbq, tiles, psums,
                  xn_scr, gate_sb=None, glogc=None, glog=None, xw=None):
    xsq, xnt, rowbuf = tiles["xsq"], tiles["xnt"], tiles["rowbuf"]
    ps_s, ps_q, ps_A, ps_M, ps_pp = (psums[k] for k in ("s", "q", "A", "M", "pp"))
    row = lambda i: rowbuf[0:1, i * 512:(i + 1) * 512]
    rows = {k: row(i) for i, k in enumerate(("mu", "msq", "var", "t0", "t1", "A", "negMA"))}

    if xw is None:
        for c in range(NCH):
            p.op("sync", lambda e, c=c: e.dma_start(
                X_sup[:, c * 2048 + sbq * 512: c * 2048 + (sbq + 1) * 512],
                x_src[c * 128:(c + 1) * 128, tb * 512:(tb + 1) * 512]),
                inc=sm.d_x, inc_k=16)
        xw = [(sm.d_x, sm.d_x.n)]

    def stats_s(t):
        for c in range(NCH):
            ins = t.matmul(ps_s[:, :], ctx.onesf_col[:, :],
                           X_sup[:, c * 2048 + sbq * 512: c * 2048 + (sbq + 1) * 512],
                           start=(c == 0), stop=(c == 7))
        return ins
    p.op("tensor", stats_s, inc=sm.t_mm, wait=xw)
    for c in range(NCH):
        slot = c % 2
        sq_wait = ([(sm.t_mm, sm.t_mm.n - 1)] if c >= 2 else []) + (xw if c == 0 else [])
        p.op("scalar", lambda e, c=c, slot=slot: e.activation(
            xsq[:, slot * 512:(slot + 1) * 512],
            X_sup[:, c * 2048 + sbq * 512: c * 2048 + (sbq + 1) * 512], AF.Square),
            inc=sm.s_sq, wait=sq_wait)
        p.op("tensor", lambda t, c=c, slot=slot: t.matmul(
            ps_q[:, :], ctx.onesf_col[:, :], xsq[:, slot * 512:(slot + 1) * 512],
            start=(c == 0), stop=(c == 7)),
            inc=sm.t_mm, wait=[(sm.s_sq, sm.s_sq.n)])
    plan_ln_rows(p, sm, ps_s, ps_q, rows, with_ln2=False,
                 pre_wait=[(sm.t_mm, sm.t_mm.n)])
    p.op("tensor", lambda t: t.matmul(ps_A[:, :], ctx.ones_1x128_f[:, :],
                                      rows["A"], start=True, stop=True),
         inc=sm.t_mm, wait=[(sm.sv, sm.sv.n)])
    p.op("tensor", lambda t: t.matmul(ps_M[:, :], ctx.ones_1x128_f[:, :],
                                      rows["negMA"], start=True, stop=True),
         inc=sm.t_mm)
    bcw = [(sm.t_mm, sm.t_mm.n)]
    for c in range(NCH):
        slot = c % 2
        slot_free = []
        if c >= 2:
            slot_free.append((sm.d_xnw, sm.d_xnw.n - 16))
            if gate_sb is not None:
                slot_free.append((sm.t_gm, sm.t_gm.n - 4))
        p.op("vector", lambda e, c=c, slot=slot: e.tensor_mul(
            xnt[:, slot * 512:(slot + 1) * 512],
            X_sup[:, c * 2048 + sbq * 512: c * 2048 + (sbq + 1) * 512], ps_A[:, :]),
            wait=(bcw if c == 0 else []) + slot_free)
        p.op("vector", lambda e, c=c, slot=slot: e.tensor_add(
            xnt[:, slot * 512:(slot + 1) * 512], xnt[:, slot * 512:(slot + 1) * 512],
            ps_M[:, :]), inc=sm.v_xn0)
        p.op("gpsimd", lambda e, c=c, slot=slot: e.dma_start(
            xn_scr[c * 128:(c + 1) * 128, tb * 512:(tb + 1) * 512],
            xnt[:, slot * 512:(slot + 1) * 512]),
            inc=sm.d_xnw, inc_k=16, wait=[(sm.v_xn0, sm.v_xn0.n)])
        if gate_sb is not None:
            for st in range(4):
                gi = sm.t_gm.n
                gw = [(sm.v_gl, gi - 1)] if gi >= 2 else []
                p.op("tensor", lambda t, c=c, st=st, slot=slot, gi=gi: t.matmul(
                    ps_pp[gi % 2][0:128, 0:8],
                    xnt[:, slot * 512 + st * 128: slot * 512 + (st + 1) * 128],
                    gate_sb[:, c * 8:(c + 1) * 8], start=True, stop=True),
                    inc=sm.t_gm, wait=gw + [(sm.v_xn0, sm.v_xn0.n)])
                p.op("vector", lambda e, c=c, st=st, gi=gi: e.tensor_copy(
                    glogc[:, st * 64 + c * 8: st * 64 + (c + 1) * 8],
                    ps_pp[gi % 2][0:128, 0:8]),
                    inc=sm.v_gl, wait=[(sm.t_gm, sm.t_gm.n)])
    if gate_sb is not None:
        for st in range(4):
            stg = sbq * 4 + st
            p.op("vector", lambda e, st=st, stg=stg: e.reduce_sum(
                glog[:, stg * 8:(stg + 1) * 8].rearrange("p (e o) -> p e o", o=1),
                glogc[:, st * 64:(st + 1) * 64].rearrange("p (c e) -> p e c", e=8),
                axis=AX.X), inc=sm.v_gl, wait=[(sm.v_gl, sm.v_gl.n)])


def emit_phase1(nc, ctx, sm, x_src, x_dst, w):
    es = ExitStack()
    sb = lambda name, shape, dt: es.enter_context(nc.sbuf_tensor("p1_" + name, shape, dt))
    X_sup = sb("Xsup", [128, 8 * 2048], F32)
    w1_sb = sb("w1", [128, 8 * 2048], BF16)
    w2_sb = sb("w2", [128, 16 * 1024], BF16)
    h_sb = sb("h", [128, 16 * 512], BF16)
    xn_blk = sb("xnblk", [128, 2 * 8 * 512], BF16)
    xsq = sb("xsq", [128, 2 * 512], F32)
    xnt = sb("xnt", [128, 2 * 512], F32)
    rowbuf = sb("rowbuf", [1, 7 * 512], F32)
    glog = sb("glog", [128, 128], F32)
    glogc = sb("glogc", [128, 256], F32)
    ga = sb("ga", [128, 128], F32)
    gb = sb("gb", [128, 128], F32)
    gc = sb("gc", [128, 128], F32)
    gm1 = sb("gm1", [128, 16], F32)
    gm2 = sb("gm2", [128, 16], F32)
    gm3 = sb("gm3", [128, 16], F32)
    g_bf = sb("g_bf", [128, 128], BF16)
    g_sb = sb("g_sb", [8, 2048], BF16)
    bcg_sb = sb("bcg", [128, 512], BF16)
    b1e_row = sb("b1e", [1, 2 * 2048], BF16)
    b2e_row = sb("b2e", [1, 2 * 1024], BF16)
    gate_sb = sb("gate", [128, 64], F32)
    xn_scr = nc.dram_tensor("p1_xn_scr", [1024, 4096], BF16, kind="Internal").ap()

    esp = ExitStack()
    psb = lambda name, shape, dt=F32: esp.enter_context(
        nc.psum_tensor("p1_" + name, shape, dt))
    ps_s = psb("s", [1, 512])
    ps_q = psb("q", [1, 512])
    ps_A = psb("A", [128, 512])
    ps_M = psb("M", [128, 512])
    ps_p0 = psb("pp0", [128, 512])
    ps_p1 = psb("pp1", [128, 512])
    ps_pp = [ps_p0, ps_p1]
    ps_gt = psb("gt", [8, 128], BF16)
    ps_bc = psb("bc", [128, 512])
    tiles = dict(xsq=xsq, xnt=xnt, rowbuf=rowbuf)
    psums = dict(s=ps_s, q=ps_q, A=ps_A, M=ps_M, pp=ps_pp)

    p = Plan(nc)
    p.op("sync", lambda e: e.dma_start(
        gate_sb[:, :].rearrange("p (c e) -> p c e", e=8),
        w["L0_ffn_gate"].rearrange("(c p) e -> p c e", p=128)),
        inc=sm.d_g, inc_k=16)
    p.op("sync", lambda e: e.nop(), wait=[(sm.d_g, sm.d_g.n)])
    p.run()

    for u in range(2):
        xbase = sm.d_x.n
        xws = []
        pld = Plan(nc)
        for sbq in range(4):
            for c in range(NCH):
                pld.op("sync", lambda e, c=c, sbq=sbq, u=u: e.dma_start(
                    X_sup[:, c * 2048 + sbq * 512: c * 2048 + (sbq + 1) * 512],
                    x_src[c * 128:(c + 1) * 128,
                          (u * 4 + sbq) * 512:(u * 4 + sbq + 1) * 512]),
                    inc=sm.d_x, inc_k=16)
            xws.append([(sm.d_x, sm.d_x.n)])
        pld.run()
        for sbq in range(4):
            p = Plan(nc)
            plan_ln_block(nc, sm, ctx, p, x_src, X_sup, u * 4 + sbq, sbq, tiles,
                          psums, xn_scr, gate_sb, glogc, glog, xw=xws[sbq])
            p.run()

        p = Plan(nc)

        def vstep(build, wait=()):
            p.op("vector", build, inc=sm.sv, wait=list(wait) + [(sm.sv, sm.sv.n)])
        g3 = lambda t: t[:, :].rearrange("p (s e) -> p s e", e=8)
        bc16 = lambda t: t[:, :].unsqueeze(2).broadcast_to([128, 16, 8])
        vstep(lambda e: e.reduce_max(gm1[:, :], g3(glog), axis=AX.X),
              wait=[(sm.v_gl, sm.v_gl.n)])
        vstep(lambda e: e.tensor_sub(g3(ga), g3(glog), bc16(gm1)))
        p.op("scalar", lambda e: e.activation(gb[:, :], ga[:, :], AF.Exp),
             inc=sm.ss, wait=[(sm.sv, sm.sv.n)])
        vstep(lambda e: e.reduce_sum(gm2[:, :], g3(gb), axis=AX.X),
              wait=[(sm.ss, sm.ss.n)])
        vstep(lambda e: e.reciprocal(gm3[:, :], gm2[:, :]))
        vstep(lambda e: e.tensor_mul(g3(ga), g3(gb), bc16(gm3)))
        vstep(lambda e: e.reduce_max(gm1[:, :], g3(ga), axis=AX.X))
        vstep(lambda e: e.tensor_tensor(g3(gb), g3(ga), bc16(gm1), ALU.is_equal))
        vstep(lambda e: e.scalar_tensor_tensor(g3(gb), g3(gb), -1e9, g3(ga),
                                               ALU.mult, ALU.add))
        vstep(lambda e: e.reduce_max(gm2[:, :], g3(gb), axis=AX.X))
        vstep(lambda e: e.tensor_tensor(g3(gb), g3(ga), bc16(gm2), ALU.is_ge))
        vstep(lambda e: e.tensor_mul(g3(ga), g3(ga), g3(gb)))
        vstep(lambda e: e.reduce_sum(gm1[:, :], g3(ga), axis=AX.X))
        vstep(lambda e: e.tensor_scalar_max(gm1[:, :], gm1[:, :], 1e-12))
        vstep(lambda e: e.reciprocal(gm2[:, :], gm1[:, :]))
        vstep(lambda e: e.tensor_mul(g3(ga), g3(ga), bc16(gm2)))
        p.op("vector", lambda e: e.tensor_copy(g_bf[:, :], ga[:, :]),
             inc=sm.v_gbf, wait=[(sm.sv, sm.sv.n)])
        for st in range(16):
            p.op("tensor", lambda t, st=st: t.transpose(
                ps_gt[:, :], g_bf[:, st * 8:(st + 1) * 8], ctx.identb[:, :]),
                inc=sm.t_tr,
                wait=[(sm.v_gbf, sm.v_gbf.n)] + ([(sm.s_tr, sm.s_tr.n)] if st else []))
            p.op("scalar", lambda e, st=st: e.copy(
                g_sb[0:8, st * 128:(st + 1) * 128], ps_gt[:, :]),
                inc=sm.s_tr, wait=[(sm.t_tr, sm.t_tr.n)])
        p.run()

        wmark = {}
        p = Plan(nc)
        for e_i in range(8):
            if u == 0 and e_i == 0:
                def ld0(g):
                    g.dma_start(w1_sb[:, :].rearrange("p (c f) -> p c f", f=2048),
                                w["L0_ffn_W1"][0].rearrange("(c p) f -> p c f", p=128)
                                ).then_inc(sm.d_wB.h, 16)
                    g.dma_start(w2_sb[:, :].rearrange("p (c f) -> p c f", f=1024),
                                w["L0_ffn_W2"][0].rearrange("(c p) f -> p c f", p=128)
                                ).then_inc(sm.d_wB.h, 16)
                    g.dma_start(b1e_row[:, 0:2048],
                                w["L0_ffn_b1"][0:1, :]).then_inc(sm.d_wB.h, 16)
                    return g.dma_start(b2e_row[:, 0:1024],
                                       w["L0_ffn_b2"][0:1, :])
                sm.d_wB.n += 48
                p.op("gpsimd", ld0, inc=sm.d_wB, inc_k=16)
            wweights = [(sm.d_wB, wmark.get((u, e_i), sm.d_wB.n))]
            for c in range(NCH):
                p.op("sync", lambda e, c=c, u=u: e.dma_start(
                    xn_blk[:, c * 512:(c + 1) * 512],
                    xn_scr[c * 128:(c + 1) * 128, u * 2048: u * 2048 + 512]),
                    inc=sm.d_xn2, inc_k=16,
                    wait=[(sm.d_xnw, sm.d_xnw.n), (sm.t_mm, sm.t_mm.n)] if c == 0 else ())
            bank_records = []
            for s_blk in range(4):
                slot = s_blk % 2
                xnw2 = [(sm.d_xn2, sm.d_xn2.n)]
                p.op("tensor", lambda t, e_i=e_i, s_blk=s_blk, u=u: t.matmul(
                    ps_bc[:, :], ctx.sel_all[:, e_i * 128:(e_i + 1) * 128],
                    g_sb[0:8, s_blk * 512:(s_blk + 1) * 512], start=True, stop=True),
                    inc=sm.t_bc, wait=[(sm.s_bc, sm.s_bc.n)])
                sbc_w = [(sm.t_bc, sm.t_bc.n), (sm.v_h, sm.v_h.n)]
                p.op("scalar", lambda e: e.copy(bcg_sb[:, :], ps_bc[:, :]),
                     inc=sm.s_bc, wait=sbc_w)
                bcg_w = [(sm.s_bc, sm.s_bc.n)]
                for ft in range(16):
                    G = len(bank_records)
                    gw = ([bank_records[G - 2]] if G >= 2 else []) +                          (wweights + xnw2 if ft == 0 else [])

                    def t_h(t, ft=ft, slot=slot, G=G, e_i=e_i):
                        for c in range(NCH):
                            t.matmul(ps_pp[G % 2][:, :],
                                     w1_sb[:, c * 2048 + ft * 128: c * 2048 + (ft + 1) * 128],
                                     xn_blk[:, slot * 4096 + c * 512: slot * 4096 + (c + 1) * 512],
                                     start=(c == 0), stop=False)
                        return t.matmul(ps_pp[G % 2][:, :],
                                        b1e_row[0:1, (e_i % 2) * 2048 + ft * 128:
                                                (e_i % 2) * 2048 + (ft + 1) * 128],
                                        ctx.ones_row_bf[:, :], start=False, stop=True)
                    p.op("tensor", t_h, inc=sm.t_mm, wait=gw)
                    hev_w = [(sm.t_mm, sm.t_mm.n)] + bcg_w
                    p.op("vector", lambda e, ft=ft, G=G: e.scalar_tensor_tensor(
                        h_sb[:, ft * 512:(ft + 1) * 512], ps_pp[G % 2][:, :], 0.0,
                        bcg_sb[:, :], ALU.max, ALU.mult),
                        inc=sm.v_h, wait=hev_w)
                    bank_records.append((sm.v_h, sm.v_h.n))
                t_after_h_last = sm.t_mm.n
                if s_blk < 3:
                    t_after_h = sm.t_mm.n
                    for c in range(NCH):
                        p.op("sync", lambda e, c=c, u=u, s_blk=s_blk, slot=slot: e.dma_start(
                            xn_blk[:, (1 - slot) * 4096 + c * 512: (1 - slot) * 4096 + (c + 1) * 512],
                            xn_scr[c * 128:(c + 1) * 128,
                                   u * 2048 + (s_blk + 1) * 512: u * 2048 + (s_blk + 2) * 512]),
                            inc=sm.d_xn2, inc_k=16,
                            wait=[(sm.t_mm, t_after_h)] if c == 0 else ())
                for dt in range(8):
                    G = len(bank_records)
                    gw = [bank_records[G - 2]] if G >= 2 else []

                    def t_o(t, dt=dt, G=G, e_i=e_i):
                        for hc in range(16):
                            t.matmul(ps_pp[G % 2][:, :],
                                     w2_sb[:, hc * 1024 + dt * 128: hc * 1024 + (dt + 1) * 128],
                                     h_sb[:, hc * 512:(hc + 1) * 512],
                                     start=(hc == 0), stop=False)
                        return t.matmul(ps_pp[G % 2][:, :],
                                        b2e_row[0:1, (e_i % 2) * 1024 + dt * 128:
                                                (e_i % 2) * 1024 + (dt + 1) * 128],
                                        bcg_sb[0:1, :], start=False, stop=True)
                    p.op("tensor", t_o, inc=sm.t_mm,
                         wait=gw + ([(sm.v_h, sm.v_h.n)] if dt == 0 else []))
                    p.op("vector", lambda e, dt=dt, s_blk=s_blk, G=G: e.tensor_add(
                        X_sup[:, dt * 2048 + s_blk * 512: dt * 2048 + (s_blk + 1) * 512],
                        ps_pp[G % 2][:, :],
                        X_sup[:, dt * 2048 + s_blk * 512: dt * 2048 + (s_blk + 1) * 512]),
                        inc=sm.v_o, wait=[(sm.t_mm, sm.t_mm.n)])
                    bank_records.append((sm.v_o, sm.v_o.n))
                if s_blk == 3 and not (u == 1 and e_i == 7):
                    ne = (e_i + 1) % 8
                    nu = u if e_i < 7 else 1
                    t_now = sm.t_mm.n
                    p.op("gpsimd", lambda g, ne=ne, nu=nu: g.dma_start(
                        w1_sb[:, :].rearrange("p (c f) -> p c f", f=2048),
                        w["L0_ffn_W1"][ne].rearrange("(c p) f -> p c f", p=128)),
                        inc=sm.d_wB, inc_k=16, wait=[(sm.t_mm, t_after_h_last)])
                    p.op("gpsimd", lambda g, ne=ne: g.dma_start(
                        b1e_row[:, (ne % 2) * 2048:(ne % 2) * 2048 + 2048],
                        w["L0_ffn_b1"][ne:ne + 1, :]), inc=sm.d_wB, inc_k=16)
                    p.op("gpsimd", lambda g, ne=ne: g.dma_start(
                        b2e_row[:, (ne % 2) * 1024:(ne % 2) * 1024 + 1024],
                        w["L0_ffn_b2"][ne:ne + 1, :]), inc=sm.d_wB, inc_k=16)
                    p.op("gpsimd", lambda g, ne=ne: g.dma_start(
                        w2_sb[:, :].rearrange("p (c f) -> p c f", f=1024),
                        w["L0_ffn_W2"][ne].rearrange("(c p) f -> p c f", p=128)),
                        inc=sm.d_wB, inc_k=16, wait=[(sm.t_mm, t_now)])
                    wmark[(nu, ne)] = sm.d_wB.n
        for c in range(NCH):
            p.op("sync", lambda e, c=c, u=u: e.dma_start(
                x_dst[c * 128:(c + 1) * 128, u * 2048:(u + 1) * 2048],
                X_sup[:, c * 2048:(c + 1) * 2048]),
                inc=sm.d_out, inc_k=16,
                wait=[(sm.v_o, sm.v_o.n)] if c == 0 else ())
        p.op("sync", lambda e: e.nop(), wait=[(sm.d_out, sm.d_out.n)])
        p.run()

    es.close()
    esp.close()


def emit_phase2(nc, ctx, sm, x_src, x_dst, w):
    es = ExitStack()
    sb = lambda name, shape, dt: es.enter_context(nc.sbuf_tensor("p2_" + name, shape, dt))
    v_tok = sb("v_tok", [128, 32 * 1040], BF16)
    cmask = sb("cmask", [128, 4 * 512], BF16)
    selhalf = sb("selhalf", [128, 2], BF16)
    self32 = sb("self32", [128, 2], F32)
    halfsel = sb("halfsel", [2, 128], F32)
    qn_scr = nc.dram_tensor("p2_qn_scr", [1024, 4096], BF16, kind="Internal").ap()
    kn_scr = nc.dram_tensor("p2_kn_scr", [1024, 4096], BF16, kind="Internal").ap()

    p = Plan(nc)

    def g_fn(g):
        g.memset(v_tok[:, :], 1.0)
        for a in range(4):
            mv = cmask[:, a * 512:(a + 1) * 512]
            g.memset(mv, 1.0)
            g.affine_select(mv, mv, [[1, 512]], ALU.is_ge, 0.0,
                            base=-128 * a, channel_multiplier=-1)
        g.memset(selhalf[:, :], 0.0)
        g.memset(selhalf[0:64, 0:1], 1.0)
        g.memset(selhalf[64:128, 1:2], 1.0)
        g.memset(self32[:, :], 0.0)
        g.memset(self32[0:64, 0:1], 1.0)
        g.memset(self32[64:128, 1:2], 1.0)
        return g.nop()
    p.op("gpsimd", g_fn)
    p.run()

    es2 = ExitStack()
    sb2 = lambda name, shape, dt: es2.enter_context(nc.sbuf_tensor("p2p_" + name, shape, dt))
    wq_sb = sb2("wq", [128, 8 * 1024], BF16)
    wk_sb = sb2("wk", [128, 8 * 1024], BF16)
    wv_sb = sb2("wv", [128, 8 * 1024], BF16)
    rowbuf = sb2("rowbuf", [1, 7 * 512], F32)
    xblk = sb2("xblk", [128, 8 * 512], F32)
    xn_blk = sb2("xn", [128, 8 * 512], BF16)
    xsq = sb2("xsq", [128, 2 * 512], F32)
    xnt = sb2("xnt", [128, 2 * 512], F32)
    q_full = sb2("qfull", [128, 8 * 512], BF16)
    sq_full = sb2("sqfull", [128, 8 * 512], BF16)
    sr2 = sb2("sr2", [2, 2 * 512], F32)
    rn2 = sb2("rn2", [2, 2 * 512], F32)

    esp = ExitStack()
    psb = lambda name, shape, dt=F32: esp.enter_context(
        nc.psum_tensor("p2_" + name, shape, dt))
    ps_s = psb("s", [1, 512])
    ps_q = psb("q", [1, 512])
    ps_A = psb("A", [128, 512])
    ps_M = psb("M", [128, 512])
    ps_p0 = psb("pp0", [128, 512])
    ps_p1 = psb("pp1", [128, 512])
    ps_pp = [ps_p0, ps_p1]
    ps_ss = psb("ss", [2, 512])
    ps_fac = psb("fac", [128, 512])

    p = Plan(nc)
    for nm, dst in (("L1_attn_Wq", wq_sb), ("L1_attn_Wk", wk_sb), ("L1_attn_Wv", wv_sb)):
        p.op("gpsimd", lambda g, nm=nm, dst=dst: g.dma_start(
            dst[:, :].rearrange("p (c f) -> p c f", f=1024),
            w[nm].rearrange("(c p) f -> p c f", p=128)), inc=sm.d_w, inc_k=16)
    p.op("tensor", lambda t: t.transpose(ps_ss[0:2, 0:128], self32[:, :],
                                         ctx.identf[:, :]), inc=sm.t_mm)
    p.op("scalar", lambda e: e.copy(halfsel[:, :], ps_ss[0:2, 0:128]),
         inc=sm.s_ev, wait=[(sm.t_mm, sm.t_mm.n)])
    p.run()

    row = lambda i: rowbuf[0:1, i * 512:(i + 1) * 512]
    rows = {k: row(i) for i, k in enumerate(
        ("mu", "msq", "var", "t0", "t1", "A", "negMA"))}

    p = Plan(nc)
    t_vproj_end = 0
    t_statq_end = 0
    for sbi in range(8):
        xdma_gate = [(sm.v_xn0, sm.v_xn0.n), (sm.s_sq, sm.s_sq.n),
                     (sm.t_mm, t_statq_end)] if sbi else []
        for c in range(NCH):
            p.op("sync", lambda e, c=c, sbi=sbi: e.dma_start(
                xblk[:, c * 512:(c + 1) * 512],
                x_src[c * 128:(c + 1) * 128, sbi * 512:(sbi + 1) * 512]),
                inc=sm.d_x, inc_k=16, wait=xdma_gate if c == 0 else ())
        xw = [(sm.d_x, sm.d_x.n)]

        def stats_s(t):
            for c in range(NCH):
                ins = t.matmul(ps_s[:, :], ctx.onesf_col[:, :],
                               xblk[:, c * 512:(c + 1) * 512],
                               start=(c == 0), stop=(c == 7))
            return ins
        p.op("tensor", stats_s, inc=sm.t_mm, wait=xw + [(sm.sv, sm.sv.n)])
        for c in range(NCH):
            slot = c % 2
            sq_wait = ([(sm.t_mm, sm.t_mm.n - 1)] if c >= 2 else []) + (xw if c == 0 else [])
            if sbi and c < 2:
                sq_wait = sq_wait + [(sm.t_mm, t_statq_end)]
            p.op("scalar", lambda e, c=c, slot=slot: e.activation(
                xsq[:, slot * 512:(slot + 1) * 512], xblk[:, c * 512:(c + 1) * 512],
                AF.Square), inc=sm.s_sq, wait=sq_wait)
            p.op("tensor", lambda t, c=c, slot=slot: t.matmul(
                ps_q[:, :], ctx.onesf_col[:, :], xsq[:, slot * 512:(slot + 1) * 512],
                start=(c == 0), stop=(c == 7)),
                inc=sm.t_mm, wait=[(sm.s_sq, sm.s_sq.n)])
        t_statq_end = sm.t_mm.n
        plan_ln_rows(p, sm, ps_s, ps_q, rows, with_ln2=False,
                     pre_wait=[(sm.t_mm, sm.t_mm.n)])
        p.op("tensor", lambda t: t.matmul(ps_A[:, :], ctx.ones_1x128_f[:, :],
                                          rows["A"], start=True, stop=True),
             inc=sm.t_mm, wait=[(sm.sv, sm.sv.n), (sm.v_xn0, sm.v_xn0.n)])
        p.op("tensor", lambda t: t.matmul(ps_M[:, :], ctx.ones_1x128_f[:, :],
                                          rows["negMA"], start=True, stop=True),
             inc=sm.t_mm)
        bcw = [(sm.t_mm, sm.t_mm.n)]
        for c in range(NCH):
            slot = c % 2
            slot_free = [(sm.s_xn, sm.s_xn.n - 1)] if sm.s_xn.n >= 1 else []
            p.op("vector", lambda e, c=c, slot=slot: e.tensor_mul(
                xnt[:, slot * 512:(slot + 1) * 512], xblk[:, c * 512:(c + 1) * 512],
                ps_A[:, :]), wait=(bcw if c == 0 else []) + slot_free)
            p.op("vector", lambda e, c=c, slot=slot: e.tensor_add(
                xnt[:, slot * 512:(slot + 1) * 512], xnt[:, slot * 512:(slot + 1) * 512],
                ps_M[:, :]), inc=sm.v_xn0)
            xng = [(sm.t_mm, t_vproj_end)] if (sbi and c == 0) else []
            p.op("scalar", lambda e, c=c, slot=slot: e.activation(
                xn_blk[:, c * 512:(c + 1) * 512], xnt[:, slot * 512:(slot + 1) * 512],
                AF.Copy), inc=sm.s_xn, wait=[(sm.v_xn0, sm.v_xn0.n)] + xng)
        xnw = [(sm.s_xn, sm.s_xn.n)]

        for wsb, scr in ((wq_sb, qn_scr), (wk_sb, kn_scr)):
            stage_a = []
            stage_b = []

            def plan_chain_a(fc):
                sl = fc % 2
                p.op("tensor", lambda t, fc=fc: t.matmul(
                    ps_ss[:, :], selhalf[:, :], sq_full[:, fc * 512:(fc + 1) * 512],
                    start=True, stop=True),
                    inc=sm.t_ss, wait=[(sm.sv, sm.sv.n), (sm.s_ev, sm.s_ev.n)])
                p.op("vector", lambda e, sl=sl: e.tensor_scalar_max(
                    sr2[:, sl * 512:(sl + 1) * 512], ps_ss[:, :], 1e-24),
                    inc=sm.sv, wait=[(sm.t_ss, sm.t_ss.n), (sm.sv, sm.sv.n)])
                p.op("scalar", lambda e, sl=sl: e.activation(
                    rn2[:, sl * 512:(sl + 1) * 512], sr2[:, sl * 512:(sl + 1) * 512],
                    AF.Ln), inc=sm.ss, wait=[(sm.sv, sm.sv.n)])
                p.op("scalar", lambda e, sl=sl: e.activation(
                    rn2[:, sl * 512:(sl + 1) * 512], rn2[:, sl * 512:(sl + 1) * 512],
                    AF.Exp, scale=np.float32(-0.5)), inc=sm.ss)

            def plan_chain_b(fc, scr, sbi):
                sl = fc % 2
                p.op("tensor", lambda t, sl=sl: t.matmul(
                    ps_fac[:, :], halfsel[:, :], rn2[:, sl * 512:(sl + 1) * 512],
                    start=True, stop=True),
                    inc=sm.t_fac, wait=[(sm.ss, sm.ss.n), (sm.v_ev, sm.v_ev.n)])
                p.op("vector", lambda e, fc=fc: e.tensor_mul(
                    q_full[:, fc * 512:(fc + 1) * 512],
                    q_full[:, fc * 512:(fc + 1) * 512], ps_fac[:, :]),
                    inc=sm.v_ev, wait=[(sm.t_fac, sm.t_fac.n)])
                p.op("sync", lambda e, fc=fc, sbi=sbi, scr=scr: e.dma_start(
                    scr[fc * 128:(fc + 1) * 128, sbi * 512:(sbi + 1) * 512],
                    q_full[:, fc * 512:(fc + 1) * 512]),
                    inc=sm.d_qk, inc_k=16, wait=[(sm.v_ev, sm.v_ev.n)])

            for fc in range(8):
                pj = fc % 2

                def t_proj(t, wsb=wsb, fc=fc, pj=pj):
                    for c in range(NCH):
                        ins = t.matmul(
                            ps_pp[pj][:, :],
                            wsb[:, c * 1024 + fc * 128: c * 1024 + (fc + 1) * 128],
                            xn_blk[:, c * 512:(c + 1) * 512],
                            start=(c == 0), stop=(c == 7))
                    return ins
                pw = [(sm.s_ev, sm.s_ev.n - 2)] if fc >= 2 else []
                p.op("tensor", t_proj, inc=sm.t_mm,
                     wait=pw + (xnw if fc == 0 else []))
                mmw = [(sm.t_mm, sm.t_mm.n)]
                dmaw = [(sm.d_qk, sm.d_qk.n - 32)] if sm.d_qk.n >= 32 else []
                p.op("scalar", lambda e, fc=fc, pj=pj: e.copy(
                    q_full[:, fc * 512:(fc + 1) * 512], ps_pp[pj][:, :]),
                    inc=sm.s_ev, wait=mmw + dmaw)
                p.op("scalar", lambda e, fc=fc, pj=pj: e.activation(
                    sq_full[:, fc * 512:(fc + 1) * 512], ps_pp[pj][:, :],
                    AF.Square), inc=sm.s_ev)
                stage_a.append(fc)
                if len(stage_a) >= 2:
                    plan_chain_a(stage_a.pop(0))
                    stage_b.append(fc - 1)
                if len(stage_b) >= 2:
                    plan_chain_b(stage_b.pop(0), scr, sbi)
            while stage_a:
                fca = stage_a.pop(0)
                plan_chain_a(fca)
                stage_b.append(fca)
            while stage_b:
                plan_chain_b(stage_b.pop(0), scr, sbi)
        vrec = []
        for tt in range(4):
            for half in range(2):
                gi = len(vrec)
                ttg = sbi * 4 + tt

                def t_v(t, tt=tt, half=half, gi=gi):
                    for c in range(NCH):
                        ins = t.matmul(
                            ps_pp[gi % 2][:, :],
                            xn_blk[:, c * 512 + tt * 128: c * 512 + (tt + 1) * 128],
                            wv_sb[:, c * 1024 + half * 512: c * 1024 + (half + 1) * 512],
                            start=(c == 0), stop=(c == 7))
                    return ins
                vw = [vrec[gi - 2]] if gi >= 2 else [(sm.v_ev, sm.v_ev.n)]
                p.op("tensor", t_v, inc=sm.t_mm, wait=vw)
                base = ttg * 1040 + half * 8 * 65

                def e_v(e, base=base, gi=gi):
                    return e.tensor_copy(
                        v_tok[:, base: base + 520].rearrange("p (h j) -> p h j", j=65)[:, :, 0:64],
                        ps_pp[gi % 2][:, :].rearrange("p (h j) -> p h j", j=64))
                p.op("vector", e_v, inc=sm.v_ev, wait=[(sm.t_mm, sm.t_mm.n)])
                vrec.append((sm.v_ev, sm.v_ev.n))
        t_vproj_end = sm.t_mm.n
    p.run()
    es2.close()
    esp.close()

    es3 = ExitStack()
    sb3 = lambda name, shape, dt: es3.enter_context(nc.sbuf_tensor("p2a_" + name, shape, dt))
    obuf = sb3("obuf", [128, 8 * 4096], BF16)
    es3a = ExitStack()
    sb3a = lambda name, shape, dt: es3a.enter_context(nc.sbuf_tensor("p2a2_" + name, shape, dt))
    qh2 = sb3a("qh2", [64, 2 * 4096], BF16)
    kh2 = sb3a("kh2", [64, 2 * 4096], BF16)
    exp_sb = sb3a("exp", [128, 2 * 1024], BF16)
    sum_row = sb3a("sumrow", [1, 512], F32)
    rec_row = sb3a("recrow", [1, 512], F32)
    fac_sb = sb3a("fac", [64, 512], F32)

    esp3 = ExitStack()
    psb3 = lambda name, shape, dt=F32: esp3.enter_context(
        nc.psum_tensor("p2a_" + name, shape, dt))
    ps_sc = [psb3("scA", [128, 1024]), psb3("scB", [128, 1024])]
    ps_oo = [psb3("o0", [65, 512]), psb3("o1", [65, 512])]
    ps_f2 = psb3("f2", [64, 512])

    n_qk = sm.d_qk.n
    jglob = [0]
    p = Plan(nc)
    for hp in range(8):
        qk_gate = [(sm.t_sc, sm.t_sc.n), (sm.t_pv, sm.t_pv.n)]
        for hh in range(2):
            h = 2 * hp + hh
            p.op("sync", lambda e, h=h, hh=hh: e.dma_start(
                qh2[:, hh * 4096:(hh + 1) * 4096], qn_scr[h * 64:(h + 1) * 64, :]),
                inc=sm.d_qh, inc_k=16,
                wait=([(sm.d_qk, n_qk)] if h == 0 else []) +
                     (qk_gate if (hh == 0 and hp > 0) else []))
            p.op("sync", lambda e, h=h, hh=hh: e.dma_start(
                kh2[:, hh * 4096:(hh + 1) * 4096], kn_scr[h * 64:(h + 1) * 64, :]),
                inc=sm.d_qh, inc_k=16)
        qkw = [(sm.d_qh, sm.d_qh.n)]
        for ib in range(8):
            njs = 4 * ib + 4
            on_gates = {0: [(sm.v_on, sm.v_on.n - 1)] if sm.v_on.n >= 1 else [],
                        1: [(sm.v_on, sm.v_on.n)]}
            pending = None
            for bt in range(njs // 2):
                for hh in range(2):
                    h = 2 * hp + hh
                    j = jglob[0]
                    J0 = 2 * bt
                    scw = ([(sm.s_exp, j - 1)] if j >= 2 else []) + \
                        (qkw if (ib == 0 and bt == 0 and hh == 0) else [])

                    def t_sc(t, J0=J0, ib=ib, j=j, hh=hh):
                        t.matmul(ps_sc[j % 2][:, 0:512],
                                 kh2[:, hh * 4096 + J0 * 128: hh * 4096 + (J0 + 1) * 128],
                                 qh2[:, hh * 4096 + ib * 512: hh * 4096 + (ib + 1) * 512],
                                 start=True, stop=True)
                        return t.matmul(
                            ps_sc[j % 2][:, 512:1024],
                            kh2[:, hh * 4096 + (J0 + 1) * 128: hh * 4096 + (J0 + 2) * 128],
                            qh2[:, hh * 4096 + ib * 512: hh * 4096 + (ib + 1) * 512],
                            start=True, stop=True)
                    p.op("tensor", t_sc, inc=sm.t_sc, wait=scw)
                    jglob[0] += 1
                    if pending is not None:
                        p.op("tensor", pending[0], inc=sm.t_pv, wait=pending[1])
                        pending = None
                    p.op("scalar", lambda s, j=j: s.activation(
                        exp_sb[:, (j % 2) * 1024:(j % 2 + 1) * 1024], ps_sc[j % 2][:, :],
                        AF.Exp, bias=np.float32(-8.0), scale=np.float32(8.0)),
                        inc=sm.s_exp, wait=[(sm.t_sc, sm.t_sc.n)])
                    pvw = [(sm.s_exp, sm.s_exp.n)]
                    for q2 in range(2):
                        J = J0 + q2
                        if J >= 4 * ib:
                            a2 = J - 4 * ib
                            p.op("vector", lambda v, j=j, a2=a2, q2=q2: v.tensor_mul(
                                exp_sb[:, (j % 2) * 1024 + q2 * 512: (j % 2) * 1024 + (q2 + 1) * 512],
                                exp_sb[:, (j % 2) * 1024 + q2 * 512: (j % 2) * 1024 + (q2 + 1) * 512],
                                cmask[:, a2 * 512:(a2 + 1) * 512]),
                                inc=sm.v_mk, wait=[(sm.s_exp, sm.s_exp.n)])
                            pvw = [(sm.v_mk, sm.v_mk.n)]

                    def t_pv(t, J0=J0, ib=ib, h=h, j=j, njs=njs, hh=hh):
                        for q2 in range(2):
                            J = J0 + q2
                            ins = t.matmul(
                                ps_oo[hh][:, :],
                                v_tok[:, J * 1040 + h * 65: J * 1040 + (h + 1) * 65],
                                exp_sb[:, (j % 2) * 1024 + q2 * 512: (j % 2) * 1024 + (q2 + 1) * 512],
                                start=(J == 0), stop=(J == njs - 1))
                        return ins
                    pending = (t_pv, pvw + (on_gates[hh] if bt == 0 else []))
            if pending is not None:
                p.op("tensor", pending[0], inc=sm.t_pv, wait=pending[1])
                pending = None
            for hh in range(2):
                h = 2 * hp + hh
                p.op("scalar", lambda s, hh=hh: s.activation(
                    sum_row[:, :], ps_oo[hh][64:65, :], AF.Ln),
                    inc=sm.s_sr, wait=[(sm.t_pv, sm.t_pv.n)])
                p.op("scalar", lambda s: s.activation(rec_row[:, :], sum_row[:, :],
                                                      AF.Exp, scale=np.float32(-1.0)),
                     inc=sm.s_sr)
                p.op("tensor", lambda t: t.matmul(ps_f2[:, :], ctx.ones_1x128_f[:, 0:64],
                                                  rec_row[:, :], start=True, stop=True),
                     inc=sm.t_fac, wait=[(sm.s_sr, sm.s_sr.n), (sm.s_rb, sm.s_rb.n)])
                p.op("scalar", lambda s: s.copy(fac_sb[:, :], ps_f2[:, :]),
                     inc=sm.s_rb, wait=[(sm.t_fac, sm.t_fac.n)])
                c0 = h // 2
                r0 = (h % 2) * 64
                p.op("vector", lambda v, c0=c0, r0=r0, ib=ib, hh=hh: v.tensor_mul(
                    obuf[r0:r0 + 64, c0 * 4096 + ib * 512: c0 * 4096 + (ib + 1) * 512],
                    ps_oo[hh][0:64, :], fac_sb[:, :]),
                    inc=sm.v_on, wait=[(sm.s_rb, sm.s_rb.n)])
    p.run()

    es3a.close()
    es3b = ExitStack()
    sb3b = lambda name, shape, dt: es3b.enter_context(nc.sbuf_tensor("p2b_" + name, shape, dt))
    wo_sb = sb3b("wo", [128, 8 * 1024], BF16)
    xblk2 = sb3b("xblk2", [128, 8 * 512], F32)
    p = Plan(nc)
    p.op("gpsimd", lambda g: g.dma_start(
        wo_sb[:, :].rearrange("p (c f) -> p c f", f=1024),
        w["L1_attn_Wo"].rearrange("(c p) f -> p c f", p=128)), inc=sm.d_w, inc_k=16)
    p.run()
    esp3.close()
    esp4 = ExitStack()
    psb4 = lambda name, shape, dt=F32: esp4.enter_context(
        nc.psum_tensor("p2e_" + name, shape, dt))
    ps_ee = [psb4("e0", [128, 512]), psb4("e1", [128, 512])]
    ow = [(sm.v_on, sm.v_on.n), (sm.d_w, sm.d_w.n)]
    erec = []
    for sbi in range(8):
        p = Plan(nc)
        for c in range(NCH):
            p.op("sync", lambda e, c=c, sbi=sbi: e.dma_start(
                xblk2[:, c * 512:(c + 1) * 512],
                x_src[c * 128:(c + 1) * 128, sbi * 512:(sbi + 1) * 512]),
                inc=sm.d_x, inc_k=16)
        xw2 = [(sm.d_x, sm.d_x.n)]
        for dt in range(8):
            gi = len(erec)

            def t_e(t, dt=dt, sbi=sbi, gi=gi):
                for c in range(NCH):
                    ins = t.matmul(
                        ps_ee[gi % 2][:, :],
                        wo_sb[:, c * 1024 + dt * 128: c * 1024 + (dt + 1) * 128],
                        obuf[:, c * 4096 + sbi * 512: c * 4096 + (sbi + 1) * 512],
                        start=(c == 0), stop=(c == 7))
                return ins
            ew = ([erec[gi - 2]] if gi >= 2 else []) + (ow if gi == 0 else [])
            p.op("tensor", t_e, inc=sm.t_mm, wait=ew)
            p.op("vector", lambda v, dt=dt, gi=gi: v.tensor_add(
                xblk2[:, dt * 512:(dt + 1) * 512], ps_ee[gi % 2][:, :],
                xblk2[:, dt * 512:(dt + 1) * 512]),
                inc=sm.v_res, wait=[(sm.t_mm, sm.t_mm.n)] + (xw2 if dt == 0 else []))
            erec.append((sm.v_res, sm.v_res.n))
            p.op("sync", lambda e, dt=dt, sbi=sbi: e.dma_start(
                x_dst[dt * 128:(dt + 1) * 128, sbi * 512:(sbi + 1) * 512],
                xblk2[:, dt * 512:(dt + 1) * 512]),
                inc=sm.d_out, inc_k=16, wait=[(sm.v_res, sm.v_res.n)])
        p.run()
    p = Plan(nc)
    p.op("sync", lambda e: e.nop(), wait=[(sm.d_out, sm.d_out.n)])
    p.run()
    es3b.close()
    es3.close()
    esp4.close()
    es.close()


def emit_phase3(nc, ctx, sm, x_src, x_dst, w):
    GC = 1.0 / math.e
    GL = 0.5 - math.log(4.0 / 3.0)
    GU = 0.5
    es = ExitStack()
    sb = lambda name, shape, dt: es.enter_context(nc.sbuf_tensor("p3_" + name, shape, dt))
    X_sup = sb("Xsup", [128, 8 * 2048], F32)
    wg_sb = sb("wg", [128, 8 * 1024], BF16)
    wu_sb = sb("wu", [128, 8 * 1024], BF16)
    wd_sb = sb("wd", [128, 8 * 1024], BF16)
    h_sb = sb("h", [128, 8 * 512], BF16)
    silu_sb = sb("silu", [128, 2 * 512], F32)
    xn_blk = sb("xnblk", [128, 2 * 8 * 512], BF16)
    xsq = sb("xsq", [128, 2 * 512], F32)
    xnt = sb("xnt", [128, 2 * 512], F32)
    rowbuf = sb("rowbuf", [1, 7 * 512], F32)
    glog = sb("glog", [128, 128], F32)
    glogc = sb("glogc", [128, 256], F32)
    ga = sb("ga", [128, 128], F32)
    gb = sb("gb", [128, 128], F32)
    gc_t = sb("gc", [128, 128], F32)
    gd = sb("gd", [128, 128], F32)
    gzn = sb("gzn", [128, 128], F32)
    gm1 = sb("gm1", [128, 16], F32)
    gm2 = sb("gm2", [128, 16], F32)
    gm3 = sb("gm3", [128, 16], F32)
    gne = sb("gne", [128, 16], F32)
    g_bf = sb("g_bf", [128, 128], BF16)
    w_sb = sb("w_sb", [8, 2048], BF16)
    bcw_sb = sb("bcw", [128, 512], BF16)
    wr_sb = sb("wr", [128, 64], F32)
    temp_sb = sb("temp", [1, 1], F32)
    invt_bc = sb("invt", [128, 1], F32)
    xn_scr = nc.dram_tensor("p3_xn_scr", [1024, 4096], BF16, kind="Internal").ap()

    esp = ExitStack()
    psb = lambda name, shape, dt=F32: esp.enter_context(
        nc.psum_tensor("p3_" + name, shape, dt))
    ps_s = psb("s", [1, 512])
    ps_q = psb("q", [1, 512])
    ps_A = psb("A", [128, 512])
    ps_M = psb("M", [128, 512])
    ps_p0 = psb("pp0", [128, 512])
    ps_p1 = psb("pp1", [128, 512])
    ps_pp = [ps_p0, ps_p1]
    ps_gt = psb("gt", [8, 128], BF16)
    ps_bc = psb("bc", [128, 512])
    tiles = dict(xsq=xsq, xnt=xnt, rowbuf=rowbuf)
    psums = dict(s=ps_s, q=ps_q, A=ps_A, M=ps_M, pp=ps_pp)

    p = Plan(nc)
    p.op("sync", lambda e: e.dma_start(
        wr_sb[:, :].rearrange("p (c e) -> p c e", e=8),
        w["L1_ffn_Wr"].rearrange("(c p) e -> p c e", p=128)), inc=sm.d_g, inc_k=16)
    p.op("sync", lambda e: e.dma_start(
        temp_sb[:, :], w["L1_ffn_temperature"].rearrange("(a b) -> a b", a=1)),
        inc=sm.d_g, inc_k=16)
    p.op("tensor", lambda t: t.matmul(ps_bc[:, 0:1], ctx.ones_1x128_f[:, :],
                                      temp_sb[:, :], start=True, stop=True),
         inc=sm.t_mm, wait=[(sm.d_g, sm.d_g.n)])
    p.op("scalar", lambda e: e.copy(invt_bc[:, :], ps_bc[:, 0:1]),
         inc=sm.s_ev, wait=[(sm.t_mm, sm.t_mm.n)])
    p.op("vector", lambda e: e.reciprocal(invt_bc[:, :], invt_bc[:, :]),
         inc=sm.sv, wait=[(sm.s_ev, sm.s_ev.n)])
    p.run()

    for u in range(2):
        xws = []
        pld = Plan(nc)
        for sbq in range(4):
            for c in range(NCH):
                pld.op("sync", lambda e, c=c, sbq=sbq, u=u: e.dma_start(
                    X_sup[:, c * 2048 + sbq * 512: c * 2048 + (sbq + 1) * 512],
                    x_src[c * 128:(c + 1) * 128,
                          (u * 4 + sbq) * 512:(u * 4 + sbq + 1) * 512]),
                    inc=sm.d_x, inc_k=16)
            xws.append([(sm.d_x, sm.d_x.n)])
        pld.run()
        for sbq in range(4):
            p = Plan(nc)
            plan_ln_block(nc, sm, ctx, p, x_src, X_sup, u * 4 + sbq, sbq, tiles,
                          psums, xn_scr, wr_sb, glogc, glog, xw=xws[sbq])
            p.run()

        p = Plan(nc)

        def vstep(build, wait=()):
            p.op("vector", build, inc=sm.sv, wait=list(wait) + [(sm.sv, sm.sv.n)])

        def sstep(build):
            p.op("scalar", build, inc=sm.ss, wait=[(sm.sv, sm.sv.n), (sm.ss, sm.ss.n)])
        g3 = lambda t: t[:, :].rearrange("p (s e) -> p s e", e=8)
        bc16 = lambda t: t[:, :].unsqueeze(2).broadcast_to([128, 16, 8])
        p.op("scalar", lambda e: e.activation(ga[:, :], glog[:, :], AF.Sigmoid,
                                              scale=invt_bc[:, :]),
             inc=sm.ss, wait=[(sm.v_gl, sm.v_gl.n), (sm.sv, sm.sv.n)])
        p.op("scalar", lambda e: e.activation(gb[:, :], ga[:, :], AF.Abs,
                                              bias=np.float32(-GC)), inc=sm.ss)
        vstep(lambda e: e.tensor_single_scalar(gzn[:, :], ga[:, :],
                                               float(GL), ALU.is_ge),
              wait=[(sm.ss, sm.ss.n)])
        vstep(lambda e: e.tensor_single_scalar(gc_t[:, :], ga[:, :],
                                               float(GU), ALU.is_le))
        vstep(lambda e: e.tensor_mul(gzn[:, :], gzn[:, :], gc_t[:, :]))
        sstep(lambda e: e.activation(gc_t[:, :], gb[:, :], AF.Exp,
                                     scale=np.float32(-10.0)))
        vstep(lambda e: e.tensor_mul(gc_t[:, :], gc_t[:, :], gzn[:, :]),
              wait=[(sm.ss, sm.ss.n)])
        vstep(lambda e: e.reduce_sum(gm1[:, :], g3(gc_t), axis=AX.X))
        vstep(lambda e: e.tensor_single_scalar(gne[:, :], gm1[:, :],
                                               1e-8, ALU.is_lt))
        sstep(lambda e: e.activation(gd[:, :], gb[:, :], AF.Exp,
                                     scale=np.float32(-10.0 / 3.0)))
        vstep(lambda e: e.reduce_max(gm1[:, :], g3(gd), axis=AX.X),
              wait=[(sm.ss, sm.ss.n)])
        vstep(lambda e: e.tensor_tensor(g3(gb), g3(gd), bc16(gm1), ALU.is_equal))
        vstep(lambda e: e.scalar_tensor_tensor(g3(gb), g3(gb), -1e9, g3(gd),
                                               ALU.mult, ALU.add))
        vstep(lambda e: e.reduce_max(gm2[:, :], g3(gb), axis=AX.X))
        vstep(lambda e: e.tensor_tensor(g3(gb), g3(gd), bc16(gm2), ALU.is_ge))
        vstep(lambda e: e.tensor_mul(gb[:, :], gb[:, :], gd[:, :]))
        vstep(lambda e: e.reduce_sum(gm3[:, :], g3(gb), axis=AX.X))
        vstep(lambda e: e.tensor_scalar_max(gm3[:, :], gm3[:, :], 1e-8))
        vstep(lambda e: e.reciprocal(gm3[:, :], gm3[:, :]))
        vstep(lambda e: e.tensor_mul(g3(gb), g3(gb), bc16(gm3)))
        vstep(lambda e: e.tensor_sub(gd[:, :], gb[:, :], gc_t[:, :]))
        vstep(lambda e: e.tensor_mul(g3(gd), g3(gd), bc16(gne)))
        vstep(lambda e: e.tensor_add(gc_t[:, :], gc_t[:, :], gd[:, :]))
        vstep(lambda e: e.reduce_sum(gm1[:, :], g3(gc_t), axis=AX.X))
        vstep(lambda e: e.tensor_scalar_max(gm1[:, :], gm1[:, :], 1e-8))
        vstep(lambda e: e.reciprocal(gm1[:, :], gm1[:, :]))
        vstep(lambda e: e.tensor_mul(g3(gc_t), g3(gc_t), bc16(gm1)))
        p.op("vector", lambda e: e.tensor_copy(g_bf[:, :], gc_t[:, :]),
             inc=sm.v_gbf, wait=[(sm.sv, sm.sv.n)])
        for st in range(16):
            p.op("tensor", lambda t, st=st: t.transpose(
                ps_gt[:, :], g_bf[:, st * 8:(st + 1) * 8], ctx.identb[:, :]),
                inc=sm.t_tr,
                wait=[(sm.v_gbf, sm.v_gbf.n)] + ([(sm.s_tr, sm.s_tr.n)] if st else []))
            p.op("scalar", lambda e, st=st: e.copy(
                w_sb[0:8, st * 128:(st + 1) * 128], ps_gt[:, :]),
                inc=sm.s_tr, wait=[(sm.t_tr, sm.t_tr.n)])
        p.run()

        p = Plan(nc)
        wmark3 = {}
        for e_i in range(8):
            if e_i == 0:
                for nm, dst in (("L1_ffn_Wg", wg_sb), ("L1_ffn_Wu", wu_sb),
                                ("L1_ffn_Wd", wd_sb)):
                    p.op("gpsimd", lambda g, nm=nm, dst=dst: g.dma_start(
                        dst[:, :].rearrange("p (c f) -> p c f", f=1024),
                        w[nm][0].rearrange("(c p) f -> p c f", p=128)),
                        inc=sm.d_wB, inc_k=16,
                        wait=[(sm.t_mm, sm.t_mm.n)] if nm == "L1_ffn_Wg" else ())
                wweights = [(sm.d_wB, sm.d_wB.n)]
            else:
                wweights = [(sm.d_wB, wmark3[e_i])]
            for c in range(NCH):
                p.op("sync", lambda e, c=c, u=u: e.dma_start(
                    xn_blk[:, c * 512:(c + 1) * 512],
                    xn_scr[c * 128:(c + 1) * 128, u * 2048: u * 2048 + 512]),
                    inc=sm.d_xn2, inc_k=16,
                    wait=[(sm.d_xnw, sm.d_xnw.n), (sm.t_mm, sm.t_mm.n)] if c == 0 else ())
            bank_records = []
            for s_blk in range(4):
                slot = s_blk % 2
                xnw2 = [(sm.d_xn2, sm.d_xn2.n)]
                p.op("tensor", lambda t, e_i=e_i, s_blk=s_blk: t.matmul(
                    ps_bc[:, :], ctx.sel_all[:, e_i * 128:(e_i + 1) * 128],
                    w_sb[0:8, s_blk * 512:(s_blk + 1) * 512], start=True, stop=True),
                    inc=sm.t_bc, wait=[(sm.s_bc, sm.s_bc.n)])
                p.op("scalar", lambda e: e.copy(bcw_sb[:, :], ps_bc[:, :]),
                     inc=sm.s_bc, wait=[(sm.t_bc, sm.t_bc.n), (sm.v_h, sm.v_h.n)])
                bcw_w = [(sm.s_bc, sm.s_bc.n)]
                for ic in range(NCH):
                    sl2 = ic % 2
                    G = len(bank_records)
                    gw = ([bank_records[G - 2]] if G >= 2 else []) +                          (wweights + xnw2 if ic == 0 else [])

                    def t_g(t, ic=ic, slot=slot, G=G):
                        for c in range(NCH):
                            ins = t.matmul(
                                ps_pp[G % 2][:, :],
                                wg_sb[:, c * 1024 + ic * 128: c * 1024 + (ic + 1) * 128],
                                xn_blk[:, slot * 4096 + c * 512: slot * 4096 + (c + 1) * 512],
                                start=(c == 0), stop=(c == 7))
                        return ins
                    p.op("tensor", t_g, inc=sm.t_mm, wait=gw)
                    si_w = [(sm.t_mm, sm.t_mm.n)]
                    if sm.v_h.n >= 1:
                        si_w.append((sm.v_h, sm.v_h.n - 1))
                    p.op("scalar", lambda e, sl2=sl2, G=G: e.activation(
                        silu_sb[:, sl2 * 512:(sl2 + 1) * 512], ps_pp[G % 2][:, :],
                        AF.Silu), inc=sm.s_si, wait=si_w)
                    bank_records.append((sm.s_si, sm.s_si.n))
                    G = len(bank_records)
                    gw = [bank_records[G - 2]] if G >= 2 else []

                    def t_u(t, ic=ic, slot=slot, G=G):
                        for c in range(NCH):
                            ins = t.matmul(
                                ps_pp[G % 2][:, :],
                                wu_sb[:, c * 1024 + ic * 128: c * 1024 + (ic + 1) * 128],
                                xn_blk[:, slot * 4096 + c * 512: slot * 4096 + (c + 1) * 512],
                                start=(c == 0), stop=(c == 7))
                        return ins
                    p.op("tensor", t_u, inc=sm.t_mm, wait=gw)
                    p.op("vector", lambda e, sl2=sl2: e.tensor_mul(
                        silu_sb[:, sl2 * 512:(sl2 + 1) * 512],
                        silu_sb[:, sl2 * 512:(sl2 + 1) * 512], bcw_sb[:, :]),
                        wait=[(sm.s_si, sm.s_si.n)] + bcw_w)
                    p.op("vector", lambda e, ic=ic, sl2=sl2, G=G: e.tensor_mul(
                        h_sb[:, ic * 512:(ic + 1) * 512], ps_pp[G % 2][:, :],
                        silu_sb[:, sl2 * 512:(sl2 + 1) * 512]),
                        inc=sm.v_h, wait=[(sm.t_mm, sm.t_mm.n)])
                    bank_records.append((sm.v_h, sm.v_h.n))
                if s_blk == 3:
                    t_after_gu = sm.t_mm.n
                for dt2 in range(8):
                    G = len(bank_records)
                    gw = [bank_records[G - 2]] if G >= 2 else []

                    def t_d(t, dt2=dt2, G=G):
                        for ic in range(NCH):
                            ins = t.matmul(
                                ps_pp[G % 2][:, :],
                                wd_sb[:, ic * 1024 + dt2 * 128: ic * 1024 + (dt2 + 1) * 128],
                                h_sb[:, ic * 512:(ic + 1) * 512],
                                start=(ic == 0), stop=(ic == 7))
                        return ins
                    p.op("tensor", t_d, inc=sm.t_mm,
                         wait=gw + ([(sm.v_h, sm.v_h.n)] if dt2 == 0 else []))
                    p.op("vector", lambda e, dt2=dt2, s_blk=s_blk, G=G: e.tensor_add(
                        X_sup[:, dt2 * 2048 + s_blk * 512: dt2 * 2048 + (s_blk + 1) * 512],
                        ps_pp[G % 2][:, :],
                        X_sup[:, dt2 * 2048 + s_blk * 512: dt2 * 2048 + (s_blk + 1) * 512]),
                        inc=sm.v_o, wait=[(sm.t_mm, sm.t_mm.n)])
                    bank_records.append((sm.v_o, sm.v_o.n))
                if s_blk < 3:
                    t_after_h = sm.t_mm.n
                    for c in range(NCH):
                        p.op("sync", lambda e, c=c, u=u, s_blk=s_blk, slot=slot: e.dma_start(
                            xn_blk[:, (1 - slot) * 4096 + c * 512: (1 - slot) * 4096 + (c + 1) * 512],
                            xn_scr[c * 128:(c + 1) * 128,
                                   u * 2048 + (s_blk + 1) * 512: u * 2048 + (s_blk + 2) * 512]),
                            inc=sm.d_xn2, inc_k=16,
                            wait=[(sm.t_mm, t_after_h)] if c == 0 else ())
                elif e_i < 7:
                    ne = e_i + 1
                    t_now = sm.t_mm.n
                    for nm, dst, gv in (("L1_ffn_Wg", wg_sb, t_after_gu),
                                        ("L1_ffn_Wu", wu_sb, t_after_gu),
                                        ("L1_ffn_Wd", wd_sb, t_now)):
                        p.op("gpsimd", lambda g, nm=nm, dst=dst, ne=ne: g.dma_start(
                            dst[:, :].rearrange("p (c f) -> p c f", f=1024),
                            w[nm][ne].rearrange("(c p) f -> p c f", p=128)),
                            inc=sm.d_wB, inc_k=16, wait=[(sm.t_mm, gv)])
                    wmark3[ne] = sm.d_wB.n
        for c in range(NCH):
            p.op("sync", lambda e, c=c, u=u: e.dma_start(
                x_dst[c * 128:(c + 1) * 128, u * 2048:(u + 1) * 2048],
                X_sup[:, c * 2048:(c + 1) * 2048]),
                inc=sm.d_out, inc_k=16,
                wait=[(sm.v_o, sm.v_o.n)] if c == 0 else ())
        p.op("sync", lambda e: e.nop(), wait=[(sm.d_out, sm.d_out.n)])
        p.run()

    es.close()
    esp.close()


SEM_NAMES = [
    "sv", "ss", "t_mm", "s_ev", "v_ev", "s_sq", "v_cast", "v_xn0", "s_xn",
    "d_w", "d_v", "d_wv", "v_wvr", "d_x", "d_qk", "d_qh", "d_xh",
    "t_sc", "s_exp", "t_pv", "s_sr", "s_rb", "v_on", "v_res", "d_out",
    "d_xnw", "d_xn2", "d_wB", "t_bc", "s_bc", "v_h", "v_o", "t_gm", "v_gl",
    "t_tr", "s_tr", "v_gbf", "d_g", "t_ss", "t_fac", "v_mk", "s_si", "v_ps",
]


def build_program(phases):
    nc = bass.Bass("TRN2", target_bir_lowering=False)
    x_in = nc.dram_tensor("x_in", [1024, 4096], F32, kind="ExternalInput").ap()
    x_out = nc.dram_tensor("x_out", [1024, 4096], F32, kind="ExternalOutput").ap()
    w = {}
    shapes = {
        "L0_attn_Wq": [1024, 1024], "L0_attn_bq": [1024],
        "L0_attn_Wk": [1024, 1024], "L0_attn_bk": [1024],
        "L0_attn_Wv": [1024, 8192], "L0_attn_bv": [8192],
        "L0_attn_gamma": [1024], "L0_attn_beta": [1024],
        "L0_ffn_gate": [1024, 8], "L0_ffn_W1": [8, 1024, 2048],
        "L0_ffn_b1": [8, 2048], "L0_ffn_W2": [8, 2048, 1024],
        "L0_ffn_b2": [8, 1024],
        "L1_attn_Wq": [1024, 1024], "L1_attn_Wk": [1024, 1024],
        "L1_attn_Wv": [1024, 1024], "L1_attn_Wo": [1024, 1024],
        "L1_ffn_Wg": [8, 1024, 1024], "L1_ffn_Wu": [8, 1024, 1024],
        "L1_ffn_Wd": [8, 1024, 1024], "L1_ffn_Wr": [1024, 8],
        "L1_ffn_temperature": [1],
    }
    for n in WEIGHT_NAMES:
        w[n] = nc.dram_tensor(n, shapes[n], F32, kind="ExternalInput").ap()

    es = ExitStack()
    ctx = type("Ctx", (), {})()
    emit_setup(nc, es, ctx)
    sm = make_sems(nc, es, SEM_NAMES)

    n_ph = len(phases)
    inter = {}
    def x_ap(i):
        if i == 0:
            return x_in
        if i == n_ph:
            return x_out
        if i not in inter:
            inter[i] = nc.dram_tensor(f"x_mid{i}", [1024, 4096], F32,
                                      kind="Internal").ap()
        return inter[i]

    emitters = {0: emit_phase0, 1: emit_phase1, 2: emit_phase2, 3: emit_phase3}
    for idx, ph in enumerate(phases):
        emitters[ph](nc, ctx, sm, x_ap(idx), x_ap(idx + 1), w)
    es.close()
    return nc


def kernel(**inputs):
    x = inputs["L0_attn_x"]
    xs = [np.ascontiguousarray(x[b].T).astype(np.float32) for b in range(8)]
    weights = {n: np.ascontiguousarray(inputs[n], dtype=np.float32)
               for n in WEIGHT_NAMES}
    phase_groups = [(0, 1, 2, 3)]
    cur = xs
    for pg in phase_groups:
        nc = build_program(pg)
        in_maps = [dict(weights, x_in=cur[b]) for b in range(8)]
        res = run_bass_kernel_spmd(nc, in_maps, core_ids=list(range(8)))
        cur = [res.results[b]["x_out"] for b in range(8)]
    out = np.stack([c.T for c in cur]).astype(np.float32)
    return out


# version 14
# speedup vs baseline: 1.0108x; 1.0108x over previous
import math
import numpy as np
from contextlib import ExitStack

import concourse.bass as bass
import concourse.mybir as mybir
from concourse.bass_utils import run_bass_kernel_spmd

F32 = mybir.dt.float32
BF16 = mybir.dt.bfloat16
AF = mybir.ActivationFunctionType
ALU = mybir.AluOpType
AX = mybir.AxisListType

D = 1024
S = 4096
NCH = 8
LN_EPS = 1e-5
RSQRT_DK0 = 1.0 / math.sqrt(128.0)

ENGINES = ("tensor", "vector", "scalar", "sync", "gpsimd")

WEIGHT_NAMES = [
    "L0_attn_Wq", "L0_attn_bq", "L0_attn_Wk", "L0_attn_bk", "L0_attn_Wv",
    "L0_attn_bv", "L0_attn_gamma", "L0_attn_beta",
    "L0_ffn_gate", "L0_ffn_W1", "L0_ffn_b1", "L0_ffn_W2", "L0_ffn_b2",
    "L1_attn_Wq", "L1_attn_Wk", "L1_attn_Wv", "L1_attn_Wo",
    "L1_ffn_Wg", "L1_ffn_Wu", "L1_ffn_Wd", "L1_ffn_Wr", "L1_ffn_temperature",
]


class Sem:
    def __init__(self, h):
        self.h = h
        self.n = 0


class Plan:

    def __init__(self, nc):
        self.nc = nc
        self.q = {e: [] for e in ENGINES}

    def op(self, eng, build, inc=None, inc_k=1, wait=()):
        waits = [(s.h, v) for (s, v) in wait]
        if inc is not None:
            inc.n += inc_k
            inc_t = (inc.h, inc_k)
        else:
            inc_t = None

        def run(e, waits=waits, build=build, inc_t=inc_t):
            for h, v in waits:
                e.wait_ge(h, v)
            ins = build(e)
            if inc_t is not None:
                ins.then_inc(*inc_t)
        self.q[eng].append(run)

    def run(self):
        with self.nc.Block() as blk:
            for e in ENGINES:
                lst = self.q[e]
                if lst:
                    getattr(blk, e)(lambda eng, lst=lst: [f(eng) for f in lst])


def make_sems(nc, es, names):
    class NS:
        pass
    ns = NS()
    for n in names:
        setattr(ns, n, Sem(es.enter_context(nc.semaphore(n))))
    return ns


def emit_setup(nc, es, ctx):
    ctx.ones_col_bf = es.enter_context(nc.sbuf_tensor("c_ones_col", [128, 1], BF16))
    ctx.ones_row_bf = es.enter_context(nc.sbuf_tensor("c_ones_row", [1, 512], BF16))
    ctx.ones_1x128_bf = es.enter_context(nc.sbuf_tensor("c_ones1x128b", [1, 128], BF16))
    ctx.ones_1x128_f = es.enter_context(nc.sbuf_tensor("c_ones1x128f", [1, 128], F32))
    ctx.identb = es.enter_context(nc.sbuf_tensor("c_identb", [128, 128], BF16))
    ctx.identf = es.enter_context(nc.sbuf_tensor("c_identf", [128, 128], F32))
    ctx.identf8 = es.enter_context(nc.sbuf_tensor("c_identf8", [8, 8], F32))
    ctx.sel_all = es.enter_context(nc.sbuf_tensor("c_sel", [8, 1024], BF16))
    ctx.onesf_col = es.enter_context(nc.sbuf_tensor("c_onesf_col", [128, 1], F32))

    p = Plan(nc)

    def g_fn(g):
        g.memset(ctx.ones_col_bf[:, :], 1.0)
        g.memset(ctx.ones_row_bf[:, :], 1.0)
        g.memset(ctx.ones_1x128_bf[:, :], 1.0)
        g.memset(ctx.ones_1x128_f[:, :], 1.0)
        g.memset(ctx.onesf_col[:, :], 1.0)
        for t, n in ((ctx.identb, 128), (ctx.identf, 128), (ctx.identf8, 8)):
            g.memset(t[:, :], 1.0)
            g.affine_select(t[:, :], t[:, :], [[-1, n]], ALU.is_equal, 0.0,
                            base=0, channel_multiplier=1)
        g.memset(ctx.sel_all[:, :], 1.0)
        g.affine_select(ctx.sel_all[:, :], ctx.sel_all[:, :], [[-1, 8], [0, 128]],
                        ALU.is_equal, 0.0, base=0, channel_multiplier=1)
        return g.nop()
    p.op("gpsimd", g_fn)
    p.run()


def plan_ln_rows(p, sm, ps_s, ps_q, rows, with_ln2, pre_wait):
    mu, msq, var, t0, t1, A, negMA = (rows[k] for k in
                                      ("mu", "msq", "var", "t0", "t1", "A", "negMA"))

    def v(build, wait=()):
        p.op("vector", build, inc=sm.sv, wait=list(wait) + [(sm.sv, sm.sv.n)])

    def s(build):
        p.op("scalar", build, inc=sm.ss, wait=[(sm.sv, sm.sv.n)])

    v(lambda e: e.tensor_scalar_mul(mu[:, :], ps_s[:, :], 1.0 / 1024.0), wait=pre_wait)
    v(lambda e: e.tensor_scalar_mul(msq[:, :], ps_q[:, :], 1.0 / 1024.0))
    v(lambda e: e.tensor_mul(t0[:, :], mu[:, :], mu[:, :]))
    v(lambda e: e.tensor_sub(var[:, :], msq[:, :], t0[:, :]))
    v(lambda e: e.tensor_scalar_add(t0[:, :], var[:, :], LN_EPS))
    s(lambda e: e.activation(t1[:, :], t0[:, :], AF.Ln))
    s(lambda e: e.activation(A[:, :], t1[:, :], AF.Exp, scale=np.float32(-0.5)))
    v(lambda e: e.nop(), wait=[(sm.ss, sm.ss.n)])
    if with_ln2:
        v(lambda e: e.tensor_mul(t0[:, :], A[:, :], A[:, :]))
        v(lambda e: e.tensor_mul(t0[:, :], var[:, :], t0[:, :]))
        v(lambda e: e.tensor_scalar_add(t0[:, :], t0[:, :], LN_EPS))
        s(lambda e: e.activation(t1[:, :], t0[:, :], AF.Ln))
        s(lambda e: e.activation(t0[:, :], t1[:, :], AF.Exp, scale=np.float32(-0.5)))
        v(lambda e: e.tensor_mul(A[:, :], A[:, :], t0[:, :]), wait=[(sm.ss, sm.ss.n)])
    v(lambda e: e.tensor_mul(negMA[:, :], mu[:, :], A[:, :]))
    v(lambda e: e.tensor_scalar_mul(negMA[:, :], negMA[:, :], -1.0))


def emit_phase0(nc, ctx, sm, x_src, x_dst, w):
    es = ExitStack()
    sb = lambda name, shape, dt: es.enter_context(nc.sbuf_tensor("p0_" + name, shape, dt))

    v_tok = sb("v_tok", [128, 32 * 1024], BF16)
    bias_fm = sb("bias_fm", [128, 32], F32)
    bvr_row = sb("bvr", [1, 1024], BF16)
    bvr_f = sb("bvrf", [1, 1024], F32)
    rows = {k: sb("row_" + k, [1, 512], F32)
            for k in ("mu", "msq", "var", "t0", "t1", "A", "negMA")}
    q_scr = nc.dram_tensor("p0_q_scr", [8, 128, 4096], BF16, kind="Internal").ap()
    k_scr = nc.dram_tensor("p0_k_scr", [8, 128, 4096], BF16, kind="Internal").ap()

    es1 = ExitStack()
    sb1 = lambda name, shape, dt: es1.enter_context(nc.sbuf_tensor("p0s_" + name, shape, dt))
    wq_sb = sb1("wq", [128, 8 * 1024], BF16)
    wk_sb = sb1("wk", [128, 8 * 1024], BF16)
    wvr_sb = sb1("wvr", [128, 8 * 1024], BF16)
    es0 = ExitStack()
    sb0 = lambda name, shape, dt: es0.enter_context(nc.sbuf_tensor("p0z_" + name, shape, dt))
    wv_buf = sb0("wvbuf", [128, 2 * 4096], F32)
    wvr_f = sb0("wvrf", [128, 512], F32)
    bv_row = sb0("bv_row", [1, 8192], BF16)
    vec8 = sb0("vec8", [8, 4 * 128], F32)

    esp = ExitStack()
    psb = lambda name, shape: esp.enter_context(nc.psum_tensor("p0_" + name, shape, F32))
    ps_s = psb("s", [1, 512])
    ps_q = psb("q", [1, 512])
    ps_A = psb("A", [128, 512])
    ps_M = psb("M", [128, 512])
    ps_p0 = psb("pp0", [128, 512])
    ps_p1 = psb("pp1", [128, 512])
    ps_pp = [ps_p0, ps_p1]

    p = Plan(nc)
    p.op("gpsimd", lambda g: g.dma_start(
        wq_sb[:, :].rearrange("p (c f) -> p c f", f=1024),
        w["L0_attn_Wq"].rearrange("(c p) f -> p c f", p=128)), inc=sm.d_w, inc_k=16)
    p.op("gpsimd", lambda g: g.dma_start(
        wk_sb[:, :].rearrange("p (c f) -> p c f", f=1024),
        w["L0_attn_Wk"].rearrange("(c p) f -> p c f", p=128)), inc=sm.d_w, inc_k=16)
    for i, nm in enumerate(["L0_attn_bq", "L0_attn_bk", "L0_attn_gamma", "L0_attn_beta"]):
        p.op("gpsimd", lambda g, i=i, nm=nm: g.dma_start(
            vec8[:, i * 128:(i + 1) * 128],
            w[nm].rearrange("(a b) -> a b", b=128)), inc=sm.d_v, inc_k=16)
    p.op("gpsimd", lambda g: g.dma_start(
        bv_row[:, :], w["L0_attn_bv"].rearrange("(a b) -> a b", a=1)),
        inc=sm.d_v, inc_k=16)
    for i in range(16):
        c, hf = i // 2, i % 2
        wv_waits = [(sm.v_wvr, i - 1)] if i >= 2 else []
        p.op("sync", lambda e, c=c, hf=hf, i=i: e.dma_start(
            wv_buf[:, (i % 2) * 4096:(i % 2 + 1) * 4096],
            w["L0_attn_Wv"][c * 128:(c + 1) * 128, hf * 4096:(hf + 1) * 4096]),
            inc=sm.d_wv, inc_k=16, wait=wv_waits)
        p.op("vector", lambda e, c=c, hf=hf, i=i: e.reduce_sum(
            wvr_f[:, :].rearrange("p (r e) -> p r e", e=1),
            wv_buf[:, (i % 2) * 4096:(i % 2 + 1) * 4096].rearrange("p (r e) -> p r e", e=8),
            axis=AX.X), wait=[(sm.d_wv, 16 * (i + 1))] + [(sm.v_wvr, sm.v_wvr.n)])
        p.op("vector", lambda e, c=c, hf=hf: e.tensor_copy(
            wvr_sb[:, c * 1024 + hf * 512: c * 1024 + (hf + 1) * 512], wvr_f[:, :]),
            inc=sm.v_wvr)
    p.op("vector", lambda e: e.reduce_sum(
        bvr_f[:, :].rearrange("p (r e) -> p r e", e=1),
        bv_row[:, :].rearrange("p (r e) -> p r e", e=8), axis=AX.X),
        wait=[(sm.d_v, sm.d_v.n), (sm.v_wvr, sm.v_wvr.n)])
    p.op("vector", lambda e: e.tensor_copy(bvr_row[:, :], bvr_f[:, :]),
        inc=sm.v_wvr)
    def bias_mms(t):
        for i in range(4):
            ins = t.matmul(ps_A[:, i * 8:(i + 1) * 8], vec8[:, i * 128:(i + 1) * 128],
                           ctx.identf8[:, :], start=(i == 0), stop=(i == 3))
        return ins
    p.op("tensor", bias_mms, inc=sm.t_mm, wait=[(sm.d_v, sm.d_v.n)])
    p.op("scalar", lambda s: s.copy(bias_fm[:, :], ps_A[:, 0:32]),
         inc=sm.s_ev, wait=[(sm.t_mm, sm.t_mm.n)])
    p.run()
    es0.close()

    xblk = sb1("xblk", [128, 8 * 512], F32)
    xbf = sb1("xbf", [128, 8 * 512], BF16)
    xsq = sb1("xsq", [128, 8 * 512], BF16)
    xn0 = sb1("xn0", [128, 2 * 512], F32)
    xn_sb = sb1("xn", [128, 8 * 512], BF16)
    q_stage = sb1("qstage", [128, 8 * 512], BF16)
    k_stage = sb1("kstage", [128, 8 * 512], BF16)

    p = Plan(nc)
    t_proj_end = 0
    t_statq_end = 0
    for sbi in range(8):
        xdma_gate = [(sm.v_xn0, sm.v_xn0.n), (sm.s_sq, sm.s_sq.n),
                     (sm.v_cast, sm.v_cast.n)] if sbi else []
        for c in range(NCH):
            p.op("sync", lambda e, c=c, sbi=sbi: e.dma_start(
                xblk[:, c * 512:(c + 1) * 512],
                x_src[c * 128:(c + 1) * 128, sbi * 512:(sbi + 1) * 512]),
                inc=sm.d_x, inc_k=16, wait=xdma_gate if c == 0 else ())
        xw = [(sm.d_x, sm.d_x.n)]
        for c in range(NCH):
            p.op("vector", lambda e, c=c: e.tensor_copy(
                xbf[:, c * 512:(c + 1) * 512], xblk[:, c * 512:(c + 1) * 512]),
                inc=sm.v_cast, wait=xw if c == 0 else ())
            sqg = [(sm.t_mm, t_statq_end)] if (sbi and c < 2) else []
            p.op("scalar", lambda e, c=c: e.activation(
                xsq[:, c * 512:(c + 1) * 512], xblk[:, c * 512:(c + 1) * 512],
                AF.Square), inc=sm.s_sq, wait=(xw if c == 0 else []) + sqg)

        def stats_s(t):
            for c in range(NCH):
                ins = t.matmul(ps_s[:, :], ctx.ones_col_bf[:, :],
                               xbf[:, c * 512:(c + 1) * 512],
                               start=(c == 0), stop=(c == 7))
            return ins
        p.op("tensor", stats_s, inc=sm.t_mm,
             wait=[(sm.v_cast, sm.v_cast.n), (sm.sv, sm.sv.n)])

        def stats_q(t):
            for c in range(NCH):
                ins = t.matmul(ps_q[:, :], ctx.ones_col_bf[:, :],
                               xsq[:, c * 512:(c + 1) * 512],
                               start=(c == 0), stop=(c == 7))
            return ins
        p.op("tensor", stats_q, inc=sm.t_mm, wait=[(sm.s_sq, sm.s_sq.n)])
        t_statq_end = sm.t_mm.n

        plan_ln_rows(p, sm, ps_s, ps_q, rows, with_ln2=True,
                     pre_wait=[(sm.t_mm, sm.t_mm.n)])
        p.op("tensor", lambda t: t.matmul(ps_A[:, :], ctx.ones_1x128_f[:, :],
                                          rows["A"][:, :], start=True, stop=True),
             inc=sm.t_mm, wait=[(sm.sv, sm.sv.n), (sm.v_xn0, sm.v_xn0.n)])
        p.op("tensor", lambda t: t.matmul(ps_M[:, :], ctx.ones_1x128_f[:, :],
                                          rows["negMA"][:, :], start=True, stop=True),
             inc=sm.t_mm)
        bcw = [(sm.t_mm, sm.t_mm.n)]
        for c in range(NCH):
            b2 = c % 2
            slot_free = [(sm.s_xn, sm.s_xn.n - 1)] if sm.s_xn.n >= 1 else []
            p.op("vector", lambda e, c=c, b2=b2: e.tensor_mul(
                xn0[:, b2 * 512:(b2 + 1) * 512], xblk[:, c * 512:(c + 1) * 512],
                ps_A[:, :]), wait=(bcw if c == 0 else []) + slot_free)
            p.op("vector", lambda e, c=c, b2=b2: e.tensor_add(
                xn0[:, b2 * 512:(b2 + 1) * 512], xn0[:, b2 * 512:(b2 + 1) * 512],
                ps_M[:, :]), inc=sm.v_xn0)
            xng = [(sm.t_mm, t_proj_end)] if (sbi and c == 0) else []
            p.op("scalar", lambda e, c=c, b2=b2: e.activation(
                xn_sb[:, c * 512:(c + 1) * 512], xn0[:, b2 * 512:(b2 + 1) * 512],
                AF.Identity, bias=bias_fm[:, 24 + c:25 + c],
                scale=bias_fm[:, 16 + c:17 + c]),
                inc=sm.s_xn, wait=[(sm.v_xn0, sm.v_xn0.n)] + xng)

        bank_records = []
        xnw = [(sm.s_xn, sm.s_xn.n)]
        dq_base = sm.d_qk.n

        def group(t_build, evac_eng, evac_build, evac_sem, first=False):
            G = len(bank_records)
            waits = ([bank_records[G - 2]] if G >= 2 else []) + (xnw if first else [])
            p.op("tensor", t_build, inc=sm.t_mm, wait=waits)
            p.op(evac_eng, evac_build, inc=evac_sem, wait=[(sm.t_mm, sm.t_mm.n)])
            bank_records.append((evac_sem, evac_sem.n))

        for wsb, stage_t, bcol in ((wq_sb, q_stage, 0), (wk_sb, k_stage, 8)):
            for ft in range(8):
                G = len(bank_records)

                def t_build(t, wsb=wsb, ft=ft, G=G):
                    for c in range(NCH):
                        ins = t.matmul(
                            ps_pp[G % 2][:, :],
                            wsb[:, c * 1024 + ft * 128: c * 1024 + (ft + 1) * 128],
                            xn_sb[:, c * 512:(c + 1) * 512],
                            start=(c == 0), stop=(c == 7))
                    return ins

                def e_build(e, stage_t=stage_t, ft=ft, bcol=bcol, G=G):
                    return e.activation(
                        stage_t[:, ft * 512:(ft + 1) * 512], ps_pp[G % 2][:, :],
                        AF.Identity, bias=bias_fm[:, bcol + ft:bcol + ft + 1])
                if G == 0 and sbi:
                    xnw.append((sm.d_qk, dq_base))
                group(t_build, "scalar", e_build, sm.s_ev, first=(G == 0))
        for tt in range(4):
            for half in range(2):
                G = len(bank_records)
                ttg = sbi * 4 + tt

                def t_build(t, tt=tt, half=half, G=G):
                    for c in range(NCH):
                        t.matmul(ps_pp[G % 2][:, :],
                                 xn_sb[:, c * 512 + tt * 128: c * 512 + (tt + 1) * 128],
                                 wvr_sb[:, c * 1024 + half * 512: c * 1024 + (half + 1) * 512],
                                 start=(c == 0), stop=False)
                    return t.matmul(ps_pp[G % 2][:, :], ctx.ones_1x128_bf[:, :],
                                    bvr_row[:, half * 512:(half + 1) * 512],
                                    start=False, stop=True)

                def e_build(e, ttg=ttg, half=half, G=G):
                    return e.tensor_copy(
                        v_tok[:, ttg * 1024 + half * 512: ttg * 1024 + (half + 1) * 512],
                        ps_pp[G % 2][:, :])
                group(t_build, "vector", e_build, sm.v_ev)

        for ft in range(8):
            p.op("sync", lambda e, ft=ft, sbi=sbi: e.dma_start(
                q_scr[ft, :, sbi * 512:(sbi + 1) * 512],
                q_stage[:, ft * 512:(ft + 1) * 512]),
                inc=sm.d_qk, inc_k=16, wait=[bank_records[ft]])
            p.op("sync", lambda e, ft=ft, sbi=sbi: e.dma_start(
                k_scr[ft, :, sbi * 512:(sbi + 1) * 512],
                k_stage[:, ft * 512:(ft + 1) * 512]),
                inc=sm.d_qk, inc_k=16, wait=[bank_records[8 + ft]])
        t_proj_end = sm.t_mm.n
    p.run()

    es1.close()
    esp.close()

    esp2 = ExitStack()
    psb2 = lambda name, shape: esp2.enter_context(nc.psum_tensor("p0a_" + name, shape, F32))
    ps_sc = [psb2("scA", [128, 1024]), psb2("scB", [128, 1024])]
    ps_oo = [psb2("o0", [128, 512]), psb2("o1", [128, 512])]
    ps_sum = psb2("sum", [1, 512])
    ps_misc = psb2("misc", [128, 512])

    qh_sb = sb("qh", [128, 2 * 4096], BF16)
    kh_sb = sb("kh", [128, 2 * 4096], BF16)
    xh_sb = sb("xh", [128, 8 * 512], F32)
    xh2_sb = sb("xh2", [128, 8 * 512], F32)
    oh_sb = sb("oh", [128, 2 * 4096], F32)
    exp_sb = sb("exp", [128, 4 * 1024], BF16)
    psh_sb = sb("psh", [128, 2 * 512], F32)
    rec_bc = sb("rec_bc", [128, 512], F32)
    cm1 = sb("cm1", [128, 1], F32)
    negC = sb("negC", [1, 1], F32)
    negC_bc = sb("negC_bc", [128, 2], F32)
    sum_row = sb("sum_row", [1, 512], F32)
    rec_row = sb("rec_row", [1, 512], F32)

    n_qk_stores = sm.d_qk.n
    sc_count = [0]

    p = Plan(nc)
    for hp in range(4):
        qk_gate = [(sm.t_sc, sm.t_sc.n), (sm.t_pv, sm.t_pv.n)]
        for hh in range(2):
            h = 2 * hp + hh
            p.op("sync", lambda e, h=h, hh=hh: e.dma_start(
                qh_sb[:, hh * 4096:(hh + 1) * 4096], q_scr[h, :, :]),
                inc=sm.d_qh, inc_k=16,
                wait=([(sm.d_qk, n_qk_stores)] if h == 0 else []) +
                     (qk_gate if (hh == 0 and hp > 0) else []))
            p.op("sync", lambda e, h=h, hh=hh: e.dma_start(
                kh_sb[:, hh * 4096:(hh + 1) * 4096], k_scr[h, :, :]),
                inc=sm.d_qh, inc_k=16)
        for c in range(NCH):
            p.op("gpsimd", lambda e, c=c, hp=hp: e.dma_start(
                xh_sb[:, c * 512:(c + 1) * 512],
                x_src[c * 128:(c + 1) * 128, 2 * hp * 512:(2 * hp + 1) * 512]),
                inc=sm.d_xh, inc_k=16,
                wait=[(sm.v_res, sm.v_res.n)] if c == 0 else ())
        qkw = [(sm.d_qh, sm.d_qh.n)]
        for hh in range(2):
            corw = (qkw if hh == 0 else []) + [(sm.s_rb, sm.s_rb.n),
                                               (sm.s_ev, sm.s_ev.n)]
            p.op("tensor", lambda t, hh=hh: t.matmul(
                ps_misc[:, :], kh_sb[:, hh * 4096: hh * 4096 + 128],
                qh_sb[:, hh * 4096: hh * 4096 + 512], start=True, stop=True),
                inc=sm.t_mm, wait=corw)
            p.op("vector", lambda v: v.reduce_max(cm1[:, :], ps_misc[:, :], axis=AX.X),
                 inc=sm.sv, wait=[(sm.t_mm, sm.t_mm.n), (sm.sv, sm.sv.n)])
            p.op("tensor", lambda t: t.matmul(ps_misc[0:1, 0:128], cm1[:, :],
                                              ctx.identf[:, :], start=True, stop=True),
                 inc=sm.t_mm, wait=[(sm.sv, sm.sv.n)])
            p.op("vector", lambda v: v.reduce_max(negC[:, :], ps_misc[0:1, 0:128],
                                                  axis=AX.X, negate=True),
                 inc=sm.sv, wait=[(sm.t_mm, sm.t_mm.n), (sm.sv, sm.sv.n)])
            p.op("vector", lambda v: v.tensor_scalar_mul(negC[:, :], negC[:, :],
                                                         RSQRT_DK0),
                 inc=sm.sv, wait=[(sm.sv, sm.sv.n)])
            p.op("tensor", lambda t, hh=hh: t.matmul(
                ps_misc[:, 2 * hh:2 * hh + 1], ctx.ones_1x128_f[:, :], negC[:, :],
                start=True, stop=True),
                inc=sm.t_mm, wait=[(sm.sv, sm.sv.n)])
            p.op("scalar", lambda s, hh=hh: s.copy(negC_bc[:, hh:hh + 1],
                                                   ps_misc[:, 2 * hh:2 * hh + 1]),
                 inc=sm.s_ev, wait=[(sm.t_mm, sm.t_mm.n)])
        cready = [(sm.s_ev, sm.s_ev.n)]

        pair_resw = [(sm.v_res, sm.v_res.n)]
        psh_gate = [(sm.t_mm, sm.t_mm.n)]
        for sbi in range(8):
            sum_gate = [(sm.s_sr, sm.s_sr.n)]
            on_gates = {0: [(sm.v_on, sm.v_on.n - 1)] if sm.v_on.n >= 1 else [],
                        1: [(sm.v_on, sm.v_on.n)]}
            pending = None
            for bt in range(16):
                for hh in range(2):
                    h = 2 * hp + hh
                    j = sc_count[0]

                    def t_sc(t, bt=bt, sbi=sbi, hh=hh, j=j):
                        t.matmul(ps_sc[j % 2][:, 0:512],
                                 kh_sb[:, hh * 4096 + (2 * bt) * 128: hh * 4096 + (2 * bt + 1) * 128],
                                 qh_sb[:, hh * 4096 + sbi * 512: hh * 4096 + (sbi + 1) * 512],
                                 start=True, stop=True)
                        return t.matmul(ps_sc[j % 2][:, 512:1024],
                                        kh_sb[:, hh * 4096 + (2 * bt + 1) * 128: hh * 4096 + (2 * bt + 2) * 128],
                                        qh_sb[:, hh * 4096 + sbi * 512: hh * 4096 + (sbi + 1) * 512],
                                        start=True, stop=True)
                    scw = [(sm.s_exp, j - 1)] if j >= 2 else []
                    p.op("tensor", t_sc, inc=sm.t_sc, wait=scw)
                    sc_count[0] += 1
                    if pending is not None:
                        p.op("tensor", pending[0], inc=sm.t_pv, wait=pending[1])
                        pending = None
                    expw = [(sm.v_ps, sm.v_ps.n - 6)] if sm.v_ps.n >= 6 else []
                    p.op("scalar", lambda s, j=j, hh=hh: s.activation(
                        exp_sb[:, (j % 4) * 1024:(j % 4 + 1) * 1024], ps_sc[j % 2][:, :],
                        AF.Exp, bias=negC_bc[:, hh:hh + 1], scale=RSQRT_DK0),
                        inc=sm.s_exp,
                        wait=[(sm.t_sc, sm.t_sc.n)] + expw + (cready if (sbi == 0 and bt == 0) else []))
                    for q2 in range(2):
                        p.op("vector", lambda v, j=j, hh=hh, q2=q2, bt=bt: (
                            v.tensor_copy(psh_sb[:, hh * 512:(hh + 1) * 512],
                                          exp_sb[:, (j % 4) * 1024 + q2 * 512: (j % 4) * 1024 + (q2 + 1) * 512])
                            if (bt == 0 and q2 == 0) else
                            v.tensor_add(psh_sb[:, hh * 512:(hh + 1) * 512],
                                         psh_sb[:, hh * 512:(hh + 1) * 512],
                                         exp_sb[:, (j % 4) * 1024 + q2 * 512: (j % 4) * 1024 + (q2 + 1) * 512])),
                            inc=sm.v_ps,
                            wait=[(sm.s_exp, sm.s_exp.n)] +
                                 (psh_gate if (bt == 0 and q2 == 0) else []))

                    def t_pv(t, bt=bt, hh=hh, h=h, j=j):
                        for q2 in range(2):
                            tc = 2 * bt + q2
                            ins = t.matmul(
                                ps_oo[hh][:, :],
                                v_tok[:, tc * 1024 + h * 128: tc * 1024 + (h + 1) * 128],
                                exp_sb[:, (j % 4) * 1024 + q2 * 512: (j % 4) * 1024 + (q2 + 1) * 512],
                                start=(tc == 0), stop=(tc == 31))
                        return ins
                    pvw = [(sm.s_exp, sm.s_exp.n)] + (on_gates[hh] if bt == 0 else [])
                    pending = (t_pv, pvw)
            if pending is not None:
                p.op("tensor", pending[0], inc=sm.t_pv, wait=pending[1])
                pending = None
            for hh in range(2):
                h = 2 * hp + hh
                p.op("tensor", lambda t, hh=hh: t.matmul(
                    ps_sum[:, :], ctx.onesf_col[:, :], psh_sb[:, hh * 512:(hh + 1) * 512],
                    start=True, stop=True),
                    inc=sm.t_mm, wait=[(sm.v_ps, sm.v_ps.n)] + sum_gate)
                p.op("scalar", lambda s: s.activation(sum_row[:, :], ps_sum[:, :],
                                                      AF.Ln),
                     inc=sm.s_sr, wait=[(sm.t_mm, sm.t_mm.n)])
                p.op("scalar", lambda s: s.activation(rec_row[:, :], sum_row[:, :],
                                                      AF.Exp, scale=np.float32(-1.0)),
                     inc=sm.s_sr)
                p.op("tensor", lambda t: t.matmul(ps_misc[:, :], ctx.ones_1x128_f[:, :],
                                                  rec_row[:, :], start=True, stop=True),
                     inc=sm.t_mm, wait=[(sm.s_sr, sm.s_sr.n), (sm.s_rb, sm.s_rb.n)])
                p.op("scalar", lambda s: s.copy(rec_bc[:, :], ps_misc[:, :]),
                     inc=sm.s_rb, wait=[(sm.t_mm, sm.t_mm.n)])
                p.op("vector", lambda v, sbi=sbi, hh=hh: v.tensor_mul(
                    oh_sb[:, hh * 4096 + sbi * 512: hh * 4096 + (sbi + 1) * 512],
                    ps_oo[hh][:, :], rec_bc[:, :]),
                    inc=sm.v_on, wait=[(sm.s_rb, sm.s_rb.n)] +
                    (pair_resw if sbi == 0 else []))
                sum_gate = [(sm.s_sr, sm.s_sr.n)]
                psh_gate = [(sm.t_mm, sm.t_mm.n)]
        for hh in range(2):
            h = 2 * hp + hh
            if hh == 1:
                for c in range(NCH):
                    p.op("gpsimd", lambda e, c=c, h=h: e.dma_start(
                        xh_sb[:, c * 512:(c + 1) * 512],
                        x_src[c * 128:(c + 1) * 128, h * 512:(h + 1) * 512]),
                        inc=sm.d_xh, inc_k=16,
                        wait=[(sm.v_res, sm.v_res.n)] if c == 0 else ())
            rw = [(sm.v_on, sm.v_on.n - (1 - hh)), (sm.d_xh, sm.d_xh.n)]
            for c in range(NCH):
                p.op("vector", lambda v, c=c, hh=hh: v.tensor_add(
                    xh2_sb[:, c * 512:(c + 1) * 512], xh_sb[:, c * 512:(c + 1) * 512],
                    oh_sb[:, hh * 4096 + c: (hh + 1) * 4096: 8]),
                    inc=sm.v_res, wait=(rw if c == 0 else []) +
                    ([(sm.d_out, sm.d_out.n)] if c == 0 and hh == 1 else []))
                p.op("sync", lambda e, c=c, h=h: e.dma_start(
                    x_dst[c * 128:(c + 1) * 128, h * 512:(h + 1) * 512],
                    xh2_sb[:, c * 512:(c + 1) * 512]),
                    inc=sm.d_out, inc_k=16, wait=[(sm.v_res, sm.v_res.n)])
    p.op("sync", lambda e: e.nop(), wait=[(sm.d_out, sm.d_out.n)])
    p.run()

    es.close()
    esp2.close()


def plan_ln_block(nc, sm, ctx, p, x_src, X_sup, tb, sbq, tiles, psums,
                  xn_scr, gate_sb=None, glogc=None, glog=None, xw=None):
    xsq, xnt, rowbuf = tiles["xsq"], tiles["xnt"], tiles["rowbuf"]
    ps_s, ps_q, ps_A, ps_M, ps_pp = (psums[k] for k in ("s", "q", "A", "M", "pp"))
    row = lambda i: rowbuf[0:1, i * 512:(i + 1) * 512]
    rows = {k: row(i) for i, k in enumerate(("mu", "msq", "var", "t0", "t1", "A", "negMA"))}

    if xw is None:
        for c in range(NCH):
            p.op("sync", lambda e, c=c: e.dma_start(
                X_sup[:, c * 2048 + sbq * 512: c * 2048 + (sbq + 1) * 512],
                x_src[c * 128:(c + 1) * 128, tb * 512:(tb + 1) * 512]),
                inc=sm.d_x, inc_k=16)
        xw = [(sm.d_x, sm.d_x.n)]

    def stats_s(t):
        for c in range(NCH):
            ins = t.matmul(ps_s[:, :], ctx.onesf_col[:, :],
                           X_sup[:, c * 2048 + sbq * 512: c * 2048 + (sbq + 1) * 512],
                           start=(c == 0), stop=(c == 7))
        return ins
    p.op("tensor", stats_s, inc=sm.t_mm, wait=xw)
    for c in range(NCH):
        slot = c % 2
        sq_wait = ([(sm.t_mm, sm.t_mm.n - 1)] if c >= 2 else []) + (xw if c == 0 else [])
        p.op("scalar", lambda e, c=c, slot=slot: e.activation(
            xsq[:, slot * 512:(slot + 1) * 512],
            X_sup[:, c * 2048 + sbq * 512: c * 2048 + (sbq + 1) * 512], AF.Square),
            inc=sm.s_sq, wait=sq_wait)
        p.op("tensor", lambda t, c=c, slot=slot: t.matmul(
            ps_q[:, :], ctx.onesf_col[:, :], xsq[:, slot * 512:(slot + 1) * 512],
            start=(c == 0), stop=(c == 7)),
            inc=sm.t_mm, wait=[(sm.s_sq, sm.s_sq.n)])
    plan_ln_rows(p, sm, ps_s, ps_q, rows, with_ln2=False,
                 pre_wait=[(sm.t_mm, sm.t_mm.n)])
    p.op("tensor", lambda t: t.matmul(ps_A[:, :], ctx.ones_1x128_f[:, :],
                                      rows["A"], start=True, stop=True),
         inc=sm.t_mm, wait=[(sm.sv, sm.sv.n)])
    p.op("tensor", lambda t: t.matmul(ps_M[:, :], ctx.ones_1x128_f[:, :],
                                      rows["negMA"], start=True, stop=True),
         inc=sm.t_mm)
    bcw = [(sm.t_mm, sm.t_mm.n)]
    for c in range(NCH):
        slot = c % 2
        slot_free = []
        if c >= 2:
            slot_free.append((sm.d_xnw, sm.d_xnw.n - 16))
            if gate_sb is not None:
                slot_free.append((sm.t_gm, sm.t_gm.n - 4))
        p.op("vector", lambda e, c=c, slot=slot: e.tensor_mul(
            xnt[:, slot * 512:(slot + 1) * 512],
            X_sup[:, c * 2048 + sbq * 512: c * 2048 + (sbq + 1) * 512], ps_A[:, :]),
            wait=(bcw if c == 0 else []) + slot_free)
        p.op("vector", lambda e, c=c, slot=slot: e.tensor_add(
            xnt[:, slot * 512:(slot + 1) * 512], xnt[:, slot * 512:(slot + 1) * 512],
            ps_M[:, :]), inc=sm.v_xn0)
        p.op("gpsimd", lambda e, c=c, slot=slot: e.dma_start(
            xn_scr[c * 128:(c + 1) * 128, tb * 512:(tb + 1) * 512],
            xnt[:, slot * 512:(slot + 1) * 512]),
            inc=sm.d_xnw, inc_k=16, wait=[(sm.v_xn0, sm.v_xn0.n)])
        if gate_sb is not None:
            for st in range(4):
                gi = sm.t_gm.n
                gw = [(sm.v_gl, gi - 1)] if gi >= 2 else []
                p.op("tensor", lambda t, c=c, st=st, slot=slot, gi=gi: t.matmul(
                    ps_pp[gi % 2][0:128, 0:8],
                    xnt[:, slot * 512 + st * 128: slot * 512 + (st + 1) * 128],
                    gate_sb[:, c * 8:(c + 1) * 8], start=True, stop=True),
                    inc=sm.t_gm, wait=gw + [(sm.v_xn0, sm.v_xn0.n)])
                p.op("vector", lambda e, c=c, st=st, gi=gi: e.tensor_copy(
                    glogc[:, st * 64 + c * 8: st * 64 + (c + 1) * 8],
                    ps_pp[gi % 2][0:128, 0:8]),
                    inc=sm.v_gl, wait=[(sm.t_gm, sm.t_gm.n)])
    if gate_sb is not None:
        for st in range(4):
            stg = sbq * 4 + st
            p.op("vector", lambda e, st=st, stg=stg: e.reduce_sum(
                glog[:, stg * 8:(stg + 1) * 8].rearrange("p (e o) -> p e o", o=1),
                glogc[:, st * 64:(st + 1) * 64].rearrange("p (c e) -> p e c", e=8),
                axis=AX.X), inc=sm.v_gl, wait=[(sm.v_gl, sm.v_gl.n)])


def emit_phase1(nc, ctx, sm, x_src, x_dst, w):
    es = ExitStack()
    sb = lambda name, shape, dt: es.enter_context(nc.sbuf_tensor("p1_" + name, shape, dt))
    X_sup = sb("Xsup", [128, 8 * 2048], F32)
    w1_sb = sb("w1", [128, 8 * 2048], BF16)
    w2_sb = sb("w2", [128, 16 * 1024], BF16)
    h_sb = sb("h", [128, 16 * 512], BF16)
    xn_blk = sb("xnblk", [128, 2 * 8 * 512], BF16)
    xsq = sb("xsq", [128, 2 * 512], F32)
    xnt = sb("xnt", [128, 2 * 512], F32)
    rowbuf = sb("rowbuf", [1, 7 * 512], F32)
    glog = sb("glog", [128, 128], F32)
    glogc = sb("glogc", [128, 256], F32)
    ga = sb("ga", [128, 128], F32)
    gb = sb("gb", [128, 128], F32)
    gc = sb("gc", [128, 128], F32)
    gm1 = sb("gm1", [128, 16], F32)
    gm2 = sb("gm2", [128, 16], F32)
    gm3 = sb("gm3", [128, 16], F32)
    g_bf = sb("g_bf", [128, 128], BF16)
    g_sb = sb("g_sb", [8, 2048], BF16)
    bcg_sb = sb("bcg", [128, 512], BF16)
    b1e_row = sb("b1e", [1, 2 * 2048], BF16)
    b2e_row = sb("b2e", [1, 2 * 1024], BF16)
    gate_sb = sb("gate", [128, 64], F32)
    xn_scr = nc.dram_tensor("p1_xn_scr", [1024, 4096], BF16, kind="Internal").ap()

    esp = ExitStack()
    psb = lambda name, shape, dt=F32: esp.enter_context(
        nc.psum_tensor("p1_" + name, shape, dt))
    ps_s = psb("s", [1, 512])
    ps_q = psb("q", [1, 512])
    ps_A = psb("A", [128, 512])
    ps_M = psb("M", [128, 512])
    ps_p0 = psb("pp0", [128, 512])
    ps_p1 = psb("pp1", [128, 512])
    ps_pp = [ps_p0, ps_p1]
    ps_gt = psb("gt", [8, 128], BF16)
    ps_bc = psb("bc", [128, 512])
    tiles = dict(xsq=xsq, xnt=xnt, rowbuf=rowbuf)
    psums = dict(s=ps_s, q=ps_q, A=ps_A, M=ps_M, pp=ps_pp)

    p = Plan(nc)
    p.op("sync", lambda e: e.dma_start(
        gate_sb[:, :].rearrange("p (c e) -> p c e", e=8),
        w["L0_ffn_gate"].rearrange("(c p) e -> p c e", p=128)),
        inc=sm.d_g, inc_k=16)
    p.op("sync", lambda e: e.nop(), wait=[(sm.d_g, sm.d_g.n)])
    p.run()

    for u in range(2):
        xbase = sm.d_x.n
        xws = []
        pld = Plan(nc)
        for sbq in range(4):
            for c in range(NCH):
                pld.op("sync", lambda e, c=c, sbq=sbq, u=u: e.dma_start(
                    X_sup[:, c * 2048 + sbq * 512: c * 2048 + (sbq + 1) * 512],
                    x_src[c * 128:(c + 1) * 128,
                          (u * 4 + sbq) * 512:(u * 4 + sbq + 1) * 512]),
                    inc=sm.d_x, inc_k=16)
            xws.append([(sm.d_x, sm.d_x.n)])
        pld.run()
        for sbq in range(4):
            p = Plan(nc)
            plan_ln_block(nc, sm, ctx, p, x_src, X_sup, u * 4 + sbq, sbq, tiles,
                          psums, xn_scr, gate_sb, glogc, glog, xw=xws[sbq])
            p.run()

        p = Plan(nc)

        def vstep(build, wait=()):
            p.op("vector", build, inc=sm.sv, wait=list(wait) + [(sm.sv, sm.sv.n)])
        g3 = lambda t: t[:, :].rearrange("p (s e) -> p s e", e=8)
        bc16 = lambda t: t[:, :].unsqueeze(2).broadcast_to([128, 16, 8])
        vstep(lambda e: e.reduce_max(gm1[:, :], g3(glog), axis=AX.X),
              wait=[(sm.v_gl, sm.v_gl.n)])
        vstep(lambda e: e.tensor_sub(g3(ga), g3(glog), bc16(gm1)))
        p.op("scalar", lambda e: e.activation(gb[:, :], ga[:, :], AF.Exp),
             inc=sm.ss, wait=[(sm.sv, sm.sv.n)])
        vstep(lambda e: e.reduce_sum(gm2[:, :], g3(gb), axis=AX.X),
              wait=[(sm.ss, sm.ss.n)])
        vstep(lambda e: e.reciprocal(gm3[:, :], gm2[:, :]))
        vstep(lambda e: e.tensor_mul(g3(ga), g3(gb), bc16(gm3)))
        vstep(lambda e: e.reduce_max(gm1[:, :], g3(ga), axis=AX.X))
        vstep(lambda e: e.tensor_tensor(g3(gb), g3(ga), bc16(gm1), ALU.is_equal))
        vstep(lambda e: e.scalar_tensor_tensor(g3(gb), g3(gb), -1e9, g3(ga),
                                               ALU.mult, ALU.add))
        vstep(lambda e: e.reduce_max(gm2[:, :], g3(gb), axis=AX.X))
        vstep(lambda e: e.tensor_tensor(g3(gb), g3(ga), bc16(gm2), ALU.is_ge))
        vstep(lambda e: e.tensor_mul(g3(ga), g3(ga), g3(gb)))
        vstep(lambda e: e.reduce_sum(gm1[:, :], g3(ga), axis=AX.X))
        vstep(lambda e: e.tensor_scalar_max(gm1[:, :], gm1[:, :], 1e-12))
        vstep(lambda e: e.reciprocal(gm2[:, :], gm1[:, :]))
        vstep(lambda e: e.tensor_mul(g3(ga), g3(ga), bc16(gm2)))
        p.op("vector", lambda e: e.tensor_copy(g_bf[:, :], ga[:, :]),
             inc=sm.v_gbf, wait=[(sm.sv, sm.sv.n)])
        for st in range(16):
            p.op("tensor", lambda t, st=st: t.transpose(
                ps_gt[:, :], g_bf[:, st * 8:(st + 1) * 8], ctx.identb[:, :]),
                inc=sm.t_tr,
                wait=[(sm.v_gbf, sm.v_gbf.n)] + ([(sm.s_tr, sm.s_tr.n)] if st else []))
            p.op("scalar", lambda e, st=st: e.copy(
                g_sb[0:8, st * 128:(st + 1) * 128], ps_gt[:, :]),
                inc=sm.s_tr, wait=[(sm.t_tr, sm.t_tr.n)])
        p.run()

        wmark = {}
        p = Plan(nc)
        for e_i in range(8):
            if u == 0 and e_i == 0:
                def ld0(g):
                    g.dma_start(w1_sb[:, :].rearrange("p (c f) -> p c f", f=2048),
                                w["L0_ffn_W1"][0].rearrange("(c p) f -> p c f", p=128)
                                ).then_inc(sm.d_wB.h, 16)
                    g.dma_start(w2_sb[:, :].rearrange("p (c f) -> p c f", f=1024),
                                w["L0_ffn_W2"][0].rearrange("(c p) f -> p c f", p=128)
                                ).then_inc(sm.d_wB.h, 16)
                    g.dma_start(b1e_row[:, 0:2048],
                                w["L0_ffn_b1"][0:1, :]).then_inc(sm.d_wB.h, 16)
                    return g.dma_start(b2e_row[:, 0:1024],
                                       w["L0_ffn_b2"][0:1, :])
                sm.d_wB.n += 48
                p.op("gpsimd", ld0, inc=sm.d_wB, inc_k=16)
            wweights = [(sm.d_wB, wmark.get((u, e_i), sm.d_wB.n))]
            for c in range(NCH):
                p.op("sync", lambda e, c=c, u=u: e.dma_start(
                    xn_blk[:, c * 512:(c + 1) * 512],
                    xn_scr[c * 128:(c + 1) * 128, u * 2048: u * 2048 + 512]),
                    inc=sm.d_xn2, inc_k=16,
                    wait=[(sm.d_xnw, sm.d_xnw.n), (sm.t_mm, sm.t_mm.n)] if c == 0 else ())
            bank_records = []
            for s_blk in range(4):
                slot = s_blk % 2
                xnw2 = [(sm.d_xn2, sm.d_xn2.n)]
                p.op("tensor", lambda t, e_i=e_i, s_blk=s_blk, u=u: t.matmul(
                    ps_bc[:, :], ctx.sel_all[:, e_i * 128:(e_i + 1) * 128],
                    g_sb[0:8, s_blk * 512:(s_blk + 1) * 512], start=True, stop=True),
                    inc=sm.t_bc, wait=[(sm.s_bc, sm.s_bc.n)])
                sbc_w = [(sm.t_bc, sm.t_bc.n), (sm.v_h, sm.v_h.n)]
                p.op("scalar", lambda e: e.copy(bcg_sb[:, :], ps_bc[:, :]),
                     inc=sm.s_bc, wait=sbc_w)
                bcg_w = [(sm.s_bc, sm.s_bc.n)]
                for ft in range(16):
                    G = len(bank_records)
                    gw = ([bank_records[G - 2]] if G >= 2 else []) +                          (wweights + xnw2 if ft == 0 else [])

                    def t_h(t, ft=ft, slot=slot, G=G, e_i=e_i):
                        for c in range(NCH):
                            t.matmul(ps_pp[G % 2][:, :],
                                     w1_sb[:, c * 2048 + ft * 128: c * 2048 + (ft + 1) * 128],
                                     xn_blk[:, slot * 4096 + c * 512: slot * 4096 + (c + 1) * 512],
                                     start=(c == 0), stop=False)
                        return t.matmul(ps_pp[G % 2][:, :],
                                        b1e_row[0:1, (e_i % 2) * 2048 + ft * 128:
                                                (e_i % 2) * 2048 + (ft + 1) * 128],
                                        ctx.ones_row_bf[:, :], start=False, stop=True)
                    p.op("tensor", t_h, inc=sm.t_mm, wait=gw)
                    hev_w = [(sm.t_mm, sm.t_mm.n)] + bcg_w
                    p.op("vector", lambda e, ft=ft, G=G: e.scalar_tensor_tensor(
                        h_sb[:, ft * 512:(ft + 1) * 512], ps_pp[G % 2][:, :], 0.0,
                        bcg_sb[:, :], ALU.max, ALU.mult),
                        inc=sm.v_h, wait=hev_w)
                    bank_records.append((sm.v_h, sm.v_h.n))
                t_after_h_last = sm.t_mm.n
                if s_blk < 3:
                    t_after_h = sm.t_mm.n
                    for c in range(NCH):
                        p.op("sync", lambda e, c=c, u=u, s_blk=s_blk, slot=slot: e.dma_start(
                            xn_blk[:, (1 - slot) * 4096 + c * 512: (1 - slot) * 4096 + (c + 1) * 512],
                            xn_scr[c * 128:(c + 1) * 128,
                                   u * 2048 + (s_blk + 1) * 512: u * 2048 + (s_blk + 2) * 512]),
                            inc=sm.d_xn2, inc_k=16,
                            wait=[(sm.t_mm, t_after_h)] if c == 0 else ())
                for dt in range(8):
                    G = len(bank_records)
                    gw = [bank_records[G - 2]] if G >= 2 else []

                    def t_o(t, dt=dt, G=G, e_i=e_i):
                        for hc in range(16):
                            t.matmul(ps_pp[G % 2][:, :],
                                     w2_sb[:, hc * 1024 + dt * 128: hc * 1024 + (dt + 1) * 128],
                                     h_sb[:, hc * 512:(hc + 1) * 512],
                                     start=(hc == 0), stop=False)
                        return t.matmul(ps_pp[G % 2][:, :],
                                        b2e_row[0:1, (e_i % 2) * 1024 + dt * 128:
                                                (e_i % 2) * 1024 + (dt + 1) * 128],
                                        bcg_sb[0:1, :], start=False, stop=True)
                    p.op("tensor", t_o, inc=sm.t_mm,
                         wait=gw + ([(sm.v_h, sm.v_h.n)] if dt == 0 else []))
                    p.op("vector", lambda e, dt=dt, s_blk=s_blk, G=G: e.tensor_add(
                        X_sup[:, dt * 2048 + s_blk * 512: dt * 2048 + (s_blk + 1) * 512],
                        ps_pp[G % 2][:, :],
                        X_sup[:, dt * 2048 + s_blk * 512: dt * 2048 + (s_blk + 1) * 512]),
                        inc=sm.v_o, wait=[(sm.t_mm, sm.t_mm.n)])
                    bank_records.append((sm.v_o, sm.v_o.n))
                if s_blk == 3 and not (u == 1 and e_i == 7):
                    ne = (e_i + 1) % 8
                    nu = u if e_i < 7 else 1
                    t_now = sm.t_mm.n
                    p.op("gpsimd", lambda g, ne=ne, nu=nu: g.dma_start(
                        w1_sb[:, :].rearrange("p (c f) -> p c f", f=2048),
                        w["L0_ffn_W1"][ne].rearrange("(c p) f -> p c f", p=128)),
                        inc=sm.d_wB, inc_k=16, wait=[(sm.t_mm, t_after_h_last)])
                    p.op("gpsimd", lambda g, ne=ne: g.dma_start(
                        b1e_row[:, (ne % 2) * 2048:(ne % 2) * 2048 + 2048],
                        w["L0_ffn_b1"][ne:ne + 1, :]), inc=sm.d_wB, inc_k=16)
                    p.op("gpsimd", lambda g, ne=ne: g.dma_start(
                        b2e_row[:, (ne % 2) * 1024:(ne % 2) * 1024 + 1024],
                        w["L0_ffn_b2"][ne:ne + 1, :]), inc=sm.d_wB, inc_k=16)
                    p.op("gpsimd", lambda g, ne=ne: g.dma_start(
                        w2_sb[:, :].rearrange("p (c f) -> p c f", f=1024),
                        w["L0_ffn_W2"][ne].rearrange("(c p) f -> p c f", p=128)),
                        inc=sm.d_wB, inc_k=16, wait=[(sm.t_mm, t_now)])
                    wmark[(nu, ne)] = sm.d_wB.n
        for c in range(NCH):
            p.op("sync", lambda e, c=c, u=u: e.dma_start(
                x_dst[c * 128:(c + 1) * 128, u * 2048:(u + 1) * 2048],
                X_sup[:, c * 2048:(c + 1) * 2048]),
                inc=sm.d_out, inc_k=16,
                wait=[(sm.v_o, sm.v_o.n)] if c == 0 else ())
        p.op("sync", lambda e: e.nop(), wait=[(sm.d_out, sm.d_out.n)])
        p.run()

    es.close()
    esp.close()


def emit_phase2(nc, ctx, sm, x_src, x_dst, w):
    es = ExitStack()
    sb = lambda name, shape, dt: es.enter_context(nc.sbuf_tensor("p2_" + name, shape, dt))
    v_tok = sb("v_tok", [128, 32 * 1040], BF16)
    cmask = sb("cmask", [128, 4 * 512], BF16)
    selhalf = sb("selhalf", [128, 2], BF16)
    self32 = sb("self32", [128, 2], F32)
    halfsel = sb("halfsel", [2, 128], F32)
    qn_scr = nc.dram_tensor("p2_qn_scr", [1024, 4096], BF16, kind="Internal").ap()
    kn_scr = nc.dram_tensor("p2_kn_scr", [1024, 4096], BF16, kind="Internal").ap()

    p = Plan(nc)

    def g_fn(g):
        g.memset(v_tok[:, :], 1.0)
        for a in range(4):
            mv = cmask[:, a * 512:(a + 1) * 512]
            g.memset(mv, 1.0)
            g.affine_select(mv, mv, [[1, 512]], ALU.is_ge, 0.0,
                            base=-128 * a, channel_multiplier=-1)
        g.memset(selhalf[:, :], 0.0)
        g.memset(selhalf[0:64, 0:1], 1.0)
        g.memset(selhalf[64:128, 1:2], 1.0)
        g.memset(self32[:, :], 0.0)
        g.memset(self32[0:64, 0:1], 1.0)
        g.memset(self32[64:128, 1:2], 1.0)
        return g.nop()
    p.op("gpsimd", g_fn)
    p.run()

    es2 = ExitStack()
    sb2 = lambda name, shape, dt: es2.enter_context(nc.sbuf_tensor("p2p_" + name, shape, dt))
    wq_sb = sb2("wq", [128, 8 * 1024], BF16)
    wk_sb = sb2("wk", [128, 8 * 1024], BF16)
    wv_sb = sb2("wv", [128, 8 * 1024], BF16)
    rowbuf = sb2("rowbuf", [1, 7 * 512], F32)
    xblk = sb2("xblk", [128, 8 * 512], F32)
    xn_blk = sb2("xn", [128, 8 * 512], BF16)
    xsq = sb2("xsq", [128, 2 * 512], F32)
    xnt = sb2("xnt", [128, 2 * 512], F32)
    q_full = sb2("qfull", [128, 8 * 512], BF16)
    sq_full = sb2("sqfull", [128, 8 * 512], BF16)
    sr2 = sb2("sr2", [2, 2 * 512], F32)
    rn2 = sb2("rn2", [2, 2 * 512], F32)

    esp = ExitStack()
    psb = lambda name, shape, dt=F32: esp.enter_context(
        nc.psum_tensor("p2_" + name, shape, dt))
    ps_s = psb("s", [1, 512])
    ps_q = psb("q", [1, 512])
    ps_A = psb("A", [128, 512])
    ps_M = psb("M", [128, 512])
    ps_p0 = psb("pp0", [128, 512])
    ps_p1 = psb("pp1", [128, 512])
    ps_pp = [ps_p0, ps_p1]
    ps_ss = psb("ss", [2, 512])
    ps_fac = psb("fac", [128, 512])

    p = Plan(nc)
    for nm, dst in (("L1_attn_Wq", wq_sb), ("L1_attn_Wk", wk_sb), ("L1_attn_Wv", wv_sb)):
        p.op("gpsimd", lambda g, nm=nm, dst=dst: g.dma_start(
            dst[:, :].rearrange("p (c f) -> p c f", f=1024),
            w[nm].rearrange("(c p) f -> p c f", p=128)), inc=sm.d_w, inc_k=16)
    p.op("tensor", lambda t: t.transpose(ps_ss[0:2, 0:128], self32[:, :],
                                         ctx.identf[:, :]), inc=sm.t_mm)
    p.op("scalar", lambda e: e.copy(halfsel[:, :], ps_ss[0:2, 0:128]),
         inc=sm.s_ev, wait=[(sm.t_mm, sm.t_mm.n)])
    p.run()

    row = lambda i: rowbuf[0:1, i * 512:(i + 1) * 512]
    rows = {k: row(i) for i, k in enumerate(
        ("mu", "msq", "var", "t0", "t1", "A", "negMA"))}

    p = Plan(nc)
    t_vproj_end = 0
    t_statq_end = 0
    for sbi in range(8):
        xdma_gate = [(sm.v_xn0, sm.v_xn0.n), (sm.s_sq, sm.s_sq.n),
                     (sm.t_mm, t_statq_end)] if sbi else []
        for c in range(NCH):
            p.op("sync", lambda e, c=c, sbi=sbi: e.dma_start(
                xblk[:, c * 512:(c + 1) * 512],
                x_src[c * 128:(c + 1) * 128, sbi * 512:(sbi + 1) * 512]),
                inc=sm.d_x, inc_k=16, wait=xdma_gate if c == 0 else ())
        xw = [(sm.d_x, sm.d_x.n)]

        def stats_s(t):
            for c in range(NCH):
                ins = t.matmul(ps_s[:, :], ctx.onesf_col[:, :],
                               xblk[:, c * 512:(c + 1) * 512],
                               start=(c == 0), stop=(c == 7))
            return ins
        p.op("tensor", stats_s, inc=sm.t_mm, wait=xw + [(sm.sv, sm.sv.n)])
        for c in range(NCH):
            slot = c % 2
            sq_wait = ([(sm.t_mm, sm.t_mm.n - 1)] if c >= 2 else []) + (xw if c == 0 else [])
            if sbi and c < 2:
                sq_wait = sq_wait + [(sm.t_mm, t_statq_end)]
            p.op("scalar", lambda e, c=c, slot=slot: e.activation(
                xsq[:, slot * 512:(slot + 1) * 512], xblk[:, c * 512:(c + 1) * 512],
                AF.Square), inc=sm.s_sq, wait=sq_wait)
            p.op("tensor", lambda t, c=c, slot=slot: t.matmul(
                ps_q[:, :], ctx.onesf_col[:, :], xsq[:, slot * 512:(slot + 1) * 512],
                start=(c == 0), stop=(c == 7)),
                inc=sm.t_mm, wait=[(sm.s_sq, sm.s_sq.n)])
        t_statq_end = sm.t_mm.n
        plan_ln_rows(p, sm, ps_s, ps_q, rows, with_ln2=False,
                     pre_wait=[(sm.t_mm, sm.t_mm.n)])
        p.op("tensor", lambda t: t.matmul(ps_A[:, :], ctx.ones_1x128_f[:, :],
                                          rows["A"], start=True, stop=True),
             inc=sm.t_mm, wait=[(sm.sv, sm.sv.n), (sm.v_xn0, sm.v_xn0.n)])
        p.op("tensor", lambda t: t.matmul(ps_M[:, :], ctx.ones_1x128_f[:, :],
                                          rows["negMA"], start=True, stop=True),
             inc=sm.t_mm)
        bcw = [(sm.t_mm, sm.t_mm.n)]
        for c in range(NCH):
            slot = c % 2
            slot_free = [(sm.s_xn, sm.s_xn.n - 1)] if sm.s_xn.n >= 1 else []
            p.op("vector", lambda e, c=c, slot=slot: e.tensor_mul(
                xnt[:, slot * 512:(slot + 1) * 512], xblk[:, c * 512:(c + 1) * 512],
                ps_A[:, :]), wait=(bcw if c == 0 else []) + slot_free)
            p.op("vector", lambda e, c=c, slot=slot: e.tensor_add(
                xnt[:, slot * 512:(slot + 1) * 512], xnt[:, slot * 512:(slot + 1) * 512],
                ps_M[:, :]), inc=sm.v_xn0)
            xng = [(sm.t_mm, t_vproj_end)] if (sbi and c == 0) else []
            p.op("scalar", lambda e, c=c, slot=slot: e.activation(
                xn_blk[:, c * 512:(c + 1) * 512], xnt[:, slot * 512:(slot + 1) * 512],
                AF.Copy), inc=sm.s_xn, wait=[(sm.v_xn0, sm.v_xn0.n)] + xng)
        xnw = [(sm.s_xn, sm.s_xn.n)]

        for wsb, scr in ((wq_sb, qn_scr), (wk_sb, kn_scr)):
            stage_a = []
            stage_b = []

            def plan_chain_a(fc):
                sl = fc % 2
                p.op("tensor", lambda t, fc=fc: t.matmul(
                    ps_ss[:, :], selhalf[:, :], sq_full[:, fc * 512:(fc + 1) * 512],
                    start=True, stop=True),
                    inc=sm.t_ss, wait=[(sm.sv, sm.sv.n), (sm.s_ev, sm.s_ev.n)])
                p.op("vector", lambda e, sl=sl: e.tensor_scalar_max(
                    sr2[:, sl * 512:(sl + 1) * 512], ps_ss[:, :], 1e-24),
                    inc=sm.sv, wait=[(sm.t_ss, sm.t_ss.n), (sm.sv, sm.sv.n)])
                p.op("scalar", lambda e, sl=sl: e.activation(
                    rn2[:, sl * 512:(sl + 1) * 512], sr2[:, sl * 512:(sl + 1) * 512],
                    AF.Ln), inc=sm.ss, wait=[(sm.sv, sm.sv.n)])
                p.op("scalar", lambda e, sl=sl: e.activation(
                    rn2[:, sl * 512:(sl + 1) * 512], rn2[:, sl * 512:(sl + 1) * 512],
                    AF.Exp, scale=np.float32(-0.5)), inc=sm.ss)

            def plan_chain_b(fc, scr, sbi):
                sl = fc % 2
                p.op("tensor", lambda t, sl=sl: t.matmul(
                    ps_fac[:, :], halfsel[:, :], rn2[:, sl * 512:(sl + 1) * 512],
                    start=True, stop=True),
                    inc=sm.t_fac, wait=[(sm.ss, sm.ss.n), (sm.v_ev, sm.v_ev.n)])
                p.op("vector", lambda e, fc=fc: e.tensor_mul(
                    q_full[:, fc * 512:(fc + 1) * 512],
                    q_full[:, fc * 512:(fc + 1) * 512], ps_fac[:, :]),
                    inc=sm.v_ev, wait=[(sm.t_fac, sm.t_fac.n)])
                p.op("sync", lambda e, fc=fc, sbi=sbi, scr=scr: e.dma_start(
                    scr[fc * 128:(fc + 1) * 128, sbi * 512:(sbi + 1) * 512],
                    q_full[:, fc * 512:(fc + 1) * 512]),
                    inc=sm.d_qk, inc_k=16, wait=[(sm.v_ev, sm.v_ev.n)])

            for fc in range(8):
                pj = fc % 2

                def t_proj(t, wsb=wsb, fc=fc, pj=pj):
                    for c in range(NCH):
                        ins = t.matmul(
                            ps_pp[pj][:, :],
                            wsb[:, c * 1024 + fc * 128: c * 1024 + (fc + 1) * 128],
                            xn_blk[:, c * 512:(c + 1) * 512],
                            start=(c == 0), stop=(c == 7))
                    return ins
                pw = [(sm.s_ev, sm.s_ev.n - 2)] if fc >= 2 else []
                p.op("tensor", t_proj, inc=sm.t_mm,
                     wait=pw + (xnw if fc == 0 else []))
                mmw = [(sm.t_mm, sm.t_mm.n)]
                dmaw = [(sm.d_qk, sm.d_qk.n - 32)] if sm.d_qk.n >= 32 else []
                p.op("scalar", lambda e, fc=fc, pj=pj: e.copy(
                    q_full[:, fc * 512:(fc + 1) * 512], ps_pp[pj][:, :]),
                    inc=sm.s_ev, wait=mmw + dmaw)
                p.op("scalar", lambda e, fc=fc, pj=pj: e.activation(
                    sq_full[:, fc * 512:(fc + 1) * 512], ps_pp[pj][:, :],
                    AF.Square), inc=sm.s_ev)
                stage_a.append(fc)
                if len(stage_a) >= 2:
                    plan_chain_a(stage_a.pop(0))
                    stage_b.append(fc - 1)
                if len(stage_b) >= 2:
                    plan_chain_b(stage_b.pop(0), scr, sbi)
            while stage_a:
                fca = stage_a.pop(0)
                plan_chain_a(fca)
                stage_b.append(fca)
            while stage_b:
                plan_chain_b(stage_b.pop(0), scr, sbi)
        vrec = []
        for tt in range(4):
            for half in range(2):
                gi = len(vrec)
                ttg = sbi * 4 + tt

                def t_v(t, tt=tt, half=half, gi=gi):
                    for c in range(NCH):
                        ins = t.matmul(
                            ps_pp[gi % 2][:, :],
                            xn_blk[:, c * 512 + tt * 128: c * 512 + (tt + 1) * 128],
                            wv_sb[:, c * 1024 + half * 512: c * 1024 + (half + 1) * 512],
                            start=(c == 0), stop=(c == 7))
                    return ins
                vw = [vrec[gi - 2]] if gi >= 2 else [(sm.v_ev, sm.v_ev.n)]
                p.op("tensor", t_v, inc=sm.t_mm, wait=vw)
                base = ttg * 1040 + half * 8 * 65

                def e_v(e, base=base, gi=gi):
                    return e.tensor_copy(
                        v_tok[:, base: base + 520].rearrange("p (h j) -> p h j", j=65)[:, :, 0:64],
                        ps_pp[gi % 2][:, :].rearrange("p (h j) -> p h j", j=64))
                p.op("vector", e_v, inc=sm.v_ev, wait=[(sm.t_mm, sm.t_mm.n)])
                vrec.append((sm.v_ev, sm.v_ev.n))
        t_vproj_end = sm.t_mm.n
    p.run()
    es2.close()
    esp.close()

    es3 = ExitStack()
    sb3 = lambda name, shape, dt: es3.enter_context(nc.sbuf_tensor("p2a_" + name, shape, dt))
    obuf = sb3("obuf", [128, 8 * 4096], BF16)
    es3a = ExitStack()
    sb3a = lambda name, shape, dt: es3a.enter_context(nc.sbuf_tensor("p2a2_" + name, shape, dt))
    qh2 = sb3a("qh2", [64, 2 * 4096], BF16)
    kh2 = sb3a("kh2", [64, 2 * 4096], BF16)
    exp_sb = sb3a("exp", [128, 2 * 1024], BF16)
    sum_row = sb3a("sumrow", [1, 512], F32)
    rec_row = sb3a("recrow", [1, 512], F32)
    fac_sb = sb3a("fac", [64, 512], F32)

    esp3 = ExitStack()
    psb3 = lambda name, shape, dt=F32: esp3.enter_context(
        nc.psum_tensor("p2a_" + name, shape, dt))
    ps_sc = [psb3("scA", [128, 1024]), psb3("scB", [128, 1024])]
    ps_oo = [psb3("o0", [65, 512]), psb3("o1", [65, 512])]
    ps_f2 = psb3("f2", [64, 512])

    n_qk = sm.d_qk.n
    jglob = [0]
    p = Plan(nc)
    for hp in range(8):
        qk_gate = [(sm.t_sc, sm.t_sc.n), (sm.t_pv, sm.t_pv.n)]
        for hh in range(2):
            h = 2 * hp + hh
            p.op("sync", lambda e, h=h, hh=hh: e.dma_start(
                qh2[:, hh * 4096:(hh + 1) * 4096], qn_scr[h * 64:(h + 1) * 64, :]),
                inc=sm.d_qh, inc_k=16,
                wait=([(sm.d_qk, n_qk)] if h == 0 else []) +
                     (qk_gate if (hh == 0 and hp > 0) else []))
            p.op("sync", lambda e, h=h, hh=hh: e.dma_start(
                kh2[:, hh * 4096:(hh + 1) * 4096], kn_scr[h * 64:(h + 1) * 64, :]),
                inc=sm.d_qh, inc_k=16)
        qkw = [(sm.d_qh, sm.d_qh.n)]
        for ib in range(8):
            njs = 4 * ib + 4
            on_gates = {0: [(sm.v_on, sm.v_on.n - 1)] if sm.v_on.n >= 1 else [],
                        1: [(sm.v_on, sm.v_on.n)]}
            pending = None
            j0_list = [4 * ib, 4 * ib + 2] + list(range(0, 4 * ib, 2))
            for bt in range(njs // 2):
                for hh in range(2):
                    h = 2 * hp + hh
                    j = jglob[0]
                    J0 = j0_list[bt]
                    scw = ([(sm.s_exp, j - 1)] if j >= 2 else []) + \
                        (qkw if (ib == 0 and bt == 0 and hh == 0) else [])

                    def t_sc(t, J0=J0, ib=ib, j=j, hh=hh):
                        t.matmul(ps_sc[j % 2][:, 0:512],
                                 kh2[:, hh * 4096 + J0 * 128: hh * 4096 + (J0 + 1) * 128],
                                 qh2[:, hh * 4096 + ib * 512: hh * 4096 + (ib + 1) * 512],
                                 start=True, stop=True)
                        return t.matmul(
                            ps_sc[j % 2][:, 512:1024],
                            kh2[:, hh * 4096 + (J0 + 1) * 128: hh * 4096 + (J0 + 2) * 128],
                            qh2[:, hh * 4096 + ib * 512: hh * 4096 + (ib + 1) * 512],
                            start=True, stop=True)
                    p.op("tensor", t_sc, inc=sm.t_sc, wait=scw)
                    jglob[0] += 1
                    if pending is not None:
                        p.op("tensor", pending[0], inc=sm.t_pv, wait=pending[1])
                        pending = None
                    p.op("scalar", lambda s, j=j: s.activation(
                        exp_sb[:, (j % 2) * 1024:(j % 2 + 1) * 1024], ps_sc[j % 2][:, :],
                        AF.Exp, bias=np.float32(-8.0), scale=np.float32(8.0)),
                        inc=sm.s_exp, wait=[(sm.t_sc, sm.t_sc.n)])
                    pvw = [(sm.s_exp, sm.s_exp.n)]
                    for q2 in range(2):
                        J = J0 + q2
                        if J >= 4 * ib:
                            a2 = J - 4 * ib
                            p.op("vector", lambda v, j=j, a2=a2, q2=q2: v.tensor_mul(
                                exp_sb[:, (j % 2) * 1024 + q2 * 512: (j % 2) * 1024 + (q2 + 1) * 512],
                                exp_sb[:, (j % 2) * 1024 + q2 * 512: (j % 2) * 1024 + (q2 + 1) * 512],
                                cmask[:, a2 * 512:(a2 + 1) * 512]),
                                inc=sm.v_mk, wait=[(sm.s_exp, sm.s_exp.n)])
                            pvw = [(sm.v_mk, sm.v_mk.n)]

                    lastbt = njs // 2 - 1

                    def t_pv(t, J0=J0, h=h, j=j, bt=bt, lastbt=lastbt, hh=hh):
                        for q2 in range(2):
                            J = J0 + q2
                            ins = t.matmul(
                                ps_oo[hh][:, :],
                                v_tok[:, J * 1040 + h * 65: J * 1040 + (h + 1) * 65],
                                exp_sb[:, (j % 2) * 1024 + q2 * 512: (j % 2) * 1024 + (q2 + 1) * 512],
                                start=(bt == 0 and q2 == 0), stop=(bt == lastbt and q2 == 1))
                        return ins
                    pending = (t_pv, pvw + (on_gates[hh] if bt == 0 else []))
            if pending is not None:
                p.op("tensor", pending[0], inc=sm.t_pv, wait=pending[1])
                pending = None
            for hh in range(2):
                h = 2 * hp + hh
                p.op("scalar", lambda s, hh=hh: s.activation(
                    sum_row[:, :], ps_oo[hh][64:65, :], AF.Ln),
                    inc=sm.s_sr, wait=[(sm.t_pv, sm.t_pv.n)])
                p.op("scalar", lambda s: s.activation(rec_row[:, :], sum_row[:, :],
                                                      AF.Exp, scale=np.float32(-1.0)),
                     inc=sm.s_sr)
                p.op("tensor", lambda t: t.matmul(ps_f2[:, :], ctx.ones_1x128_f[:, 0:64],
                                                  rec_row[:, :], start=True, stop=True),
                     inc=sm.t_fac, wait=[(sm.s_sr, sm.s_sr.n), (sm.s_rb, sm.s_rb.n)])
                p.op("scalar", lambda s: s.copy(fac_sb[:, :], ps_f2[:, :]),
                     inc=sm.s_rb, wait=[(sm.t_fac, sm.t_fac.n)])
                c0 = h // 2
                r0 = (h % 2) * 64
                p.op("vector", lambda v, c0=c0, r0=r0, ib=ib, hh=hh: v.tensor_mul(
                    obuf[r0:r0 + 64, c0 * 4096 + ib * 512: c0 * 4096 + (ib + 1) * 512],
                    ps_oo[hh][0:64, :], fac_sb[:, :]),
                    inc=sm.v_on, wait=[(sm.s_rb, sm.s_rb.n)])
    p.run()

    es3a.close()
    es3b = ExitStack()
    sb3b = lambda name, shape, dt: es3b.enter_context(nc.sbuf_tensor("p2b_" + name, shape, dt))
    wo_sb = sb3b("wo", [128, 8 * 1024], BF16)
    xblk2 = sb3b("xblk2", [128, 8 * 512], F32)
    p = Plan(nc)
    p.op("gpsimd", lambda g: g.dma_start(
        wo_sb[:, :].rearrange("p (c f) -> p c f", f=1024),
        w["L1_attn_Wo"].rearrange("(c p) f -> p c f", p=128)), inc=sm.d_w, inc_k=16)
    p.run()
    esp3.close()
    esp4 = ExitStack()
    psb4 = lambda name, shape, dt=F32: esp4.enter_context(
        nc.psum_tensor("p2e_" + name, shape, dt))
    ps_ee = [psb4("e0", [128, 512]), psb4("e1", [128, 512])]
    ow = [(sm.v_on, sm.v_on.n), (sm.d_w, sm.d_w.n)]
    erec = []
    for sbi in range(8):
        p = Plan(nc)
        for c in range(NCH):
            p.op("sync", lambda e, c=c, sbi=sbi: e.dma_start(
                xblk2[:, c * 512:(c + 1) * 512],
                x_src[c * 128:(c + 1) * 128, sbi * 512:(sbi + 1) * 512]),
                inc=sm.d_x, inc_k=16)
        xw2 = [(sm.d_x, sm.d_x.n)]
        for dt in range(8):
            gi = len(erec)

            def t_e(t, dt=dt, sbi=sbi, gi=gi):
                for c in range(NCH):
                    ins = t.matmul(
                        ps_ee[gi % 2][:, :],
                        wo_sb[:, c * 1024 + dt * 128: c * 1024 + (dt + 1) * 128],
                        obuf[:, c * 4096 + sbi * 512: c * 4096 + (sbi + 1) * 512],
                        start=(c == 0), stop=(c == 7))
                return ins
            ew = ([erec[gi - 2]] if gi >= 2 else []) + (ow if gi == 0 else [])
            p.op("tensor", t_e, inc=sm.t_mm, wait=ew)
            p.op("vector", lambda v, dt=dt, gi=gi: v.tensor_add(
                xblk2[:, dt * 512:(dt + 1) * 512], ps_ee[gi % 2][:, :],
                xblk2[:, dt * 512:(dt + 1) * 512]),
                inc=sm.v_res, wait=[(sm.t_mm, sm.t_mm.n)] + (xw2 if dt == 0 else []))
            erec.append((sm.v_res, sm.v_res.n))
            p.op("sync", lambda e, dt=dt, sbi=sbi: e.dma_start(
                x_dst[dt * 128:(dt + 1) * 128, sbi * 512:(sbi + 1) * 512],
                xblk2[:, dt * 512:(dt + 1) * 512]),
                inc=sm.d_out, inc_k=16, wait=[(sm.v_res, sm.v_res.n)])
        p.run()
    p = Plan(nc)
    p.op("sync", lambda e: e.nop(), wait=[(sm.d_out, sm.d_out.n)])
    p.run()
    es3b.close()
    es3.close()
    esp4.close()
    es.close()


def emit_phase3(nc, ctx, sm, x_src, x_dst, w):
    GC = 1.0 / math.e
    GL = 0.5 - math.log(4.0 / 3.0)
    GU = 0.5
    es = ExitStack()
    sb = lambda name, shape, dt: es.enter_context(nc.sbuf_tensor("p3_" + name, shape, dt))
    X_sup = sb("Xsup", [128, 8 * 2048], F32)
    wg_sb = sb("wg", [128, 8 * 1024], BF16)
    wu_sb = sb("wu", [128, 8 * 1024], BF16)
    wd_sb = sb("wd", [128, 8 * 1024], BF16)
    h_sb = sb("h", [128, 8 * 512], BF16)
    silu_sb = sb("silu", [128, 2 * 512], F32)
    xn_blk = sb("xnblk", [128, 2 * 8 * 512], BF16)
    xsq = sb("xsq", [128, 2 * 512], F32)
    xnt = sb("xnt", [128, 2 * 512], F32)
    rowbuf = sb("rowbuf", [1, 7 * 512], F32)
    glog = sb("glog", [128, 128], F32)
    glogc = sb("glogc", [128, 256], F32)
    ga = sb("ga", [128, 128], F32)
    gb = sb("gb", [128, 128], F32)
    gc_t = sb("gc", [128, 128], F32)
    gd = sb("gd", [128, 128], F32)
    gzn = sb("gzn", [128, 128], F32)
    gm1 = sb("gm1", [128, 16], F32)
    gm2 = sb("gm2", [128, 16], F32)
    gm3 = sb("gm3", [128, 16], F32)
    gne = sb("gne", [128, 16], F32)
    g_bf = sb("g_bf", [128, 128], BF16)
    w_sb = sb("w_sb", [8, 2048], BF16)
    bcw_sb = sb("bcw", [128, 512], BF16)
    wr_sb = sb("wr", [128, 64], F32)
    temp_sb = sb("temp", [1, 1], F32)
    invt_bc = sb("invt", [128, 1], F32)
    xn_scr = nc.dram_tensor("p3_xn_scr", [1024, 4096], BF16, kind="Internal").ap()

    esp = ExitStack()
    psb = lambda name, shape, dt=F32: esp.enter_context(
        nc.psum_tensor("p3_" + name, shape, dt))
    ps_s = psb("s", [1, 512])
    ps_q = psb("q", [1, 512])
    ps_A = psb("A", [128, 512])
    ps_M = psb("M", [128, 512])
    ps_p0 = psb("pp0", [128, 512])
    ps_p1 = psb("pp1", [128, 512])
    ps_pp = [ps_p0, ps_p1]
    ps_gt = psb("gt", [8, 128], BF16)
    ps_bc = psb("bc", [128, 512])
    tiles = dict(xsq=xsq, xnt=xnt, rowbuf=rowbuf)
    psums = dict(s=ps_s, q=ps_q, A=ps_A, M=ps_M, pp=ps_pp)

    p = Plan(nc)
    p.op("sync", lambda e: e.dma_start(
        wr_sb[:, :].rearrange("p (c e) -> p c e", e=8),
        w["L1_ffn_Wr"].rearrange("(c p) e -> p c e", p=128)), inc=sm.d_g, inc_k=16)
    p.op("sync", lambda e: e.dma_start(
        temp_sb[:, :], w["L1_ffn_temperature"].rearrange("(a b) -> a b", a=1)),
        inc=sm.d_g, inc_k=16)
    p.op("tensor", lambda t: t.matmul(ps_bc[:, 0:1], ctx.ones_1x128_f[:, :],
                                      temp_sb[:, :], start=True, stop=True),
         inc=sm.t_mm, wait=[(sm.d_g, sm.d_g.n)])
    p.op("scalar", lambda e: e.copy(invt_bc[:, :], ps_bc[:, 0:1]),
         inc=sm.s_ev, wait=[(sm.t_mm, sm.t_mm.n)])
    p.op("vector", lambda e: e.reciprocal(invt_bc[:, :], invt_bc[:, :]),
         inc=sm.sv, wait=[(sm.s_ev, sm.s_ev.n)])
    p.run()

    for u in range(2):
        xws = []
        pld = Plan(nc)
        for sbq in range(4):
            for c in range(NCH):
                pld.op("sync", lambda e, c=c, sbq=sbq, u=u: e.dma_start(
                    X_sup[:, c * 2048 + sbq * 512: c * 2048 + (sbq + 1) * 512],
                    x_src[c * 128:(c + 1) * 128,
                          (u * 4 + sbq) * 512:(u * 4 + sbq + 1) * 512]),
                    inc=sm.d_x, inc_k=16)
            xws.append([(sm.d_x, sm.d_x.n)])
        pld.run()
        for sbq in range(4):
            p = Plan(nc)
            plan_ln_block(nc, sm, ctx, p, x_src, X_sup, u * 4 + sbq, sbq, tiles,
                          psums, xn_scr, wr_sb, glogc, glog, xw=xws[sbq])
            p.run()

        p = Plan(nc)

        def vstep(build, wait=()):
            p.op("vector", build, inc=sm.sv, wait=list(wait) + [(sm.sv, sm.sv.n)])

        def sstep(build):
            p.op("scalar", build, inc=sm.ss, wait=[(sm.sv, sm.sv.n), (sm.ss, sm.ss.n)])
        g3 = lambda t: t[:, :].rearrange("p (s e) -> p s e", e=8)
        bc16 = lambda t: t[:, :].unsqueeze(2).broadcast_to([128, 16, 8])
        p.op("scalar", lambda e: e.activation(ga[:, :], glog[:, :], AF.Sigmoid,
                                              scale=invt_bc[:, :]),
             inc=sm.ss, wait=[(sm.v_gl, sm.v_gl.n), (sm.sv, sm.sv.n)])
        p.op("scalar", lambda e: e.activation(gb[:, :], ga[:, :], AF.Abs,
                                              bias=np.float32(-GC)), inc=sm.ss)
        vstep(lambda e: e.tensor_single_scalar(gzn[:, :], ga[:, :],
                                               float(GL), ALU.is_ge),
              wait=[(sm.ss, sm.ss.n)])
        vstep(lambda e: e.tensor_single_scalar(gc_t[:, :], ga[:, :],
                                               float(GU), ALU.is_le))
        vstep(lambda e: e.tensor_mul(gzn[:, :], gzn[:, :], gc_t[:, :]))
        sstep(lambda e: e.activation(gc_t[:, :], gb[:, :], AF.Exp,
                                     scale=np.float32(-10.0)))
        vstep(lambda e: e.tensor_mul(gc_t[:, :], gc_t[:, :], gzn[:, :]),
              wait=[(sm.ss, sm.ss.n)])
        vstep(lambda e: e.reduce_sum(gm1[:, :], g3(gc_t), axis=AX.X))
        vstep(lambda e: e.tensor_single_scalar(gne[:, :], gm1[:, :],
                                               1e-8, ALU.is_lt))
        sstep(lambda e: e.activation(gd[:, :], gb[:, :], AF.Exp,
                                     scale=np.float32(-10.0 / 3.0)))
        vstep(lambda e: e.reduce_max(gm1[:, :], g3(gd), axis=AX.X),
              wait=[(sm.ss, sm.ss.n)])
        vstep(lambda e: e.tensor_tensor(g3(gb), g3(gd), bc16(gm1), ALU.is_equal))
        vstep(lambda e: e.scalar_tensor_tensor(g3(gb), g3(gb), -1e9, g3(gd),
                                               ALU.mult, ALU.add))
        vstep(lambda e: e.reduce_max(gm2[:, :], g3(gb), axis=AX.X))
        vstep(lambda e: e.tensor_tensor(g3(gb), g3(gd), bc16(gm2), ALU.is_ge))
        vstep(lambda e: e.tensor_mul(gb[:, :], gb[:, :], gd[:, :]))
        vstep(lambda e: e.reduce_sum(gm3[:, :], g3(gb), axis=AX.X))
        vstep(lambda e: e.tensor_scalar_max(gm3[:, :], gm3[:, :], 1e-8))
        vstep(lambda e: e.reciprocal(gm3[:, :], gm3[:, :]))
        vstep(lambda e: e.tensor_mul(g3(gb), g3(gb), bc16(gm3)))
        vstep(lambda e: e.tensor_sub(gd[:, :], gb[:, :], gc_t[:, :]))
        vstep(lambda e: e.tensor_mul(g3(gd), g3(gd), bc16(gne)))
        vstep(lambda e: e.tensor_add(gc_t[:, :], gc_t[:, :], gd[:, :]))
        vstep(lambda e: e.reduce_sum(gm1[:, :], g3(gc_t), axis=AX.X))
        vstep(lambda e: e.tensor_scalar_max(gm1[:, :], gm1[:, :], 1e-8))
        vstep(lambda e: e.reciprocal(gm1[:, :], gm1[:, :]))
        vstep(lambda e: e.tensor_mul(g3(gc_t), g3(gc_t), bc16(gm1)))
        p.op("vector", lambda e: e.tensor_copy(g_bf[:, :], gc_t[:, :]),
             inc=sm.v_gbf, wait=[(sm.sv, sm.sv.n)])
        for st in range(16):
            p.op("tensor", lambda t, st=st: t.transpose(
                ps_gt[:, :], g_bf[:, st * 8:(st + 1) * 8], ctx.identb[:, :]),
                inc=sm.t_tr,
                wait=[(sm.v_gbf, sm.v_gbf.n)] + ([(sm.s_tr, sm.s_tr.n)] if st else []))
            p.op("scalar", lambda e, st=st: e.copy(
                w_sb[0:8, st * 128:(st + 1) * 128], ps_gt[:, :]),
                inc=sm.s_tr, wait=[(sm.t_tr, sm.t_tr.n)])
        p.run()

        p = Plan(nc)
        wmark3 = {}
        for e_i in range(8):
            if e_i == 0:
                for nm, dst in (("L1_ffn_Wg", wg_sb), ("L1_ffn_Wu", wu_sb),
                                ("L1_ffn_Wd", wd_sb)):
                    p.op("gpsimd", lambda g, nm=nm, dst=dst: g.dma_start(
                        dst[:, :].rearrange("p (c f) -> p c f", f=1024),
                        w[nm][0].rearrange("(c p) f -> p c f", p=128)),
                        inc=sm.d_wB, inc_k=16,
                        wait=[(sm.t_mm, sm.t_mm.n)] if nm == "L1_ffn_Wg" else ())
                wweights = [(sm.d_wB, sm.d_wB.n)]
            else:
                wweights = [(sm.d_wB, wmark3[e_i])]
            for c in range(NCH):
                p.op("sync", lambda e, c=c, u=u: e.dma_start(
                    xn_blk[:, c * 512:(c + 1) * 512],
                    xn_scr[c * 128:(c + 1) * 128, u * 2048: u * 2048 + 512]),
                    inc=sm.d_xn2, inc_k=16,
                    wait=[(sm.d_xnw, sm.d_xnw.n), (sm.t_mm, sm.t_mm.n)] if c == 0 else ())
            if e_i == 0:
                bank_records = []
            ps_pp3 = [ps_p0, ps_p1, ps_A]
            for s_blk in range(4):
                slot = s_blk % 2
                xnw2 = [(sm.d_xn2, sm.d_xn2.n)]
                p.op("tensor", lambda t, e_i=e_i, s_blk=s_blk: t.matmul(
                    ps_bc[:, :], ctx.sel_all[:, e_i * 128:(e_i + 1) * 128],
                    w_sb[0:8, s_blk * 512:(s_blk + 1) * 512], start=True, stop=True),
                    inc=sm.t_bc, wait=[(sm.s_bc, sm.s_bc.n)])
                p.op("scalar", lambda e: e.copy(bcw_sb[:, :], ps_bc[:, :]),
                     inc=sm.s_bc, wait=[(sm.t_bc, sm.t_bc.n), (sm.v_h, sm.v_h.n)])
                bcw_w = [(sm.s_bc, sm.s_bc.n)]
                for ic in range(NCH):
                    sl2 = ic % 2
                    G = len(bank_records)
                    gw = ([bank_records[G - 3]] if G >= 3 else []) + (wweights + xnw2 if ic == 0 else [])

                    def t_g(t, ic=ic, slot=slot, G=G):
                        for c in range(NCH):
                            ins = t.matmul(
                                ps_pp3[G % 3][:, :],
                                wg_sb[:, c * 1024 + ic * 128: c * 1024 + (ic + 1) * 128],
                                xn_blk[:, slot * 4096 + c * 512: slot * 4096 + (c + 1) * 512],
                                start=(c == 0), stop=(c == 7))
                        return ins
                    p.op("tensor", t_g, inc=sm.t_mm, wait=gw)
                    si_w = [(sm.t_mm, sm.t_mm.n)]
                    if sm.v_h.n >= 1:
                        si_w.append((sm.v_h, sm.v_h.n - 1))
                    p.op("scalar", lambda e, sl2=sl2, G=G: e.activation(
                        silu_sb[:, sl2 * 512:(sl2 + 1) * 512], ps_pp3[G % 3][:, :],
                        AF.Silu), inc=sm.s_si, wait=si_w)
                    bank_records.append((sm.s_si, sm.s_si.n))
                    G = len(bank_records)
                    gw = [bank_records[G - 3]] if G >= 3 else []

                    def t_u(t, ic=ic, slot=slot, G=G):
                        for c in range(NCH):
                            ins = t.matmul(
                                ps_pp3[G % 3][:, :],
                                wu_sb[:, c * 1024 + ic * 128: c * 1024 + (ic + 1) * 128],
                                xn_blk[:, slot * 4096 + c * 512: slot * 4096 + (c + 1) * 512],
                                start=(c == 0), stop=(c == 7))
                        return ins
                    p.op("tensor", t_u, inc=sm.t_mm, wait=gw)
                    p.op("vector", lambda e, sl2=sl2: e.tensor_mul(
                        silu_sb[:, sl2 * 512:(sl2 + 1) * 512],
                        silu_sb[:, sl2 * 512:(sl2 + 1) * 512], bcw_sb[:, :]),
                        wait=[(sm.s_si, sm.s_si.n)] + bcw_w)
                    p.op("vector", lambda e, ic=ic, sl2=sl2, G=G: e.tensor_mul(
                        h_sb[:, ic * 512:(ic + 1) * 512], ps_pp3[G % 3][:, :],
                        silu_sb[:, sl2 * 512:(sl2 + 1) * 512]),
                        inc=sm.v_h, wait=[(sm.t_mm, sm.t_mm.n)])
                    bank_records.append((sm.v_h, sm.v_h.n))
                if s_blk == 3:
                    t_after_gu = sm.t_mm.n
                for dt2 in range(8):
                    G = len(bank_records)
                    gw = [bank_records[G - 3]] if G >= 3 else []

                    def t_d(t, dt2=dt2, G=G):
                        for ic in range(NCH):
                            ins = t.matmul(
                                ps_pp3[G % 3][:, :],
                                wd_sb[:, ic * 1024 + dt2 * 128: ic * 1024 + (dt2 + 1) * 128],
                                h_sb[:, ic * 512:(ic + 1) * 512],
                                start=(ic == 0), stop=(ic == 7))
                        return ins
                    p.op("tensor", t_d, inc=sm.t_mm,
                         wait=gw + ([(sm.v_h, sm.v_h.n)] if dt2 == 0 else []))
                    p.op("vector", lambda e, dt2=dt2, s_blk=s_blk, G=G: e.tensor_add(
                        X_sup[:, dt2 * 2048 + s_blk * 512: dt2 * 2048 + (s_blk + 1) * 512],
                        ps_pp3[G % 3][:, :],
                        X_sup[:, dt2 * 2048 + s_blk * 512: dt2 * 2048 + (s_blk + 1) * 512]),
                        inc=sm.v_o, wait=[(sm.t_mm, sm.t_mm.n)])
                    bank_records.append((sm.v_o, sm.v_o.n))
                if s_blk < 3:
                    t_after_h = sm.t_mm.n
                    for c in range(NCH):
                        p.op("sync", lambda e, c=c, u=u, s_blk=s_blk, slot=slot: e.dma_start(
                            xn_blk[:, (1 - slot) * 4096 + c * 512: (1 - slot) * 4096 + (c + 1) * 512],
                            xn_scr[c * 128:(c + 1) * 128,
                                   u * 2048 + (s_blk + 1) * 512: u * 2048 + (s_blk + 2) * 512]),
                            inc=sm.d_xn2, inc_k=16,
                            wait=[(sm.t_mm, t_after_h)] if c == 0 else ())
                elif e_i < 7:
                    ne = e_i + 1
                    t_now = sm.t_mm.n
                    for nm, dst, gv in (("L1_ffn_Wg", wg_sb, t_after_gu),
                                        ("L1_ffn_Wu", wu_sb, t_after_gu),
                                        ("L1_ffn_Wd", wd_sb, t_now)):
                        p.op("gpsimd", lambda g, nm=nm, dst=dst, ne=ne: g.dma_start(
                            dst[:, :].rearrange("p (c f) -> p c f", f=1024),
                            w[nm][ne].rearrange("(c p) f -> p c f", p=128)),
                            inc=sm.d_wB, inc_k=16, wait=[(sm.t_mm, gv)])
                    wmark3[ne] = sm.d_wB.n
        for c in range(NCH):
            p.op("sync", lambda e, c=c, u=u: e.dma_start(
                x_dst[c * 128:(c + 1) * 128, u * 2048:(u + 1) * 2048],
                X_sup[:, c * 2048:(c + 1) * 2048]),
                inc=sm.d_out, inc_k=16,
                wait=[(sm.v_o, sm.v_o.n)] if c == 0 else ())
        p.op("sync", lambda e: e.nop(), wait=[(sm.d_out, sm.d_out.n)])
        p.run()

    es.close()
    esp.close()


SEM_NAMES = [
    "sv", "ss", "t_mm", "s_ev", "v_ev", "s_sq", "v_cast", "v_xn0", "s_xn",
    "d_w", "d_v", "d_wv", "v_wvr", "d_x", "d_qk", "d_qh", "d_xh",
    "t_sc", "s_exp", "t_pv", "s_sr", "s_rb", "v_on", "v_res", "d_out",
    "d_xnw", "d_xn2", "d_wB", "t_bc", "s_bc", "v_h", "v_o", "t_gm", "v_gl",
    "t_tr", "s_tr", "v_gbf", "d_g", "t_ss", "t_fac", "v_mk", "s_si", "v_ps",
]


def build_program(phases):
    nc = bass.Bass("TRN2", target_bir_lowering=False)
    x_in = nc.dram_tensor("x_in", [1024, 4096], F32, kind="ExternalInput").ap()
    x_out = nc.dram_tensor("x_out", [1024, 4096], F32, kind="ExternalOutput").ap()
    w = {}
    shapes = {
        "L0_attn_Wq": [1024, 1024], "L0_attn_bq": [1024],
        "L0_attn_Wk": [1024, 1024], "L0_attn_bk": [1024],
        "L0_attn_Wv": [1024, 8192], "L0_attn_bv": [8192],
        "L0_attn_gamma": [1024], "L0_attn_beta": [1024],
        "L0_ffn_gate": [1024, 8], "L0_ffn_W1": [8, 1024, 2048],
        "L0_ffn_b1": [8, 2048], "L0_ffn_W2": [8, 2048, 1024],
        "L0_ffn_b2": [8, 1024],
        "L1_attn_Wq": [1024, 1024], "L1_attn_Wk": [1024, 1024],
        "L1_attn_Wv": [1024, 1024], "L1_attn_Wo": [1024, 1024],
        "L1_ffn_Wg": [8, 1024, 1024], "L1_ffn_Wu": [8, 1024, 1024],
        "L1_ffn_Wd": [8, 1024, 1024], "L1_ffn_Wr": [1024, 8],
        "L1_ffn_temperature": [1],
    }
    for n in WEIGHT_NAMES:
        w[n] = nc.dram_tensor(n, shapes[n], F32, kind="ExternalInput").ap()

    es = ExitStack()
    ctx = type("Ctx", (), {})()
    emit_setup(nc, es, ctx)
    sm = make_sems(nc, es, SEM_NAMES)

    n_ph = len(phases)
    inter = {}
    def x_ap(i):
        if i == 0:
            return x_in
        if i == n_ph:
            return x_out
        if i not in inter:
            inter[i] = nc.dram_tensor(f"x_mid{i}", [1024, 4096], F32,
                                      kind="Internal").ap()
        return inter[i]

    emitters = {0: emit_phase0, 1: emit_phase1, 2: emit_phase2, 3: emit_phase3}
    for idx, ph in enumerate(phases):
        emitters[ph](nc, ctx, sm, x_ap(idx), x_ap(idx + 1), w)
    es.close()
    return nc


def kernel(**inputs):
    x = inputs["L0_attn_x"]
    xs = [np.ascontiguousarray(x[b].T).astype(np.float32) for b in range(8)]
    weights = {n: np.ascontiguousarray(inputs[n], dtype=np.float32)
               for n in WEIGHT_NAMES}
    phase_groups = [(0, 1, 2, 3)]
    cur = xs
    for pg in phase_groups:
        nc = build_program(pg)
        in_maps = [dict(weights, x_in=cur[b]) for b in range(8)]
        res = run_bass_kernel_spmd(nc, in_maps, core_ids=list(range(8)))
        cur = [res.results[b]["x_out"] for b in range(8)]
    out = np.stack([c.T for c in cur]).astype(np.float32)
    return out


# version 15 (final)
# speedup vs baseline: 1.0179x; 1.0070x over previous
import math
import numpy as np
from contextlib import ExitStack

import concourse.bass as bass
import concourse.mybir as mybir
from concourse.bass_utils import run_bass_kernel_spmd

F32 = mybir.dt.float32
BF16 = mybir.dt.bfloat16
AF = mybir.ActivationFunctionType
ALU = mybir.AluOpType
AX = mybir.AxisListType

D = 1024
S = 4096
NCH = 8
LN_EPS = 1e-5
RSQRT_DK0 = 1.0 / math.sqrt(128.0)

ENGINES = ("tensor", "vector", "scalar", "sync", "gpsimd")

WEIGHT_NAMES = [
    "L0_attn_Wq", "L0_attn_bq", "L0_attn_Wk", "L0_attn_bk", "L0_attn_Wv",
    "L0_attn_bv", "L0_attn_gamma", "L0_attn_beta",
    "L0_ffn_gate", "L0_ffn_W1", "L0_ffn_b1", "L0_ffn_W2", "L0_ffn_b2",
    "L1_attn_Wq", "L1_attn_Wk", "L1_attn_Wv", "L1_attn_Wo",
    "L1_ffn_Wg", "L1_ffn_Wu", "L1_ffn_Wd", "L1_ffn_Wr", "L1_ffn_temperature",
]


class Sem:
    def __init__(self, h):
        self.h = h
        self.n = 0


class Plan:

    def __init__(self, nc):
        self.nc = nc
        self.q = {e: [] for e in ENGINES}

    def op(self, eng, build, inc=None, inc_k=1, wait=()):
        waits = [(s.h, v) for (s, v) in wait]
        if inc is not None:
            inc.n += inc_k
            inc_t = (inc.h, inc_k)
        else:
            inc_t = None

        def run(e, waits=waits, build=build, inc_t=inc_t):
            for h, v in waits:
                e.wait_ge(h, v)
            ins = build(e)
            if inc_t is not None:
                ins.then_inc(*inc_t)
        self.q[eng].append(run)

    def run(self):
        with self.nc.Block() as blk:
            for e in ENGINES:
                lst = self.q[e]
                if lst:
                    getattr(blk, e)(lambda eng, lst=lst: [f(eng) for f in lst])


def make_sems(nc, es, names):
    class NS:
        pass
    ns = NS()
    for n in names:
        setattr(ns, n, Sem(es.enter_context(nc.semaphore(n))))
    return ns


def emit_setup(nc, es, ctx):
    ctx.ones_col_bf = es.enter_context(nc.sbuf_tensor("c_ones_col", [128, 1], BF16))
    ctx.ones_row_bf = es.enter_context(nc.sbuf_tensor("c_ones_row", [1, 512], BF16))
    ctx.ones_1x128_bf = es.enter_context(nc.sbuf_tensor("c_ones1x128b", [1, 128], BF16))
    ctx.ones_1x128_f = es.enter_context(nc.sbuf_tensor("c_ones1x128f", [1, 128], F32))
    ctx.identb = es.enter_context(nc.sbuf_tensor("c_identb", [128, 128], BF16))
    ctx.identf = es.enter_context(nc.sbuf_tensor("c_identf", [128, 128], F32))
    ctx.identf8 = es.enter_context(nc.sbuf_tensor("c_identf8", [8, 8], F32))
    ctx.sel_all = es.enter_context(nc.sbuf_tensor("c_sel", [8, 1024], BF16))
    ctx.onesf_col = es.enter_context(nc.sbuf_tensor("c_onesf_col", [128, 1], F32))

    p = Plan(nc)

    def g_fn(g):
        g.memset(ctx.ones_col_bf[:, :], 1.0)
        g.memset(ctx.ones_row_bf[:, :], 1.0)
        g.memset(ctx.ones_1x128_bf[:, :], 1.0)
        g.memset(ctx.ones_1x128_f[:, :], 1.0)
        g.memset(ctx.onesf_col[:, :], 1.0)
        for t, n in ((ctx.identb, 128), (ctx.identf, 128), (ctx.identf8, 8)):
            g.memset(t[:, :], 1.0)
            g.affine_select(t[:, :], t[:, :], [[-1, n]], ALU.is_equal, 0.0,
                            base=0, channel_multiplier=1)
        g.memset(ctx.sel_all[:, :], 1.0)
        g.affine_select(ctx.sel_all[:, :], ctx.sel_all[:, :], [[-1, 8], [0, 128]],
                        ALU.is_equal, 0.0, base=0, channel_multiplier=1)
        return g.nop()
    p.op("gpsimd", g_fn)
    p.run()


def plan_ln_rows(p, sm, ps_s, ps_q, rows, with_ln2, pre_wait):
    mu, msq, var, t0, t1, A, negMA = (rows[k] for k in
                                      ("mu", "msq", "var", "t0", "t1", "A", "negMA"))

    def v(build, wait=()):
        p.op("vector", build, inc=sm.sv, wait=list(wait) + [(sm.sv, sm.sv.n)])

    def s(build):
        p.op("scalar", build, inc=sm.ss, wait=[(sm.sv, sm.sv.n)])

    v(lambda e: e.tensor_scalar_mul(mu[:, :], ps_s[:, :], 1.0 / 1024.0), wait=pre_wait)
    v(lambda e: e.tensor_scalar_mul(msq[:, :], ps_q[:, :], 1.0 / 1024.0))
    v(lambda e: e.tensor_mul(t0[:, :], mu[:, :], mu[:, :]))
    v(lambda e: e.tensor_sub(var[:, :], msq[:, :], t0[:, :]))
    v(lambda e: e.tensor_scalar_add(t0[:, :], var[:, :], LN_EPS))
    s(lambda e: e.activation(t1[:, :], t0[:, :], AF.Ln))
    s(lambda e: e.activation(A[:, :], t1[:, :], AF.Exp, scale=np.float32(-0.5)))
    v(lambda e: e.nop(), wait=[(sm.ss, sm.ss.n)])
    if with_ln2:
        v(lambda e: e.tensor_mul(t0[:, :], A[:, :], A[:, :]))
        v(lambda e: e.tensor_mul(t0[:, :], var[:, :], t0[:, :]))
        v(lambda e: e.tensor_scalar_add(t0[:, :], t0[:, :], LN_EPS))
        s(lambda e: e.activation(t1[:, :], t0[:, :], AF.Ln))
        s(lambda e: e.activation(t0[:, :], t1[:, :], AF.Exp, scale=np.float32(-0.5)))
        v(lambda e: e.tensor_mul(A[:, :], A[:, :], t0[:, :]), wait=[(sm.ss, sm.ss.n)])
    v(lambda e: e.tensor_mul(negMA[:, :], mu[:, :], A[:, :]))
    v(lambda e: e.tensor_scalar_mul(negMA[:, :], negMA[:, :], -1.0))


def emit_phase0(nc, ctx, sm, x_src, x_dst, w):
    es = ExitStack()
    sb = lambda name, shape, dt: es.enter_context(nc.sbuf_tensor("p0_" + name, shape, dt))

    v_tok = sb("v_tok", [128, 32 * 1024], BF16)
    bias_fm = sb("bias_fm", [128, 32], F32)
    bvr_row = sb("bvr", [1, 1024], BF16)
    bvr_f = sb("bvrf", [1, 1024], F32)
    rows = {k: sb("row_" + k, [1, 512], F32)
            for k in ("mu", "msq", "var", "t0", "t1", "A", "negMA")}
    q_scr = nc.dram_tensor("p0_q_scr", [8, 128, 4096], BF16, kind="Internal").ap()
    k_scr = nc.dram_tensor("p0_k_scr", [8, 128, 4096], BF16, kind="Internal").ap()

    es1 = ExitStack()
    sb1 = lambda name, shape, dt: es1.enter_context(nc.sbuf_tensor("p0s_" + name, shape, dt))
    wq_sb = sb1("wq", [128, 8 * 1024], BF16)
    wk_sb = sb1("wk", [128, 8 * 1024], BF16)
    wvr_sb = sb1("wvr", [128, 8 * 1024], BF16)
    es0 = ExitStack()
    sb0 = lambda name, shape, dt: es0.enter_context(nc.sbuf_tensor("p0z_" + name, shape, dt))
    wv_buf = sb0("wvbuf", [128, 2 * 4096], F32)
    wvr_f = sb0("wvrf", [128, 512], F32)
    bv_row = sb0("bv_row", [1, 8192], BF16)
    vec8 = sb0("vec8", [8, 4 * 128], F32)

    esp = ExitStack()
    psb = lambda name, shape: esp.enter_context(nc.psum_tensor("p0_" + name, shape, F32))
    ps_s = psb("s", [1, 512])
    ps_q = psb("q", [1, 512])
    ps_A = psb("A", [128, 512])
    ps_M = psb("M", [128, 512])
    ps_p0 = psb("pp0", [128, 512])
    ps_p1 = psb("pp1", [128, 512])
    ps_pp = [ps_p0, ps_p1]

    p = Plan(nc)
    p.op("gpsimd", lambda g: g.dma_start(
        wq_sb[:, :].rearrange("p (c f) -> p c f", f=1024),
        w["L0_attn_Wq"].rearrange("(c p) f -> p c f", p=128)), inc=sm.d_w, inc_k=16)
    p.op("gpsimd", lambda g: g.dma_start(
        wk_sb[:, :].rearrange("p (c f) -> p c f", f=1024),
        w["L0_attn_Wk"].rearrange("(c p) f -> p c f", p=128)), inc=sm.d_w, inc_k=16)
    for i, nm in enumerate(["L0_attn_bq", "L0_attn_bk", "L0_attn_gamma", "L0_attn_beta"]):
        p.op("gpsimd", lambda g, i=i, nm=nm: g.dma_start(
            vec8[:, i * 128:(i + 1) * 128],
            w[nm].rearrange("(a b) -> a b", b=128)), inc=sm.d_v, inc_k=16)
    p.op("gpsimd", lambda g: g.dma_start(
        bv_row[:, :], w["L0_attn_bv"].rearrange("(a b) -> a b", a=1)),
        inc=sm.d_v, inc_k=16)
    for i in range(16):
        c, hf = i // 2, i % 2
        wv_waits = [(sm.v_wvr, i - 1)] if i >= 2 else []
        p.op("sync", lambda e, c=c, hf=hf, i=i: e.dma_start(
            wv_buf[:, (i % 2) * 4096:(i % 2 + 1) * 4096],
            w["L0_attn_Wv"][c * 128:(c + 1) * 128, hf * 4096:(hf + 1) * 4096]),
            inc=sm.d_wv, inc_k=16, wait=wv_waits)
        p.op("vector", lambda e, c=c, hf=hf, i=i: e.reduce_sum(
            wvr_f[:, :].rearrange("p (r e) -> p r e", e=1),
            wv_buf[:, (i % 2) * 4096:(i % 2 + 1) * 4096].rearrange("p (r e) -> p r e", e=8),
            axis=AX.X), wait=[(sm.d_wv, 16 * (i + 1))] + [(sm.v_wvr, sm.v_wvr.n)])
        p.op("vector", lambda e, c=c, hf=hf: e.tensor_copy(
            wvr_sb[:, c * 1024 + hf * 512: c * 1024 + (hf + 1) * 512], wvr_f[:, :]),
            inc=sm.v_wvr)
    p.op("vector", lambda e: e.reduce_sum(
        bvr_f[:, :].rearrange("p (r e) -> p r e", e=1),
        bv_row[:, :].rearrange("p (r e) -> p r e", e=8), axis=AX.X),
        wait=[(sm.d_v, sm.d_v.n), (sm.v_wvr, sm.v_wvr.n)])
    p.op("vector", lambda e: e.tensor_copy(bvr_row[:, :], bvr_f[:, :]),
        inc=sm.v_wvr)
    def bias_mms(t):
        for i in range(4):
            ins = t.matmul(ps_A[:, i * 8:(i + 1) * 8], vec8[:, i * 128:(i + 1) * 128],
                           ctx.identf8[:, :], start=(i == 0), stop=(i == 3))
        return ins
    p.op("tensor", bias_mms, inc=sm.t_mm, wait=[(sm.d_v, sm.d_v.n)])
    p.op("scalar", lambda s: s.copy(bias_fm[:, :], ps_A[:, 0:32]),
         inc=sm.s_ev, wait=[(sm.t_mm, sm.t_mm.n)])
    p.run()
    es0.close()

    xblk = sb1("xblk", [128, 8 * 512], F32)
    xbf = sb1("xbf", [128, 8 * 512], BF16)
    xsq = sb1("xsq", [128, 8 * 512], BF16)
    xn0 = sb1("xn0", [128, 2 * 512], F32)
    xn_sb = sb1("xn", [128, 8 * 512], BF16)
    q_stage = sb1("qstage", [128, 8 * 512], BF16)
    k_stage = sb1("kstage", [128, 8 * 512], BF16)

    p = Plan(nc)
    t_proj_end = 0
    t_statq_end = 0
    for sbi in range(8):
        xdma_gate = [(sm.v_xn0, sm.v_xn0.n), (sm.s_sq, sm.s_sq.n),
                     (sm.v_cast, sm.v_cast.n)] if sbi else []
        for c in range(NCH):
            p.op("sync", lambda e, c=c, sbi=sbi: e.dma_start(
                xblk[:, c * 512:(c + 1) * 512],
                x_src[c * 128:(c + 1) * 128, sbi * 512:(sbi + 1) * 512]),
                inc=sm.d_x, inc_k=16, wait=xdma_gate if c == 0 else ())
        xw = [(sm.d_x, sm.d_x.n)]
        for c in range(NCH):
            p.op("vector", lambda e, c=c: e.tensor_copy(
                xbf[:, c * 512:(c + 1) * 512], xblk[:, c * 512:(c + 1) * 512]),
                inc=sm.v_cast, wait=xw if c == 0 else ())
            sqg = [(sm.t_mm, t_statq_end)] if (sbi and c < 2) else []
            p.op("scalar", lambda e, c=c: e.activation(
                xsq[:, c * 512:(c + 1) * 512], xblk[:, c * 512:(c + 1) * 512],
                AF.Square), inc=sm.s_sq, wait=(xw if c == 0 else []) + sqg)

        def stats_s(t):
            for c in range(NCH):
                ins = t.matmul(ps_s[:, :], ctx.ones_col_bf[:, :],
                               xbf[:, c * 512:(c + 1) * 512],
                               start=(c == 0), stop=(c == 7))
            return ins
        p.op("tensor", stats_s, inc=sm.t_mm,
             wait=[(sm.v_cast, sm.v_cast.n), (sm.sv, sm.sv.n)])

        def stats_q(t):
            for c in range(NCH):
                ins = t.matmul(ps_q[:, :], ctx.ones_col_bf[:, :],
                               xsq[:, c * 512:(c + 1) * 512],
                               start=(c == 0), stop=(c == 7))
            return ins
        p.op("tensor", stats_q, inc=sm.t_mm, wait=[(sm.s_sq, sm.s_sq.n)])
        t_statq_end = sm.t_mm.n

        plan_ln_rows(p, sm, ps_s, ps_q, rows, with_ln2=True,
                     pre_wait=[(sm.t_mm, sm.t_mm.n)])
        p.op("tensor", lambda t: t.matmul(ps_A[:, :], ctx.ones_1x128_f[:, :],
                                          rows["A"][:, :], start=True, stop=True),
             inc=sm.t_mm, wait=[(sm.sv, sm.sv.n), (sm.v_xn0, sm.v_xn0.n)])
        p.op("tensor", lambda t: t.matmul(ps_M[:, :], ctx.ones_1x128_f[:, :],
                                          rows["negMA"][:, :], start=True, stop=True),
             inc=sm.t_mm)
        bcw = [(sm.t_mm, sm.t_mm.n)]
        for c in range(NCH):
            b2 = c % 2
            slot_free = [(sm.s_xn, sm.s_xn.n - 1)] if sm.s_xn.n >= 1 else []
            p.op("vector", lambda e, c=c, b2=b2: e.tensor_mul(
                xn0[:, b2 * 512:(b2 + 1) * 512], xblk[:, c * 512:(c + 1) * 512],
                ps_A[:, :]), wait=(bcw if c == 0 else []) + slot_free)
            p.op("vector", lambda e, c=c, b2=b2: e.tensor_add(
                xn0[:, b2 * 512:(b2 + 1) * 512], xn0[:, b2 * 512:(b2 + 1) * 512],
                ps_M[:, :]), inc=sm.v_xn0)
            xng = [(sm.t_mm, t_proj_end)] if (sbi and c == 0) else []
            p.op("scalar", lambda e, c=c, b2=b2: e.activation(
                xn_sb[:, c * 512:(c + 1) * 512], xn0[:, b2 * 512:(b2 + 1) * 512],
                AF.Identity, bias=bias_fm[:, 24 + c:25 + c],
                scale=bias_fm[:, 16 + c:17 + c]),
                inc=sm.s_xn, wait=[(sm.v_xn0, sm.v_xn0.n)] + xng)

        bank_records = []
        xnw = [(sm.s_xn, sm.s_xn.n)]
        dq_base = sm.d_qk.n

        def group(t_build, evac_eng, evac_build, evac_sem, first=False):
            G = len(bank_records)
            waits = ([bank_records[G - 2]] if G >= 2 else []) + (xnw if first else [])
            p.op("tensor", t_build, inc=sm.t_mm, wait=waits)
            p.op(evac_eng, evac_build, inc=evac_sem, wait=[(sm.t_mm, sm.t_mm.n)])
            bank_records.append((evac_sem, evac_sem.n))

        for wsb, stage_t, bcol in ((wq_sb, q_stage, 0), (wk_sb, k_stage, 8)):
            for ft in range(8):
                G = len(bank_records)

                def t_build(t, wsb=wsb, ft=ft, G=G):
                    for c in range(NCH):
                        ins = t.matmul(
                            ps_pp[G % 2][:, :],
                            wsb[:, c * 1024 + ft * 128: c * 1024 + (ft + 1) * 128],
                            xn_sb[:, c * 512:(c + 1) * 512],
                            start=(c == 0), stop=(c == 7))
                    return ins

                def e_build(e, stage_t=stage_t, ft=ft, bcol=bcol, G=G):
                    return e.activation(
                        stage_t[:, ft * 512:(ft + 1) * 512], ps_pp[G % 2][:, :],
                        AF.Identity, bias=bias_fm[:, bcol + ft:bcol + ft + 1])
                if G == 0 and sbi:
                    xnw.append((sm.d_qk, dq_base))
                group(t_build, "scalar", e_build, sm.s_ev, first=(G == 0))
        for tt in range(4):
            for half in range(2):
                G = len(bank_records)
                ttg = sbi * 4 + tt

                def t_build(t, tt=tt, half=half, G=G):
                    for c in range(NCH):
                        t.matmul(ps_pp[G % 2][:, :],
                                 xn_sb[:, c * 512 + tt * 128: c * 512 + (tt + 1) * 128],
                                 wvr_sb[:, c * 1024 + half * 512: c * 1024 + (half + 1) * 512],
                                 start=(c == 0), stop=False)
                    return t.matmul(ps_pp[G % 2][:, :], ctx.ones_1x128_bf[:, :],
                                    bvr_row[:, half * 512:(half + 1) * 512],
                                    start=False, stop=True)

                def e_build(e, ttg=ttg, half=half, G=G):
                    return e.tensor_copy(
                        v_tok[:, ttg * 1024 + half * 512: ttg * 1024 + (half + 1) * 512],
                        ps_pp[G % 2][:, :])
                group(t_build, "vector", e_build, sm.v_ev)

        for ft in range(8):
            p.op("sync", lambda e, ft=ft, sbi=sbi: e.dma_start(
                q_scr[ft, :, sbi * 512:(sbi + 1) * 512],
                q_stage[:, ft * 512:(ft + 1) * 512]),
                inc=sm.d_qk, inc_k=16, wait=[bank_records[ft]])
            p.op("sync", lambda e, ft=ft, sbi=sbi: e.dma_start(
                k_scr[ft, :, sbi * 512:(sbi + 1) * 512],
                k_stage[:, ft * 512:(ft + 1) * 512]),
                inc=sm.d_qk, inc_k=16, wait=[bank_records[8 + ft]])
        t_proj_end = sm.t_mm.n
    p.run()

    es1.close()
    esp.close()

    esp2 = ExitStack()
    psb2 = lambda name, shape: esp2.enter_context(nc.psum_tensor("p0a_" + name, shape, F32))
    ps_sc = [psb2("scA", [128, 1024]), psb2("scB", [128, 1024])]
    ps_oo = [psb2("o0", [128, 512]), psb2("o1", [128, 512])]
    ps_sum = psb2("sum", [1, 512])
    ps_misc = psb2("misc", [128, 512])

    qh_sb = sb("qh", [128, 2 * 4096], BF16)
    kh_sb = sb("kh", [128, 2 * 4096], BF16)
    xh_sb = sb("xh", [128, 8 * 512], F32)
    xh2_sb = sb("xh2", [128, 8 * 512], F32)
    oh_sb = sb("oh", [128, 2 * 4096], F32)
    exp_sb = sb("exp", [128, 4 * 1024], BF16)
    psh_sb = sb("psh", [128, 2 * 512], F32)
    rec_bc = sb("rec_bc", [128, 512], F32)
    cm1 = sb("cm1", [128, 1], F32)
    negC = sb("negC", [1, 1], F32)
    negC_bc = sb("negC_bc", [128, 2], F32)
    sum_row = sb("sum_row", [1, 512], F32)
    rec_row = sb("rec_row", [1, 512], F32)

    n_qk_stores = sm.d_qk.n
    sc_count = [0]

    p = Plan(nc)
    for hp in range(4):
        qk_gate = [(sm.t_sc, sm.t_sc.n), (sm.t_pv, sm.t_pv.n)]
        for hh in range(2):
            h = 2 * hp + hh
            p.op("sync", lambda e, h=h, hh=hh: e.dma_start(
                qh_sb[:, hh * 4096:(hh + 1) * 4096], q_scr[h, :, :]),
                inc=sm.d_qh, inc_k=16,
                wait=([(sm.d_qk, n_qk_stores)] if h == 0 else []) +
                     (qk_gate if (hh == 0 and hp > 0) else []))
            p.op("sync", lambda e, h=h, hh=hh: e.dma_start(
                kh_sb[:, hh * 4096:(hh + 1) * 4096], k_scr[h, :, :]),
                inc=sm.d_qh, inc_k=16)
        for c in range(NCH):
            p.op("gpsimd", lambda e, c=c, hp=hp: e.dma_start(
                xh_sb[:, c * 512:(c + 1) * 512],
                x_src[c * 128:(c + 1) * 128, 2 * hp * 512:(2 * hp + 1) * 512]),
                inc=sm.d_xh, inc_k=16,
                wait=[(sm.v_res, sm.v_res.n)] if c == 0 else ())
        qkw = [(sm.d_qh, sm.d_qh.n)]
        for hh in range(2):
            corw = (qkw if hh == 0 else []) + [(sm.s_rb, sm.s_rb.n),
                                               (sm.s_ev, sm.s_ev.n)]
            p.op("tensor", lambda t, hh=hh: t.matmul(
                ps_misc[:, :], kh_sb[:, hh * 4096: hh * 4096 + 128],
                qh_sb[:, hh * 4096: hh * 4096 + 512], start=True, stop=True),
                inc=sm.t_mm, wait=corw)
            p.op("vector", lambda v: v.reduce_max(cm1[:, :], ps_misc[:, :], axis=AX.X),
                 inc=sm.sv, wait=[(sm.t_mm, sm.t_mm.n), (sm.sv, sm.sv.n)])
            p.op("tensor", lambda t: t.matmul(ps_misc[0:1, 0:128], cm1[:, :],
                                              ctx.identf[:, :], start=True, stop=True),
                 inc=sm.t_mm, wait=[(sm.sv, sm.sv.n)])
            p.op("vector", lambda v: v.reduce_max(negC[:, :], ps_misc[0:1, 0:128],
                                                  axis=AX.X, negate=True),
                 inc=sm.sv, wait=[(sm.t_mm, sm.t_mm.n), (sm.sv, sm.sv.n)])
            p.op("vector", lambda v: v.tensor_scalar_mul(negC[:, :], negC[:, :],
                                                         RSQRT_DK0),
                 inc=sm.sv, wait=[(sm.sv, sm.sv.n)])
            p.op("tensor", lambda t, hh=hh: t.matmul(
                ps_misc[:, 2 * hh:2 * hh + 1], ctx.ones_1x128_f[:, :], negC[:, :],
                start=True, stop=True),
                inc=sm.t_mm, wait=[(sm.sv, sm.sv.n)])
            p.op("scalar", lambda s, hh=hh: s.copy(negC_bc[:, hh:hh + 1],
                                                   ps_misc[:, 2 * hh:2 * hh + 1]),
                 inc=sm.s_ev, wait=[(sm.t_mm, sm.t_mm.n)])
        cready = [(sm.s_ev, sm.s_ev.n)]

        pair_resw = [(sm.v_res, sm.v_res.n)]
        psh_gate = [(sm.t_mm, sm.t_mm.n)]
        for sbi in range(8):
            sum_gate = [(sm.s_sr, sm.s_sr.n)]
            on_gates = {0: [(sm.v_on, sm.v_on.n - 1)] if sm.v_on.n >= 1 else [],
                        1: [(sm.v_on, sm.v_on.n)]}
            pending = None
            for bt in range(16):
                for hh in range(2):
                    h = 2 * hp + hh
                    j = sc_count[0]

                    def t_sc(t, bt=bt, sbi=sbi, hh=hh, j=j):
                        t.matmul(ps_sc[j % 2][:, 0:512],
                                 kh_sb[:, hh * 4096 + (2 * bt) * 128: hh * 4096 + (2 * bt + 1) * 128],
                                 qh_sb[:, hh * 4096 + sbi * 512: hh * 4096 + (sbi + 1) * 512],
                                 start=True, stop=True)
                        return t.matmul(ps_sc[j % 2][:, 512:1024],
                                        kh_sb[:, hh * 4096 + (2 * bt + 1) * 128: hh * 4096 + (2 * bt + 2) * 128],
                                        qh_sb[:, hh * 4096 + sbi * 512: hh * 4096 + (sbi + 1) * 512],
                                        start=True, stop=True)
                    scw = [(sm.s_exp, j - 1)] if j >= 2 else []
                    p.op("tensor", t_sc, inc=sm.t_sc, wait=scw)
                    sc_count[0] += 1
                    if pending is not None:
                        p.op("tensor", pending[0], inc=sm.t_pv, wait=pending[1])
                        pending = None
                    expw = [(sm.v_ps, sm.v_ps.n - 6)] if sm.v_ps.n >= 6 else []
                    p.op("scalar", lambda s, j=j, hh=hh: s.activation(
                        exp_sb[:, (j % 4) * 1024:(j % 4 + 1) * 1024], ps_sc[j % 2][:, :],
                        AF.Exp, bias=negC_bc[:, hh:hh + 1], scale=RSQRT_DK0),
                        inc=sm.s_exp,
                        wait=[(sm.t_sc, sm.t_sc.n)] + expw + (cready if (sbi == 0 and bt == 0) else []))
                    for q2 in range(2):
                        p.op("vector", lambda v, j=j, hh=hh, q2=q2, bt=bt: (
                            v.tensor_copy(psh_sb[:, hh * 512:(hh + 1) * 512],
                                          exp_sb[:, (j % 4) * 1024 + q2 * 512: (j % 4) * 1024 + (q2 + 1) * 512])
                            if (bt == 0 and q2 == 0) else
                            v.tensor_add(psh_sb[:, hh * 512:(hh + 1) * 512],
                                         psh_sb[:, hh * 512:(hh + 1) * 512],
                                         exp_sb[:, (j % 4) * 1024 + q2 * 512: (j % 4) * 1024 + (q2 + 1) * 512])),
                            inc=sm.v_ps,
                            wait=[(sm.s_exp, sm.s_exp.n)] +
                                 (psh_gate if (bt == 0 and q2 == 0) else []))

                    def t_pv(t, bt=bt, hh=hh, h=h, j=j):
                        for q2 in range(2):
                            tc = 2 * bt + q2
                            ins = t.matmul(
                                ps_oo[hh][:, :],
                                v_tok[:, tc * 1024 + h * 128: tc * 1024 + (h + 1) * 128],
                                exp_sb[:, (j % 4) * 1024 + q2 * 512: (j % 4) * 1024 + (q2 + 1) * 512],
                                start=(tc == 0), stop=(tc == 31))
                        return ins
                    pvw = [(sm.s_exp, sm.s_exp.n)] + (on_gates[hh] if bt == 0 else [])
                    pending = (t_pv, pvw)
            if pending is not None:
                p.op("tensor", pending[0], inc=sm.t_pv, wait=pending[1])
                pending = None
            for hh in range(2):
                h = 2 * hp + hh
                p.op("tensor", lambda t, hh=hh: t.matmul(
                    ps_sum[:, :], ctx.onesf_col[:, :], psh_sb[:, hh * 512:(hh + 1) * 512],
                    start=True, stop=True),
                    inc=sm.t_mm, wait=[(sm.v_ps, sm.v_ps.n)] + sum_gate)
                p.op("scalar", lambda s: s.activation(sum_row[:, :], ps_sum[:, :],
                                                      AF.Ln),
                     inc=sm.s_sr, wait=[(sm.t_mm, sm.t_mm.n)])
                p.op("scalar", lambda s: s.activation(rec_row[:, :], sum_row[:, :],
                                                      AF.Exp, scale=np.float32(-1.0)),
                     inc=sm.s_sr)
                p.op("tensor", lambda t: t.matmul(ps_misc[:, :], ctx.ones_1x128_f[:, :],
                                                  rec_row[:, :], start=True, stop=True),
                     inc=sm.t_mm, wait=[(sm.s_sr, sm.s_sr.n), (sm.s_rb, sm.s_rb.n)])
                p.op("scalar", lambda s: s.copy(rec_bc[:, :], ps_misc[:, :]),
                     inc=sm.s_rb, wait=[(sm.t_mm, sm.t_mm.n)])
                p.op("vector", lambda v, sbi=sbi, hh=hh: v.tensor_mul(
                    oh_sb[:, hh * 4096 + sbi * 512: hh * 4096 + (sbi + 1) * 512],
                    ps_oo[hh][:, :], rec_bc[:, :]),
                    inc=sm.v_on, wait=[(sm.s_rb, sm.s_rb.n)] +
                    (pair_resw if sbi == 0 else []))
                sum_gate = [(sm.s_sr, sm.s_sr.n)]
                psh_gate = [(sm.t_mm, sm.t_mm.n)]
        for hh in range(2):
            h = 2 * hp + hh
            if hh == 1:
                for c in range(NCH):
                    p.op("gpsimd", lambda e, c=c, h=h: e.dma_start(
                        xh_sb[:, c * 512:(c + 1) * 512],
                        x_src[c * 128:(c + 1) * 128, h * 512:(h + 1) * 512]),
                        inc=sm.d_xh, inc_k=16,
                        wait=[(sm.v_res, sm.v_res.n)] if c == 0 else ())
            rw = [(sm.v_on, sm.v_on.n - (1 - hh)), (sm.d_xh, sm.d_xh.n)]
            for c in range(NCH):
                p.op("vector", lambda v, c=c, hh=hh: v.tensor_add(
                    xh2_sb[:, c * 512:(c + 1) * 512], xh_sb[:, c * 512:(c + 1) * 512],
                    oh_sb[:, hh * 4096 + c: (hh + 1) * 4096: 8]),
                    inc=sm.v_res, wait=(rw if c == 0 else []) +
                    ([(sm.d_out, sm.d_out.n)] if c == 0 and hh == 1 else []))
                p.op("sync", lambda e, c=c, h=h: e.dma_start(
                    x_dst[c * 128:(c + 1) * 128, h * 512:(h + 1) * 512],
                    xh2_sb[:, c * 512:(c + 1) * 512]),
                    inc=sm.d_out, inc_k=16, wait=[(sm.v_res, sm.v_res.n)])
    p.op("sync", lambda e: e.nop(), wait=[(sm.d_out, sm.d_out.n)])
    p.run()

    es.close()
    esp2.close()


def plan_ln_block(nc, sm, ctx, p, x_src, X_sup, tb, sbq, tiles, psums,
                  xn_scr, gate_sb=None, glogc=None, glog=None, xw=None,
                  merged=False):
    xsq, xnt, rowbuf = tiles["xsq"], tiles["xnt"], tiles["rowbuf"]
    ps_s, ps_q, ps_A, ps_M, ps_pp = (psums[k] for k in ("s", "q", "A", "M", "pp"))
    row = lambda i: rowbuf[0:1, i * 512:(i + 1) * 512]
    rows = {k: row(i) for i, k in enumerate(("mu", "msq", "var", "t0", "t1", "A", "negMA"))}

    if xw is None:
        for c in range(NCH):
            p.op("sync", lambda e, c=c: e.dma_start(
                X_sup[:, c * 2048 + sbq * 512: c * 2048 + (sbq + 1) * 512],
                x_src[c * 128:(c + 1) * 128, tb * 512:(tb + 1) * 512]),
                inc=sm.d_x, inc_k=16)
        xw = [(sm.d_x, sm.d_x.n)]

    def stats_s(t):
        for c in range(NCH):
            ins = t.matmul(ps_s[:, :], ctx.onesf_col[:, :],
                           X_sup[:, c * 2048 + sbq * 512: c * 2048 + (sbq + 1) * 512],
                           start=(c == 0), stop=(c == 7))
        return ins
    p.op("tensor", stats_s, inc=sm.t_mm,
         wait=xw + ([(sm.sv, sm.sv.n)] if merged else []))
    t_statq_prev = getattr(sm, "_ln_statq_end", 0)
    for c in range(NCH):
        slot = c % 2
        sq_wait = ([(sm.t_mm, sm.t_mm.n - 1)] if c >= 2 else []) + (xw if c == 0 else [])
        if merged and c < 2:
            sq_wait = sq_wait + [(sm.t_mm, t_statq_prev)]
        p.op("scalar", lambda e, c=c, slot=slot: e.activation(
            xsq[:, slot * 512:(slot + 1) * 512],
            X_sup[:, c * 2048 + sbq * 512: c * 2048 + (sbq + 1) * 512], AF.Square),
            inc=sm.s_sq, wait=sq_wait)
        p.op("tensor", lambda t, c=c, slot=slot: t.matmul(
            ps_q[:, :], ctx.onesf_col[:, :], xsq[:, slot * 512:(slot + 1) * 512],
            start=(c == 0), stop=(c == 7)),
            inc=sm.t_mm, wait=[(sm.s_sq, sm.s_sq.n)])
    sm._ln_statq_end = sm.t_mm.n
    plan_ln_rows(p, sm, ps_s, ps_q, rows, with_ln2=False,
                 pre_wait=[(sm.t_mm, sm.t_mm.n)])
    p.op("tensor", lambda t: t.matmul(ps_A[:, :], ctx.ones_1x128_f[:, :],
                                      rows["A"], start=True, stop=True),
         inc=sm.t_mm,
         wait=[(sm.sv, sm.sv.n)] + ([(sm.v_xn0, sm.v_xn0.n),
                                     (sm.t_gm, sm.t_gm.n)] if merged else []))
    p.op("tensor", lambda t: t.matmul(ps_M[:, :], ctx.ones_1x128_f[:, :],
                                      rows["negMA"], start=True, stop=True),
         inc=sm.t_mm)
    bcw = [(sm.t_mm, sm.t_mm.n)]
    for c in range(NCH):
        slot = c % 2
        slot_free = []
        if c >= 2:
            slot_free.append((sm.d_xnw, sm.d_xnw.n - 16))
            if gate_sb is not None:
                slot_free.append((sm.t_gm, sm.t_gm.n - 4))
        p.op("vector", lambda e, c=c, slot=slot: e.tensor_mul(
            xnt[:, slot * 512:(slot + 1) * 512],
            X_sup[:, c * 2048 + sbq * 512: c * 2048 + (sbq + 1) * 512], ps_A[:, :]),
            wait=(bcw if c == 0 else []) + slot_free)
        p.op("vector", lambda e, c=c, slot=slot: e.tensor_add(
            xnt[:, slot * 512:(slot + 1) * 512], xnt[:, slot * 512:(slot + 1) * 512],
            ps_M[:, :]), inc=sm.v_xn0)
        p.op("gpsimd", lambda e, c=c, slot=slot: e.dma_start(
            xn_scr[c * 128:(c + 1) * 128, tb * 512:(tb + 1) * 512],
            xnt[:, slot * 512:(slot + 1) * 512]),
            inc=sm.d_xnw, inc_k=16, wait=[(sm.v_xn0, sm.v_xn0.n)])
        if gate_sb is not None:
            for st in range(4):
                gi = sm.t_gm.n
                gw = [(sm.v_gl, gi - 1)] if gi >= 2 else []
                p.op("tensor", lambda t, c=c, st=st, slot=slot, gi=gi: t.matmul(
                    ps_pp[gi % 2][0:128, 0:8],
                    xnt[:, slot * 512 + st * 128: slot * 512 + (st + 1) * 128],
                    gate_sb[:, c * 8:(c + 1) * 8], start=True, stop=True),
                    inc=sm.t_gm, wait=gw + [(sm.v_xn0, sm.v_xn0.n)])
                p.op("vector", lambda e, c=c, st=st, gi=gi: e.tensor_copy(
                    glogc[:, st * 64 + c * 8: st * 64 + (c + 1) * 8],
                    ps_pp[gi % 2][0:128, 0:8]),
                    inc=sm.v_gl, wait=[(sm.t_gm, sm.t_gm.n)])
    if gate_sb is not None:
        for st in range(4):
            stg = sbq * 4 + st
            p.op("vector", lambda e, st=st, stg=stg: e.reduce_sum(
                glog[:, stg * 8:(stg + 1) * 8].rearrange("p (e o) -> p e o", o=1),
                glogc[:, st * 64:(st + 1) * 64].rearrange("p (c e) -> p e c", e=8),
                axis=AX.X), inc=sm.v_gl, wait=[(sm.v_gl, sm.v_gl.n)])


def emit_phase1(nc, ctx, sm, x_src, x_dst, w):
    es = ExitStack()
    sb = lambda name, shape, dt: es.enter_context(nc.sbuf_tensor("p1_" + name, shape, dt))
    X_sup = sb("Xsup", [128, 8 * 2048], F32)
    w1_sb = sb("w1", [128, 8 * 2048], BF16)
    w2_sb = sb("w2", [128, 16 * 1024], BF16)
    h_sb = sb("h", [128, 16 * 512], BF16)
    xn_blk = sb("xnblk", [128, 2 * 8 * 512], BF16)
    xsq = sb("xsq", [128, 2 * 512], F32)
    xnt = sb("xnt", [128, 2 * 512], F32)
    rowbuf = sb("rowbuf", [1, 7 * 512], F32)
    glog = sb("glog", [128, 128], F32)
    glogc = sb("glogc", [128, 256], F32)
    ga = sb("ga", [128, 128], F32)
    gb = sb("gb", [128, 128], F32)
    gc = sb("gc", [128, 128], F32)
    gm1 = sb("gm1", [128, 16], F32)
    gm2 = sb("gm2", [128, 16], F32)
    gm3 = sb("gm3", [128, 16], F32)
    g_bf = sb("g_bf", [128, 128], BF16)
    g_sb = sb("g_sb", [8, 2048], BF16)
    bcg_sb = sb("bcg", [128, 512], BF16)
    b1e_row = sb("b1e", [1, 2 * 2048], BF16)
    b2e_row = sb("b2e", [1, 2 * 1024], BF16)
    gate_sb = sb("gate", [128, 64], F32)
    xn_scr = nc.dram_tensor("p1_xn_scr", [1024, 4096], BF16, kind="Internal").ap()

    esp = ExitStack()
    psb = lambda name, shape, dt=F32: esp.enter_context(
        nc.psum_tensor("p1_" + name, shape, dt))
    ps_s = psb("s", [1, 512])
    ps_q = psb("q", [1, 512])
    ps_A = psb("A", [128, 512])
    ps_M = psb("M", [128, 512])
    ps_p0 = psb("pp0", [128, 512])
    ps_p1 = psb("pp1", [128, 512])
    ps_pp = [ps_p0, ps_p1]
    ps_gt = psb("gt", [8, 128], BF16)
    ps_bc = psb("bc", [128, 512])
    tiles = dict(xsq=xsq, xnt=xnt, rowbuf=rowbuf)
    psums = dict(s=ps_s, q=ps_q, A=ps_A, M=ps_M, pp=ps_pp)

    p = Plan(nc)
    p.op("sync", lambda e: e.dma_start(
        gate_sb[:, :].rearrange("p (c e) -> p c e", e=8),
        w["L0_ffn_gate"].rearrange("(c p) e -> p c e", p=128)),
        inc=sm.d_g, inc_k=16)
    p.op("sync", lambda e: e.nop(), wait=[(sm.d_g, sm.d_g.n)])
    p.run()

    for u in range(2):
        xbase = sm.d_x.n
        xws = []
        pld = Plan(nc)
        for sbq in range(4):
            for c in range(NCH):
                pld.op("sync", lambda e, c=c, sbq=sbq, u=u: e.dma_start(
                    X_sup[:, c * 2048 + sbq * 512: c * 2048 + (sbq + 1) * 512],
                    x_src[c * 128:(c + 1) * 128,
                          (u * 4 + sbq) * 512:(u * 4 + sbq + 1) * 512]),
                    inc=sm.d_x, inc_k=16)
            xws.append([(sm.d_x, sm.d_x.n)])
        pld.run()
        p = Plan(nc)
        for sbq in range(4):
            plan_ln_block(nc, sm, ctx, p, x_src, X_sup, u * 4 + sbq, sbq, tiles,
                          psums, xn_scr, gate_sb, glogc, glog, xw=xws[sbq],
                          merged=True)


        def vstep(build, wait=()):
            p.op("vector", build, inc=sm.sv, wait=list(wait) + [(sm.sv, sm.sv.n)])
        g3 = lambda t: t[:, :].rearrange("p (s e) -> p s e", e=8)
        bc16 = lambda t: t[:, :].unsqueeze(2).broadcast_to([128, 16, 8])
        vstep(lambda e: e.reduce_max(gm1[:, :], g3(glog), axis=AX.X),
              wait=[(sm.v_gl, sm.v_gl.n)])
        vstep(lambda e: e.tensor_sub(g3(ga), g3(glog), bc16(gm1)))
        p.op("scalar", lambda e: e.activation(gb[:, :], ga[:, :], AF.Exp),
             inc=sm.ss, wait=[(sm.sv, sm.sv.n)])
        vstep(lambda e: e.reduce_sum(gm2[:, :], g3(gb), axis=AX.X),
              wait=[(sm.ss, sm.ss.n)])
        vstep(lambda e: e.reciprocal(gm3[:, :], gm2[:, :]))
        vstep(lambda e: e.tensor_mul(g3(ga), g3(gb), bc16(gm3)))
        vstep(lambda e: e.reduce_max(gm1[:, :], g3(ga), axis=AX.X))
        vstep(lambda e: e.tensor_tensor(g3(gb), g3(ga), bc16(gm1), ALU.is_equal))
        vstep(lambda e: e.scalar_tensor_tensor(g3(gb), g3(gb), -1e9, g3(ga),
                                               ALU.mult, ALU.add))
        vstep(lambda e: e.reduce_max(gm2[:, :], g3(gb), axis=AX.X))
        vstep(lambda e: e.tensor_tensor(g3(gb), g3(ga), bc16(gm2), ALU.is_ge))
        vstep(lambda e: e.tensor_mul(g3(ga), g3(ga), g3(gb)))
        vstep(lambda e: e.reduce_sum(gm1[:, :], g3(ga), axis=AX.X))
        vstep(lambda e: e.tensor_scalar_max(gm1[:, :], gm1[:, :], 1e-12))
        vstep(lambda e: e.reciprocal(gm2[:, :], gm1[:, :]))
        vstep(lambda e: e.tensor_mul(g3(ga), g3(ga), bc16(gm2)))
        p.op("vector", lambda e: e.tensor_copy(g_bf[:, :], ga[:, :]),
             inc=sm.v_gbf, wait=[(sm.sv, sm.sv.n)])
        for st in range(16):
            p.op("tensor", lambda t, st=st: t.transpose(
                ps_gt[:, :], g_bf[:, st * 8:(st + 1) * 8], ctx.identb[:, :]),
                inc=sm.t_tr,
                wait=[(sm.v_gbf, sm.v_gbf.n)] + ([(sm.s_tr, sm.s_tr.n)] if st else []))
            p.op("scalar", lambda e, st=st: e.copy(
                g_sb[0:8, st * 128:(st + 1) * 128], ps_gt[:, :]),
                inc=sm.s_tr, wait=[(sm.t_tr, sm.t_tr.n)])
        p.run()

        wmark = {}
        p = Plan(nc)
        for e_i in range(8):
            if u == 0 and e_i == 0:
                def ld0(g):
                    g.dma_start(w1_sb[:, :].rearrange("p (c f) -> p c f", f=2048),
                                w["L0_ffn_W1"][0].rearrange("(c p) f -> p c f", p=128)
                                ).then_inc(sm.d_wB.h, 16)
                    g.dma_start(w2_sb[:, :].rearrange("p (c f) -> p c f", f=1024),
                                w["L0_ffn_W2"][0].rearrange("(c p) f -> p c f", p=128)
                                ).then_inc(sm.d_wB.h, 16)
                    g.dma_start(b1e_row[:, 0:2048],
                                w["L0_ffn_b1"][0:1, :]).then_inc(sm.d_wB.h, 16)
                    return g.dma_start(b2e_row[:, 0:1024],
                                       w["L0_ffn_b2"][0:1, :])
                sm.d_wB.n += 48
                p.op("gpsimd", ld0, inc=sm.d_wB, inc_k=16)
            wweights = [(sm.d_wB, wmark.get((u, e_i), sm.d_wB.n))]
            for c in range(NCH):
                p.op("sync", lambda e, c=c, u=u: e.dma_start(
                    xn_blk[:, c * 512:(c + 1) * 512],
                    xn_scr[c * 128:(c + 1) * 128, u * 2048: u * 2048 + 512]),
                    inc=sm.d_xn2, inc_k=16,
                    wait=[(sm.d_xnw, sm.d_xnw.n), (sm.t_mm, sm.t_mm.n)] if c == 0 else ())
            bank_records = []
            for s_blk in range(4):
                slot = s_blk % 2
                xnw2 = [(sm.d_xn2, sm.d_xn2.n)]
                p.op("tensor", lambda t, e_i=e_i, s_blk=s_blk, u=u: t.matmul(
                    ps_bc[:, :], ctx.sel_all[:, e_i * 128:(e_i + 1) * 128],
                    g_sb[0:8, s_blk * 512:(s_blk + 1) * 512], start=True, stop=True),
                    inc=sm.t_bc, wait=[(sm.s_bc, sm.s_bc.n)])
                sbc_w = [(sm.t_bc, sm.t_bc.n), (sm.v_h, sm.v_h.n)]
                p.op("scalar", lambda e: e.copy(bcg_sb[:, :], ps_bc[:, :]),
                     inc=sm.s_bc, wait=sbc_w)
                bcg_w = [(sm.s_bc, sm.s_bc.n)]
                for ft in range(16):
                    G = len(bank_records)
                    gw = ([bank_records[G - 2]] if G >= 2 else []) +                          (wweights + xnw2 if ft == 0 else [])

                    def t_h(t, ft=ft, slot=slot, G=G, e_i=e_i):
                        for c in range(NCH):
                            t.matmul(ps_pp[G % 2][:, :],
                                     w1_sb[:, c * 2048 + ft * 128: c * 2048 + (ft + 1) * 128],
                                     xn_blk[:, slot * 4096 + c * 512: slot * 4096 + (c + 1) * 512],
                                     start=(c == 0), stop=False)
                        return t.matmul(ps_pp[G % 2][:, :],
                                        b1e_row[0:1, (e_i % 2) * 2048 + ft * 128:
                                                (e_i % 2) * 2048 + (ft + 1) * 128],
                                        ctx.ones_row_bf[:, :], start=False, stop=True)
                    p.op("tensor", t_h, inc=sm.t_mm, wait=gw)
                    hev_w = [(sm.t_mm, sm.t_mm.n)] + bcg_w
                    p.op("vector", lambda e, ft=ft, G=G: e.scalar_tensor_tensor(
                        h_sb[:, ft * 512:(ft + 1) * 512], ps_pp[G % 2][:, :], 0.0,
                        bcg_sb[:, :], ALU.max, ALU.mult),
                        inc=sm.v_h, wait=hev_w)
                    bank_records.append((sm.v_h, sm.v_h.n))
                t_after_h_last = sm.t_mm.n
                if s_blk < 3:
                    t_after_h = sm.t_mm.n
                    for c in range(NCH):
                        p.op("sync", lambda e, c=c, u=u, s_blk=s_blk, slot=slot: e.dma_start(
                            xn_blk[:, (1 - slot) * 4096 + c * 512: (1 - slot) * 4096 + (c + 1) * 512],
                            xn_scr[c * 128:(c + 1) * 128,
                                   u * 2048 + (s_blk + 1) * 512: u * 2048 + (s_blk + 2) * 512]),
                            inc=sm.d_xn2, inc_k=16,
                            wait=[(sm.t_mm, t_after_h)] if c == 0 else ())
                for dt in range(8):
                    G = len(bank_records)
                    gw = [bank_records[G - 2]] if G >= 2 else []

                    def t_o(t, dt=dt, G=G, e_i=e_i):
                        for hc in range(16):
                            t.matmul(ps_pp[G % 2][:, :],
                                     w2_sb[:, hc * 1024 + dt * 128: hc * 1024 + (dt + 1) * 128],
                                     h_sb[:, hc * 512:(hc + 1) * 512],
                                     start=(hc == 0), stop=False)
                        return t.matmul(ps_pp[G % 2][:, :],
                                        b2e_row[0:1, (e_i % 2) * 1024 + dt * 128:
                                                (e_i % 2) * 1024 + (dt + 1) * 128],
                                        bcg_sb[0:1, :], start=False, stop=True)
                    p.op("tensor", t_o, inc=sm.t_mm,
                         wait=gw + ([(sm.v_h, sm.v_h.n)] if dt == 0 else []))
                    p.op("vector", lambda e, dt=dt, s_blk=s_blk, G=G: e.tensor_add(
                        X_sup[:, dt * 2048 + s_blk * 512: dt * 2048 + (s_blk + 1) * 512],
                        ps_pp[G % 2][:, :],
                        X_sup[:, dt * 2048 + s_blk * 512: dt * 2048 + (s_blk + 1) * 512]),
                        inc=sm.v_o, wait=[(sm.t_mm, sm.t_mm.n)])
                    bank_records.append((sm.v_o, sm.v_o.n))
                if s_blk == 3 and not (u == 1 and e_i == 7):
                    ne = (e_i + 1) % 8
                    nu = u if e_i < 7 else 1
                    t_now = sm.t_mm.n
                    p.op("gpsimd", lambda g, ne=ne, nu=nu: g.dma_start(
                        w1_sb[:, :].rearrange("p (c f) -> p c f", f=2048),
                        w["L0_ffn_W1"][ne].rearrange("(c p) f -> p c f", p=128)),
                        inc=sm.d_wB, inc_k=16, wait=[(sm.t_mm, t_after_h_last)])
                    p.op("gpsimd", lambda g, ne=ne: g.dma_start(
                        b1e_row[:, (ne % 2) * 2048:(ne % 2) * 2048 + 2048],
                        w["L0_ffn_b1"][ne:ne + 1, :]), inc=sm.d_wB, inc_k=16)
                    p.op("gpsimd", lambda g, ne=ne: g.dma_start(
                        b2e_row[:, (ne % 2) * 1024:(ne % 2) * 1024 + 1024],
                        w["L0_ffn_b2"][ne:ne + 1, :]), inc=sm.d_wB, inc_k=16)
                    p.op("gpsimd", lambda g, ne=ne: g.dma_start(
                        w2_sb[:, :].rearrange("p (c f) -> p c f", f=1024),
                        w["L0_ffn_W2"][ne].rearrange("(c p) f -> p c f", p=128)),
                        inc=sm.d_wB, inc_k=16, wait=[(sm.t_mm, t_now)])
                    wmark[(nu, ne)] = sm.d_wB.n
        for c in range(NCH):
            p.op("sync", lambda e, c=c, u=u: e.dma_start(
                x_dst[c * 128:(c + 1) * 128, u * 2048:(u + 1) * 2048],
                X_sup[:, c * 2048:(c + 1) * 2048]),
                inc=sm.d_out, inc_k=16,
                wait=[(sm.v_o, sm.v_o.n)] if c == 0 else ())
        p.op("sync", lambda e: e.nop(), wait=[(sm.d_out, sm.d_out.n)])
        p.run()

    es.close()
    esp.close()


def emit_phase2(nc, ctx, sm, x_src, x_dst, w):
    es = ExitStack()
    sb = lambda name, shape, dt: es.enter_context(nc.sbuf_tensor("p2_" + name, shape, dt))
    v_tok = sb("v_tok", [128, 32 * 1040], BF16)
    cmask = sb("cmask", [128, 4 * 512], BF16)
    selhalf = sb("selhalf", [128, 2], BF16)
    self32 = sb("self32", [128, 2], F32)
    halfsel = sb("halfsel", [2, 128], F32)
    qn_scr = nc.dram_tensor("p2_qn_scr", [1024, 4096], BF16, kind="Internal").ap()
    kn_scr = nc.dram_tensor("p2_kn_scr", [1024, 4096], BF16, kind="Internal").ap()

    p = Plan(nc)

    def g_fn(g):
        g.memset(v_tok[:, :], 1.0)
        for a in range(4):
            mv = cmask[:, a * 512:(a + 1) * 512]
            g.memset(mv, 1.0)
            g.affine_select(mv, mv, [[1, 512]], ALU.is_ge, 0.0,
                            base=-128 * a, channel_multiplier=-1)
        g.memset(selhalf[:, :], 0.0)
        g.memset(selhalf[0:64, 0:1], 1.0)
        g.memset(selhalf[64:128, 1:2], 1.0)
        g.memset(self32[:, :], 0.0)
        g.memset(self32[0:64, 0:1], 1.0)
        g.memset(self32[64:128, 1:2], 1.0)
        return g.nop()
    p.op("gpsimd", g_fn)
    p.run()

    es2 = ExitStack()
    sb2 = lambda name, shape, dt: es2.enter_context(nc.sbuf_tensor("p2p_" + name, shape, dt))
    wq_sb = sb2("wq", [128, 8 * 1024], BF16)
    wk_sb = sb2("wk", [128, 8 * 1024], BF16)
    wv_sb = sb2("wv", [128, 8 * 1024], BF16)
    rowbuf = sb2("rowbuf", [1, 7 * 512], F32)
    xblk = sb2("xblk", [128, 8 * 512], F32)
    xn_blk = sb2("xn", [128, 8 * 512], BF16)
    xsq = sb2("xsq", [128, 2 * 512], F32)
    xnt = sb2("xnt", [128, 2 * 512], F32)
    q_full = sb2("qfull", [128, 8 * 512], BF16)
    sq_full = sb2("sqfull", [128, 8 * 512], BF16)
    sr2 = sb2("sr2", [2, 2 * 512], F32)
    rn2 = sb2("rn2", [2, 2 * 512], F32)

    esp = ExitStack()
    psb = lambda name, shape, dt=F32: esp.enter_context(
        nc.psum_tensor("p2_" + name, shape, dt))
    ps_s = psb("s", [1, 512])
    ps_q = psb("q", [1, 512])
    ps_A = psb("A", [128, 512])
    ps_M = psb("M", [128, 512])
    ps_p0 = psb("pp0", [128, 512])
    ps_p1 = psb("pp1", [128, 512])
    ps_pp = [ps_p0, ps_p1]
    ps_ss = psb("ss", [2, 512])
    ps_fac = psb("fac", [128, 512])

    p = Plan(nc)
    for nm, dst in (("L1_attn_Wq", wq_sb), ("L1_attn_Wk", wk_sb), ("L1_attn_Wv", wv_sb)):
        p.op("gpsimd", lambda g, nm=nm, dst=dst: g.dma_start(
            dst[:, :].rearrange("p (c f) -> p c f", f=1024),
            w[nm].rearrange("(c p) f -> p c f", p=128)), inc=sm.d_w, inc_k=16)
    p.op("tensor", lambda t: t.transpose(ps_ss[0:2, 0:128], self32[:, :],
                                         ctx.identf[:, :]), inc=sm.t_mm)
    p.op("scalar", lambda e: e.copy(halfsel[:, :], ps_ss[0:2, 0:128]),
         inc=sm.s_ev, wait=[(sm.t_mm, sm.t_mm.n)])
    p.run()

    row = lambda i: rowbuf[0:1, i * 512:(i + 1) * 512]
    rows = {k: row(i) for i, k in enumerate(
        ("mu", "msq", "var", "t0", "t1", "A", "negMA"))}

    p = Plan(nc)
    t_vproj_end = 0
    t_statq_end = 0
    for sbi in range(8):
        xdma_gate = [(sm.v_xn0, sm.v_xn0.n), (sm.s_sq, sm.s_sq.n),
                     (sm.t_mm, t_statq_end)] if sbi else []
        for c in range(NCH):
            p.op("sync", lambda e, c=c, sbi=sbi: e.dma_start(
                xblk[:, c * 512:(c + 1) * 512],
                x_src[c * 128:(c + 1) * 128, sbi * 512:(sbi + 1) * 512]),
                inc=sm.d_x, inc_k=16, wait=xdma_gate if c == 0 else ())
        xw = [(sm.d_x, sm.d_x.n)]

        def stats_s(t):
            for c in range(NCH):
                ins = t.matmul(ps_s[:, :], ctx.onesf_col[:, :],
                               xblk[:, c * 512:(c + 1) * 512],
                               start=(c == 0), stop=(c == 7))
            return ins
        p.op("tensor", stats_s, inc=sm.t_mm, wait=xw + [(sm.sv, sm.sv.n)])
        for c in range(NCH):
            slot = c % 2
            sq_wait = ([(sm.t_mm, sm.t_mm.n - 1)] if c >= 2 else []) + (xw if c == 0 else [])
            if sbi and c < 2:
                sq_wait = sq_wait + [(sm.t_mm, t_statq_end)]
            p.op("scalar", lambda e, c=c, slot=slot: e.activation(
                xsq[:, slot * 512:(slot + 1) * 512], xblk[:, c * 512:(c + 1) * 512],
                AF.Square), inc=sm.s_sq, wait=sq_wait)
            p.op("tensor", lambda t, c=c, slot=slot: t.matmul(
                ps_q[:, :], ctx.onesf_col[:, :], xsq[:, slot * 512:(slot + 1) * 512],
                start=(c == 0), stop=(c == 7)),
                inc=sm.t_mm, wait=[(sm.s_sq, sm.s_sq.n)])
        t_statq_end = sm.t_mm.n
        plan_ln_rows(p, sm, ps_s, ps_q, rows, with_ln2=False,
                     pre_wait=[(sm.t_mm, sm.t_mm.n)])
        p.op("tensor", lambda t: t.matmul(ps_A[:, :], ctx.ones_1x128_f[:, :],
                                          rows["A"], start=True, stop=True),
             inc=sm.t_mm, wait=[(sm.sv, sm.sv.n), (sm.v_xn0, sm.v_xn0.n)])
        p.op("tensor", lambda t: t.matmul(ps_M[:, :], ctx.ones_1x128_f[:, :],
                                          rows["negMA"], start=True, stop=True),
             inc=sm.t_mm)
        bcw = [(sm.t_mm, sm.t_mm.n)]
        for c in range(NCH):
            slot = c % 2
            slot_free = [(sm.s_xn, sm.s_xn.n - 1)] if sm.s_xn.n >= 1 else []
            p.op("vector", lambda e, c=c, slot=slot: e.tensor_mul(
                xnt[:, slot * 512:(slot + 1) * 512], xblk[:, c * 512:(c + 1) * 512],
                ps_A[:, :]), wait=(bcw if c == 0 else []) + slot_free)
            p.op("vector", lambda e, c=c, slot=slot: e.tensor_add(
                xnt[:, slot * 512:(slot + 1) * 512], xnt[:, slot * 512:(slot + 1) * 512],
                ps_M[:, :]), inc=sm.v_xn0)
            xng = [(sm.t_mm, t_vproj_end)] if (sbi and c == 0) else []
            p.op("scalar", lambda e, c=c, slot=slot: e.activation(
                xn_blk[:, c * 512:(c + 1) * 512], xnt[:, slot * 512:(slot + 1) * 512],
                AF.Copy), inc=sm.s_xn, wait=[(sm.v_xn0, sm.v_xn0.n)] + xng)
        xnw = [(sm.s_xn, sm.s_xn.n)]

        for wsb, scr in ((wq_sb, qn_scr), (wk_sb, kn_scr)):
            stage_a = []
            stage_b = []

            def plan_chain_a(fc):
                sl = fc % 2
                p.op("tensor", lambda t, fc=fc: t.matmul(
                    ps_ss[:, :], selhalf[:, :], sq_full[:, fc * 512:(fc + 1) * 512],
                    start=True, stop=True),
                    inc=sm.t_ss, wait=[(sm.sv, sm.sv.n), (sm.s_ev, sm.s_ev.n)])
                p.op("vector", lambda e, sl=sl: e.tensor_scalar_max(
                    sr2[:, sl * 512:(sl + 1) * 512], ps_ss[:, :], 1e-24),
                    inc=sm.sv, wait=[(sm.t_ss, sm.t_ss.n), (sm.sv, sm.sv.n)])
                p.op("scalar", lambda e, sl=sl: e.activation(
                    rn2[:, sl * 512:(sl + 1) * 512], sr2[:, sl * 512:(sl + 1) * 512],
                    AF.Ln), inc=sm.ss, wait=[(sm.sv, sm.sv.n)])
                p.op("scalar", lambda e, sl=sl: e.activation(
                    rn2[:, sl * 512:(sl + 1) * 512], rn2[:, sl * 512:(sl + 1) * 512],
                    AF.Exp, scale=np.float32(-0.5)), inc=sm.ss)

            def plan_chain_b(fc, scr, sbi):
                sl = fc % 2
                p.op("tensor", lambda t, sl=sl: t.matmul(
                    ps_fac[:, :], halfsel[:, :], rn2[:, sl * 512:(sl + 1) * 512],
                    start=True, stop=True),
                    inc=sm.t_fac, wait=[(sm.ss, sm.ss.n), (sm.v_ev, sm.v_ev.n)])
                p.op("vector", lambda e, fc=fc: e.tensor_mul(
                    q_full[:, fc * 512:(fc + 1) * 512],
                    q_full[:, fc * 512:(fc + 1) * 512], ps_fac[:, :]),
                    inc=sm.v_ev, wait=[(sm.t_fac, sm.t_fac.n)])
                p.op("sync", lambda e, fc=fc, sbi=sbi, scr=scr: e.dma_start(
                    scr[fc * 128:(fc + 1) * 128, sbi * 512:(sbi + 1) * 512],
                    q_full[:, fc * 512:(fc + 1) * 512]),
                    inc=sm.d_qk, inc_k=16, wait=[(sm.v_ev, sm.v_ev.n)])

            for fc in range(8):
                pj = fc % 2

                def t_proj(t, wsb=wsb, fc=fc, pj=pj):
                    for c in range(NCH):
                        ins = t.matmul(
                            ps_pp[pj][:, :],
                            wsb[:, c * 1024 + fc * 128: c * 1024 + (fc + 1) * 128],
                            xn_blk[:, c * 512:(c + 1) * 512],
                            start=(c == 0), stop=(c == 7))
                    return ins
                pw = [(sm.s_ev, sm.s_ev.n - 2)] if fc >= 2 else []
                p.op("tensor", t_proj, inc=sm.t_mm,
                     wait=pw + (xnw if fc == 0 else []))
                mmw = [(sm.t_mm, sm.t_mm.n)]
                dmaw = [(sm.d_qk, sm.d_qk.n - 32)] if sm.d_qk.n >= 32 else []
                p.op("scalar", lambda e, fc=fc, pj=pj: e.copy(
                    q_full[:, fc * 512:(fc + 1) * 512], ps_pp[pj][:, :]),
                    inc=sm.s_ev, wait=mmw + dmaw)
                p.op("scalar", lambda e, fc=fc, pj=pj: e.activation(
                    sq_full[:, fc * 512:(fc + 1) * 512], ps_pp[pj][:, :],
                    AF.Square), inc=sm.s_ev)
                stage_a.append(fc)
                if len(stage_a) >= 2:
                    plan_chain_a(stage_a.pop(0))
                    stage_b.append(fc - 1)
                if len(stage_b) >= 2:
                    plan_chain_b(stage_b.pop(0), scr, sbi)
            while stage_a:
                fca = stage_a.pop(0)
                plan_chain_a(fca)
                stage_b.append(fca)
            while stage_b:
                plan_chain_b(stage_b.pop(0), scr, sbi)
        vrec = []
        for tt in range(4):
            for half in range(2):
                gi = len(vrec)
                ttg = sbi * 4 + tt

                def t_v(t, tt=tt, half=half, gi=gi):
                    for c in range(NCH):
                        ins = t.matmul(
                            ps_pp[gi % 2][:, :],
                            xn_blk[:, c * 512 + tt * 128: c * 512 + (tt + 1) * 128],
                            wv_sb[:, c * 1024 + half * 512: c * 1024 + (half + 1) * 512],
                            start=(c == 0), stop=(c == 7))
                    return ins
                vw = [vrec[gi - 2]] if gi >= 2 else [(sm.v_ev, sm.v_ev.n)]
                p.op("tensor", t_v, inc=sm.t_mm, wait=vw)
                base = ttg * 1040 + half * 8 * 65

                def e_v(e, base=base, gi=gi):
                    return e.tensor_copy(
                        v_tok[:, base: base + 520].rearrange("p (h j) -> p h j", j=65)[:, :, 0:64],
                        ps_pp[gi % 2][:, :].rearrange("p (h j) -> p h j", j=64))
                p.op("vector", e_v, inc=sm.v_ev, wait=[(sm.t_mm, sm.t_mm.n)])
                vrec.append((sm.v_ev, sm.v_ev.n))
        t_vproj_end = sm.t_mm.n
    p.run()
    es2.close()
    esp.close()

    es3 = ExitStack()
    sb3 = lambda name, shape, dt: es3.enter_context(nc.sbuf_tensor("p2a_" + name, shape, dt))
    obuf = sb3("obuf", [128, 8 * 4096], BF16)
    es3a = ExitStack()
    sb3a = lambda name, shape, dt: es3a.enter_context(nc.sbuf_tensor("p2a2_" + name, shape, dt))
    qh2 = sb3a("qh2", [64, 2 * 4096], BF16)
    kh2 = sb3a("kh2", [64, 2 * 4096], BF16)
    exp_sb = sb3a("exp", [128, 2 * 1024], BF16)
    sum_row = sb3a("sumrow", [1, 512], F32)
    rec_row = sb3a("recrow", [1, 512], F32)
    fac_sb = sb3a("fac", [64, 512], F32)

    esp3 = ExitStack()
    psb3 = lambda name, shape, dt=F32: esp3.enter_context(
        nc.psum_tensor("p2a_" + name, shape, dt))
    ps_sc = [psb3("scA", [128, 1024]), psb3("scB", [128, 1024])]
    ps_oo = [psb3("o0", [65, 512]), psb3("o1", [65, 512])]
    ps_f2 = psb3("f2", [64, 512])

    n_qk = sm.d_qk.n
    jglob = [0]
    p = Plan(nc)
    for hp in range(8):
        qk_gate = [(sm.t_sc, sm.t_sc.n), (sm.t_pv, sm.t_pv.n)]
        for hh in range(2):
            h = 2 * hp + hh
            p.op("sync", lambda e, h=h, hh=hh: e.dma_start(
                qh2[:, hh * 4096:(hh + 1) * 4096], qn_scr[h * 64:(h + 1) * 64, :]),
                inc=sm.d_qh, inc_k=16,
                wait=([(sm.d_qk, n_qk)] if h == 0 else []) +
                     (qk_gate if (hh == 0 and hp > 0) else []))
            p.op("sync", lambda e, h=h, hh=hh: e.dma_start(
                kh2[:, hh * 4096:(hh + 1) * 4096], kn_scr[h * 64:(h + 1) * 64, :]),
                inc=sm.d_qh, inc_k=16)
        qkw = [(sm.d_qh, sm.d_qh.n)]
        for ib in range(8):
            njs = 4 * ib + 4
            on_gates = {0: [(sm.v_on, sm.v_on.n - 1)] if sm.v_on.n >= 1 else [],
                        1: [(sm.v_on, sm.v_on.n)]}
            pending = None
            j0_list = [4 * ib, 4 * ib + 2] + list(range(0, 4 * ib, 2))
            for bt in range(njs // 2):
                for hh in range(2):
                    h = 2 * hp + hh
                    j = jglob[0]
                    J0 = j0_list[bt]
                    scw = ([(sm.s_exp, j - 1)] if j >= 2 else []) + \
                        (qkw if (ib == 0 and bt == 0 and hh == 0) else [])

                    def t_sc(t, J0=J0, ib=ib, j=j, hh=hh):
                        t.matmul(ps_sc[j % 2][:, 0:512],
                                 kh2[:, hh * 4096 + J0 * 128: hh * 4096 + (J0 + 1) * 128],
                                 qh2[:, hh * 4096 + ib * 512: hh * 4096 + (ib + 1) * 512],
                                 start=True, stop=True)
                        return t.matmul(
                            ps_sc[j % 2][:, 512:1024],
                            kh2[:, hh * 4096 + (J0 + 1) * 128: hh * 4096 + (J0 + 2) * 128],
                            qh2[:, hh * 4096 + ib * 512: hh * 4096 + (ib + 1) * 512],
                            start=True, stop=True)
                    p.op("tensor", t_sc, inc=sm.t_sc, wait=scw)
                    jglob[0] += 1
                    if pending is not None:
                        p.op("tensor", pending[0], inc=sm.t_pv, wait=pending[1])
                        pending = None
                    p.op("scalar", lambda s, j=j: s.activation(
                        exp_sb[:, (j % 2) * 1024:(j % 2 + 1) * 1024], ps_sc[j % 2][:, :],
                        AF.Exp, bias=np.float32(-8.0), scale=np.float32(8.0)),
                        inc=sm.s_exp, wait=[(sm.t_sc, sm.t_sc.n)])
                    pvw = [(sm.s_exp, sm.s_exp.n)]
                    for q2 in range(2):
                        J = J0 + q2
                        if J >= 4 * ib:
                            a2 = J - 4 * ib
                            p.op("vector", lambda v, j=j, a2=a2, q2=q2: v.tensor_mul(
                                exp_sb[:, (j % 2) * 1024 + q2 * 512: (j % 2) * 1024 + (q2 + 1) * 512],
                                exp_sb[:, (j % 2) * 1024 + q2 * 512: (j % 2) * 1024 + (q2 + 1) * 512],
                                cmask[:, a2 * 512:(a2 + 1) * 512]),
                                inc=sm.v_mk, wait=[(sm.s_exp, sm.s_exp.n)])
                            pvw = [(sm.v_mk, sm.v_mk.n)]

                    lastbt = njs // 2 - 1

                    def t_pv(t, J0=J0, h=h, j=j, bt=bt, lastbt=lastbt, hh=hh):
                        for q2 in range(2):
                            J = J0 + q2
                            ins = t.matmul(
                                ps_oo[hh][:, :],
                                v_tok[:, J * 1040 + h * 65: J * 1040 + (h + 1) * 65],
                                exp_sb[:, (j % 2) * 1024 + q2 * 512: (j % 2) * 1024 + (q2 + 1) * 512],
                                start=(bt == 0 and q2 == 0), stop=(bt == lastbt and q2 == 1))
                        return ins
                    pending = (t_pv, pvw + (on_gates[hh] if bt == 0 else []))
            if pending is not None:
                p.op("tensor", pending[0], inc=sm.t_pv, wait=pending[1])
                pending = None
            for hh in range(2):
                h = 2 * hp + hh
                p.op("scalar", lambda s, hh=hh: s.activation(
                    sum_row[:, :], ps_oo[hh][64:65, :], AF.Ln),
                    inc=sm.s_sr, wait=[(sm.t_pv, sm.t_pv.n)])
                p.op("scalar", lambda s: s.activation(rec_row[:, :], sum_row[:, :],
                                                      AF.Exp, scale=np.float32(-1.0)),
                     inc=sm.s_sr)
                p.op("tensor", lambda t: t.matmul(ps_f2[:, :], ctx.ones_1x128_f[:, 0:64],
                                                  rec_row[:, :], start=True, stop=True),
                     inc=sm.t_fac, wait=[(sm.s_sr, sm.s_sr.n), (sm.s_rb, sm.s_rb.n)])
                p.op("scalar", lambda s: s.copy(fac_sb[:, :], ps_f2[:, :]),
                     inc=sm.s_rb, wait=[(sm.t_fac, sm.t_fac.n)])
                c0 = h // 2
                r0 = (h % 2) * 64
                p.op("vector", lambda v, c0=c0, r0=r0, ib=ib, hh=hh: v.tensor_mul(
                    obuf[r0:r0 + 64, c0 * 4096 + ib * 512: c0 * 4096 + (ib + 1) * 512],
                    ps_oo[hh][0:64, :], fac_sb[:, :]),
                    inc=sm.v_on, wait=[(sm.s_rb, sm.s_rb.n)])
    p.run()

    es3a.close()
    es3b = ExitStack()
    sb3b = lambda name, shape, dt: es3b.enter_context(nc.sbuf_tensor("p2b_" + name, shape, dt))
    wo_sb = sb3b("wo", [128, 8 * 1024], BF16)
    xblk2 = sb3b("xblk2", [128, 8 * 512], F32)
    p = Plan(nc)
    p.op("gpsimd", lambda g: g.dma_start(
        wo_sb[:, :].rearrange("p (c f) -> p c f", f=1024),
        w["L1_attn_Wo"].rearrange("(c p) f -> p c f", p=128)), inc=sm.d_w, inc_k=16)
    p.run()
    esp3.close()
    esp4 = ExitStack()
    psb4 = lambda name, shape, dt=F32: esp4.enter_context(
        nc.psum_tensor("p2e_" + name, shape, dt))
    ps_ee = [psb4("e0", [128, 512]), psb4("e1", [128, 512])]
    ow = [(sm.v_on, sm.v_on.n), (sm.d_w, sm.d_w.n)]
    erec = []
    for sbi in range(8):
        p = Plan(nc)
        for c in range(NCH):
            p.op("sync", lambda e, c=c, sbi=sbi: e.dma_start(
                xblk2[:, c * 512:(c + 1) * 512],
                x_src[c * 128:(c + 1) * 128, sbi * 512:(sbi + 1) * 512]),
                inc=sm.d_x, inc_k=16)
        xw2 = [(sm.d_x, sm.d_x.n)]
        for dt in range(8):
            gi = len(erec)

            def t_e(t, dt=dt, sbi=sbi, gi=gi):
                for c in range(NCH):
                    ins = t.matmul(
                        ps_ee[gi % 2][:, :],
                        wo_sb[:, c * 1024 + dt * 128: c * 1024 + (dt + 1) * 128],
                        obuf[:, c * 4096 + sbi * 512: c * 4096 + (sbi + 1) * 512],
                        start=(c == 0), stop=(c == 7))
                return ins
            ew = ([erec[gi - 2]] if gi >= 2 else []) + (ow if gi == 0 else [])
            p.op("tensor", t_e, inc=sm.t_mm, wait=ew)
            p.op("vector", lambda v, dt=dt, gi=gi: v.tensor_add(
                xblk2[:, dt * 512:(dt + 1) * 512], ps_ee[gi % 2][:, :],
                xblk2[:, dt * 512:(dt + 1) * 512]),
                inc=sm.v_res, wait=[(sm.t_mm, sm.t_mm.n)] + (xw2 if dt == 0 else []))
            erec.append((sm.v_res, sm.v_res.n))
            p.op("sync", lambda e, dt=dt, sbi=sbi: e.dma_start(
                x_dst[dt * 128:(dt + 1) * 128, sbi * 512:(sbi + 1) * 512],
                xblk2[:, dt * 512:(dt + 1) * 512]),
                inc=sm.d_out, inc_k=16, wait=[(sm.v_res, sm.v_res.n)])
        p.run()
    p = Plan(nc)
    p.op("sync", lambda e: e.nop(), wait=[(sm.d_out, sm.d_out.n)])
    p.run()
    es3b.close()
    es3.close()
    esp4.close()
    es.close()


def emit_phase3(nc, ctx, sm, x_src, x_dst, w):
    GC = 1.0 / math.e
    GL = 0.5 - math.log(4.0 / 3.0)
    GU = 0.5
    es = ExitStack()
    sb = lambda name, shape, dt: es.enter_context(nc.sbuf_tensor("p3_" + name, shape, dt))
    X_sup = sb("Xsup", [128, 8 * 2048], F32)
    wg_sb = sb("wg", [128, 8 * 1024], BF16)
    wu_sb = sb("wu", [128, 8 * 1024], BF16)
    wd_sb = sb("wd", [128, 8 * 1024], BF16)
    h_sb = sb("h", [128, 8 * 512], BF16)
    silu_sb = sb("silu", [128, 2 * 512], F32)
    xn_blk = sb("xnblk", [128, 2 * 8 * 512], BF16)
    xsq = sb("xsq", [128, 2 * 512], F32)
    xnt = sb("xnt", [128, 2 * 512], F32)
    rowbuf = sb("rowbuf", [1, 7 * 512], F32)
    glog = sb("glog", [128, 128], F32)
    glogc = sb("glogc", [128, 256], F32)
    ga = sb("ga", [128, 128], F32)
    gb = sb("gb", [128, 128], F32)
    gc_t = sb("gc", [128, 128], F32)
    gd = sb("gd", [128, 128], F32)
    gzn = sb("gzn", [128, 128], F32)
    gm1 = sb("gm1", [128, 16], F32)
    gm2 = sb("gm2", [128, 16], F32)
    gm3 = sb("gm3", [128, 16], F32)
    gne = sb("gne", [128, 16], F32)
    g_bf = sb("g_bf", [128, 128], BF16)
    w_sb = sb("w_sb", [8, 2048], BF16)
    bcw_sb = sb("bcw", [128, 512], BF16)
    wr_sb = sb("wr", [128, 64], F32)
    temp_sb = sb("temp", [1, 1], F32)
    invt_bc = sb("invt", [128, 1], F32)
    xn_scr = nc.dram_tensor("p3_xn_scr", [1024, 4096], BF16, kind="Internal").ap()

    esp = ExitStack()
    psb = lambda name, shape, dt=F32: esp.enter_context(
        nc.psum_tensor("p3_" + name, shape, dt))
    ps_s = psb("s", [1, 512])
    ps_q = psb("q", [1, 512])
    ps_A = psb("A", [128, 512])
    ps_M = psb("M", [128, 512])
    ps_p0 = psb("pp0", [128, 512])
    ps_p1 = psb("pp1", [128, 512])
    ps_pp = [ps_p0, ps_p1]
    ps_gt = psb("gt", [8, 128], BF16)
    ps_bc = psb("bc", [128, 512])
    tiles = dict(xsq=xsq, xnt=xnt, rowbuf=rowbuf)
    psums = dict(s=ps_s, q=ps_q, A=ps_A, M=ps_M, pp=ps_pp)

    p = Plan(nc)
    p.op("sync", lambda e: e.dma_start(
        wr_sb[:, :].rearrange("p (c e) -> p c e", e=8),
        w["L1_ffn_Wr"].rearrange("(c p) e -> p c e", p=128)), inc=sm.d_g, inc_k=16)
    p.op("sync", lambda e: e.dma_start(
        temp_sb[:, :], w["L1_ffn_temperature"].rearrange("(a b) -> a b", a=1)),
        inc=sm.d_g, inc_k=16)
    p.op("tensor", lambda t: t.matmul(ps_bc[:, 0:1], ctx.ones_1x128_f[:, :],
                                      temp_sb[:, :], start=True, stop=True),
         inc=sm.t_mm, wait=[(sm.d_g, sm.d_g.n)])
    p.op("scalar", lambda e: e.copy(invt_bc[:, :], ps_bc[:, 0:1]),
         inc=sm.s_ev, wait=[(sm.t_mm, sm.t_mm.n)])
    p.op("vector", lambda e: e.reciprocal(invt_bc[:, :], invt_bc[:, :]),
         inc=sm.sv, wait=[(sm.s_ev, sm.s_ev.n)])
    p.run()

    for u in range(2):
        xws = []
        pld = Plan(nc)
        for sbq in range(4):
            for c in range(NCH):
                pld.op("sync", lambda e, c=c, sbq=sbq, u=u: e.dma_start(
                    X_sup[:, c * 2048 + sbq * 512: c * 2048 + (sbq + 1) * 512],
                    x_src[c * 128:(c + 1) * 128,
                          (u * 4 + sbq) * 512:(u * 4 + sbq + 1) * 512]),
                    inc=sm.d_x, inc_k=16)
            xws.append([(sm.d_x, sm.d_x.n)])
        pld.run()
        p = Plan(nc)
        for sbq in range(4):
            plan_ln_block(nc, sm, ctx, p, x_src, X_sup, u * 4 + sbq, sbq, tiles,
                          psums, xn_scr, wr_sb, glogc, glog, xw=xws[sbq],
                          merged=True)


        def vstep(build, wait=()):
            p.op("vector", build, inc=sm.sv, wait=list(wait) + [(sm.sv, sm.sv.n)])

        def sstep(build):
            p.op("scalar", build, inc=sm.ss, wait=[(sm.sv, sm.sv.n), (sm.ss, sm.ss.n)])
        g3 = lambda t: t[:, :].rearrange("p (s e) -> p s e", e=8)
        bc16 = lambda t: t[:, :].unsqueeze(2).broadcast_to([128, 16, 8])
        p.op("scalar", lambda e: e.activation(ga[:, :], glog[:, :], AF.Sigmoid,
                                              scale=invt_bc[:, :]),
             inc=sm.ss, wait=[(sm.v_gl, sm.v_gl.n), (sm.sv, sm.sv.n)])
        p.op("scalar", lambda e: e.activation(gb[:, :], ga[:, :], AF.Abs,
                                              bias=np.float32(-GC)), inc=sm.ss)
        vstep(lambda e: e.tensor_single_scalar(gzn[:, :], ga[:, :],
                                               float(GL), ALU.is_ge),
              wait=[(sm.ss, sm.ss.n)])
        vstep(lambda e: e.tensor_single_scalar(gc_t[:, :], ga[:, :],
                                               float(GU), ALU.is_le))
        vstep(lambda e: e.tensor_mul(gzn[:, :], gzn[:, :], gc_t[:, :]))
        sstep(lambda e: e.activation(gc_t[:, :], gb[:, :], AF.Exp,
                                     scale=np.float32(-10.0)))
        vstep(lambda e: e.tensor_mul(gc_t[:, :], gc_t[:, :], gzn[:, :]),
              wait=[(sm.ss, sm.ss.n)])
        vstep(lambda e: e.reduce_sum(gm1[:, :], g3(gc_t), axis=AX.X))
        vstep(lambda e: e.tensor_single_scalar(gne[:, :], gm1[:, :],
                                               1e-8, ALU.is_lt))
        sstep(lambda e: e.activation(gd[:, :], gb[:, :], AF.Exp,
                                     scale=np.float32(-10.0 / 3.0)))
        vstep(lambda e: e.reduce_max(gm1[:, :], g3(gd), axis=AX.X),
              wait=[(sm.ss, sm.ss.n)])
        vstep(lambda e: e.tensor_tensor(g3(gb), g3(gd), bc16(gm1), ALU.is_equal))
        vstep(lambda e: e.scalar_tensor_tensor(g3(gb), g3(gb), -1e9, g3(gd),
                                               ALU.mult, ALU.add))
        vstep(lambda e: e.reduce_max(gm2[:, :], g3(gb), axis=AX.X))
        vstep(lambda e: e.tensor_tensor(g3(gb), g3(gd), bc16(gm2), ALU.is_ge))
        vstep(lambda e: e.tensor_mul(gb[:, :], gb[:, :], gd[:, :]))
        vstep(lambda e: e.reduce_sum(gm3[:, :], g3(gb), axis=AX.X))
        vstep(lambda e: e.tensor_scalar_max(gm3[:, :], gm3[:, :], 1e-8))
        vstep(lambda e: e.reciprocal(gm3[:, :], gm3[:, :]))
        vstep(lambda e: e.tensor_mul(g3(gb), g3(gb), bc16(gm3)))
        vstep(lambda e: e.tensor_sub(gd[:, :], gb[:, :], gc_t[:, :]))
        vstep(lambda e: e.tensor_mul(g3(gd), g3(gd), bc16(gne)))
        vstep(lambda e: e.tensor_add(gc_t[:, :], gc_t[:, :], gd[:, :]))
        vstep(lambda e: e.reduce_sum(gm1[:, :], g3(gc_t), axis=AX.X))
        vstep(lambda e: e.tensor_scalar_max(gm1[:, :], gm1[:, :], 1e-8))
        vstep(lambda e: e.reciprocal(gm1[:, :], gm1[:, :]))
        vstep(lambda e: e.tensor_mul(g3(gc_t), g3(gc_t), bc16(gm1)))
        p.op("vector", lambda e: e.tensor_copy(g_bf[:, :], gc_t[:, :]),
             inc=sm.v_gbf, wait=[(sm.sv, sm.sv.n)])
        for st in range(16):
            p.op("tensor", lambda t, st=st: t.transpose(
                ps_gt[:, :], g_bf[:, st * 8:(st + 1) * 8], ctx.identb[:, :]),
                inc=sm.t_tr,
                wait=[(sm.v_gbf, sm.v_gbf.n)] + ([(sm.s_tr, sm.s_tr.n)] if st else []))
            p.op("scalar", lambda e, st=st: e.copy(
                w_sb[0:8, st * 128:(st + 1) * 128], ps_gt[:, :]),
                inc=sm.s_tr, wait=[(sm.t_tr, sm.t_tr.n)])
        p.run()

        p = Plan(nc)
        wmark3 = {}
        for e_i in range(8):
            if e_i == 0:
                for nm, dst in (("L1_ffn_Wg", wg_sb), ("L1_ffn_Wu", wu_sb),
                                ("L1_ffn_Wd", wd_sb)):
                    p.op("gpsimd", lambda g, nm=nm, dst=dst: g.dma_start(
                        dst[:, :].rearrange("p (c f) -> p c f", f=1024),
                        w[nm][0].rearrange("(c p) f -> p c f", p=128)),
                        inc=sm.d_wB, inc_k=16,
                        wait=[(sm.t_mm, sm.t_mm.n)] if nm == "L1_ffn_Wg" else ())
                wweights = [(sm.d_wB, sm.d_wB.n)]
            else:
                wweights = [(sm.d_wB, wmark3[e_i])]
            for c in range(NCH):
                p.op("sync", lambda e, c=c, u=u: e.dma_start(
                    xn_blk[:, c * 512:(c + 1) * 512],
                    xn_scr[c * 128:(c + 1) * 128, u * 2048: u * 2048 + 512]),
                    inc=sm.d_xn2, inc_k=16,
                    wait=[(sm.d_xnw, sm.d_xnw.n), (sm.t_mm, sm.t_mm.n)] if c == 0 else ())
            if e_i == 0:
                bank_records = []
            ps_pp3 = [ps_p0, ps_p1, ps_A]
            for s_blk in range(4):
                slot = s_blk % 2
                xnw2 = [(sm.d_xn2, sm.d_xn2.n)]
                p.op("tensor", lambda t, e_i=e_i, s_blk=s_blk: t.matmul(
                    ps_bc[:, :], ctx.sel_all[:, e_i * 128:(e_i + 1) * 128],
                    w_sb[0:8, s_blk * 512:(s_blk + 1) * 512], start=True, stop=True),
                    inc=sm.t_bc, wait=[(sm.s_bc, sm.s_bc.n)])
                p.op("scalar", lambda e: e.copy(bcw_sb[:, :], ps_bc[:, :]),
                     inc=sm.s_bc, wait=[(sm.t_bc, sm.t_bc.n), (sm.v_h, sm.v_h.n)])
                bcw_w = [(sm.s_bc, sm.s_bc.n)]
                for ic in range(NCH):
                    sl2 = ic % 2
                    G = len(bank_records)
                    gw = ([bank_records[G - 3]] if G >= 3 else []) + (wweights + xnw2 if ic == 0 else [])

                    def t_g(t, ic=ic, slot=slot, G=G):
                        for c in range(NCH):
                            ins = t.matmul(
                                ps_pp3[G % 3][:, :],
                                wg_sb[:, c * 1024 + ic * 128: c * 1024 + (ic + 1) * 128],
                                xn_blk[:, slot * 4096 + c * 512: slot * 4096 + (c + 1) * 512],
                                start=(c == 0), stop=(c == 7))
                        return ins
                    p.op("tensor", t_g, inc=sm.t_mm, wait=gw)
                    si_w = [(sm.t_mm, sm.t_mm.n)]
                    if sm.v_h.n >= 1:
                        si_w.append((sm.v_h, sm.v_h.n - 1))
                    p.op("scalar", lambda e, sl2=sl2, G=G: e.activation(
                        silu_sb[:, sl2 * 512:(sl2 + 1) * 512], ps_pp3[G % 3][:, :],
                        AF.Silu), inc=sm.s_si, wait=si_w)
                    bank_records.append((sm.s_si, sm.s_si.n))
                    G = len(bank_records)
                    gw = [bank_records[G - 3]] if G >= 3 else []

                    def t_u(t, ic=ic, slot=slot, G=G):
                        for c in range(NCH):
                            ins = t.matmul(
                                ps_pp3[G % 3][:, :],
                                wu_sb[:, c * 1024 + ic * 128: c * 1024 + (ic + 1) * 128],
                                xn_blk[:, slot * 4096 + c * 512: slot * 4096 + (c + 1) * 512],
                                start=(c == 0), stop=(c == 7))
                        return ins
                    p.op("tensor", t_u, inc=sm.t_mm, wait=gw)
                    p.op("vector", lambda e, sl2=sl2: e.tensor_mul(
                        silu_sb[:, sl2 * 512:(sl2 + 1) * 512],
                        silu_sb[:, sl2 * 512:(sl2 + 1) * 512], bcw_sb[:, :]),
                        wait=[(sm.s_si, sm.s_si.n)] + bcw_w)
                    p.op("vector", lambda e, ic=ic, sl2=sl2, G=G: e.tensor_mul(
                        h_sb[:, ic * 512:(ic + 1) * 512], ps_pp3[G % 3][:, :],
                        silu_sb[:, sl2 * 512:(sl2 + 1) * 512]),
                        inc=sm.v_h, wait=[(sm.t_mm, sm.t_mm.n)])
                    bank_records.append((sm.v_h, sm.v_h.n))
                if s_blk == 3:
                    t_after_gu = sm.t_mm.n
                for dt2 in range(8):
                    G = len(bank_records)
                    gw = [bank_records[G - 3]] if G >= 3 else []

                    def t_d(t, dt2=dt2, G=G):
                        for ic in range(NCH):
                            ins = t.matmul(
                                ps_pp3[G % 3][:, :],
                                wd_sb[:, ic * 1024 + dt2 * 128: ic * 1024 + (dt2 + 1) * 128],
                                h_sb[:, ic * 512:(ic + 1) * 512],
                                start=(ic == 0), stop=(ic == 7))
                        return ins
                    p.op("tensor", t_d, inc=sm.t_mm,
                         wait=gw + ([(sm.v_h, sm.v_h.n)] if dt2 == 0 else []))
                    p.op("vector", lambda e, dt2=dt2, s_blk=s_blk, G=G: e.tensor_add(
                        X_sup[:, dt2 * 2048 + s_blk * 512: dt2 * 2048 + (s_blk + 1) * 512],
                        ps_pp3[G % 3][:, :],
                        X_sup[:, dt2 * 2048 + s_blk * 512: dt2 * 2048 + (s_blk + 1) * 512]),
                        inc=sm.v_o, wait=[(sm.t_mm, sm.t_mm.n)])
                    bank_records.append((sm.v_o, sm.v_o.n))
                if s_blk < 3:
                    t_after_h = sm.t_mm.n
                    for c in range(NCH):
                        p.op("sync", lambda e, c=c, u=u, s_blk=s_blk, slot=slot: e.dma_start(
                            xn_blk[:, (1 - slot) * 4096 + c * 512: (1 - slot) * 4096 + (c + 1) * 512],
                            xn_scr[c * 128:(c + 1) * 128,
                                   u * 2048 + (s_blk + 1) * 512: u * 2048 + (s_blk + 2) * 512]),
                            inc=sm.d_xn2, inc_k=16,
                            wait=[(sm.t_mm, t_after_h)] if c == 0 else ())
                elif e_i < 7:
                    ne = e_i + 1
                    t_now = sm.t_mm.n
                    for nm, dst, gv in (("L1_ffn_Wg", wg_sb, t_after_gu),
                                        ("L1_ffn_Wu", wu_sb, t_after_gu),
                                        ("L1_ffn_Wd", wd_sb, t_now)):
                        p.op("gpsimd", lambda g, nm=nm, dst=dst, ne=ne: g.dma_start(
                            dst[:, :].rearrange("p (c f) -> p c f", f=1024),
                            w[nm][ne].rearrange("(c p) f -> p c f", p=128)),
                            inc=sm.d_wB, inc_k=16, wait=[(sm.t_mm, gv)])
                    wmark3[ne] = sm.d_wB.n
        for c in range(NCH):
            p.op("sync", lambda e, c=c, u=u: e.dma_start(
                x_dst[c * 128:(c + 1) * 128, u * 2048:(u + 1) * 2048],
                X_sup[:, c * 2048:(c + 1) * 2048]),
                inc=sm.d_out, inc_k=16,
                wait=[(sm.v_o, sm.v_o.n)] if c == 0 else ())
        p.op("sync", lambda e: e.nop(), wait=[(sm.d_out, sm.d_out.n)])
        p.run()

    es.close()
    esp.close()


SEM_NAMES = [
    "sv", "ss", "t_mm", "s_ev", "v_ev", "s_sq", "v_cast", "v_xn0", "s_xn",
    "d_w", "d_v", "d_wv", "v_wvr", "d_x", "d_qk", "d_qh", "d_xh",
    "t_sc", "s_exp", "t_pv", "s_sr", "s_rb", "v_on", "v_res", "d_out",
    "d_xnw", "d_xn2", "d_wB", "t_bc", "s_bc", "v_h", "v_o", "t_gm", "v_gl",
    "t_tr", "s_tr", "v_gbf", "d_g", "t_ss", "t_fac", "v_mk", "s_si", "v_ps",
]


def build_program(phases):
    nc = bass.Bass("TRN2", target_bir_lowering=False)
    x_in = nc.dram_tensor("x_in", [1024, 4096], F32, kind="ExternalInput").ap()
    x_out = nc.dram_tensor("x_out", [1024, 4096], F32, kind="ExternalOutput").ap()
    w = {}
    shapes = {
        "L0_attn_Wq": [1024, 1024], "L0_attn_bq": [1024],
        "L0_attn_Wk": [1024, 1024], "L0_attn_bk": [1024],
        "L0_attn_Wv": [1024, 8192], "L0_attn_bv": [8192],
        "L0_attn_gamma": [1024], "L0_attn_beta": [1024],
        "L0_ffn_gate": [1024, 8], "L0_ffn_W1": [8, 1024, 2048],
        "L0_ffn_b1": [8, 2048], "L0_ffn_W2": [8, 2048, 1024],
        "L0_ffn_b2": [8, 1024],
        "L1_attn_Wq": [1024, 1024], "L1_attn_Wk": [1024, 1024],
        "L1_attn_Wv": [1024, 1024], "L1_attn_Wo": [1024, 1024],
        "L1_ffn_Wg": [8, 1024, 1024], "L1_ffn_Wu": [8, 1024, 1024],
        "L1_ffn_Wd": [8, 1024, 1024], "L1_ffn_Wr": [1024, 8],
        "L1_ffn_temperature": [1],
    }
    for n in WEIGHT_NAMES:
        w[n] = nc.dram_tensor(n, shapes[n], F32, kind="ExternalInput").ap()

    es = ExitStack()
    ctx = type("Ctx", (), {})()
    emit_setup(nc, es, ctx)
    sm = make_sems(nc, es, SEM_NAMES)

    n_ph = len(phases)
    inter = {}
    def x_ap(i):
        if i == 0:
            return x_in
        if i == n_ph:
            return x_out
        if i not in inter:
            inter[i] = nc.dram_tensor(f"x_mid{i}", [1024, 4096], F32,
                                      kind="Internal").ap()
        return inter[i]

    emitters = {0: emit_phase0, 1: emit_phase1, 2: emit_phase2, 3: emit_phase3}
    for idx, ph in enumerate(phases):
        emitters[ph](nc, ctx, sm, x_ap(idx), x_ap(idx + 1), w)
    es.close()
    return nc


def kernel(**inputs):
    x = inputs["L0_attn_x"]
    xs = [np.ascontiguousarray(x[b].T).astype(np.float32) for b in range(8)]
    weights = {n: np.ascontiguousarray(inputs[n], dtype=np.float32)
               for n in WEIGHT_NAMES}
    phase_groups = [(0, 1, 2, 3)]
    cur = xs
    for pg in phase_groups:
        nc = build_program(pg)
        in_maps = [dict(weights, x_in=cur[b]) for b in range(8)]
        res = run_bass_kernel_spmd(nc, in_maps, core_ids=list(range(8)))
        cur = [res.results[b]["x_out"] for b in range(8)]
    out = np.stack([c.T for c in cur]).astype(np.float32)
    return out
